# version 22 (final)
# baseline (speedup 1.0000x reference)
import numpy as np
import ml_dtypes
from contextlib import ExitStack
import concourse.bass as bass
import concourse.mybir as mybir
from concourse.bass_utils import run_bass_kernel_spmd

F32 = mybir.dt.float32
BF16 = mybir.dt.bfloat16
I32 = mybir.dt.int32
AF = mybir.ActivationFunctionType
ALU = mybir.AluOpType

T = 4096
TC = 512
NCH = T // TC
LC = 128
NLC = TC // LC
D = 1024
ND = D // 128
BIG = 1.0e9
DIN0 = 2048
NH = 32
HD = 64
DST = 128
CONVD = 2304
NTI0 = CONVD // 128
HID0 = 3072
QS = 1.0 / 127.0
DIN1 = 2048
NST = 16
DTR = 64
NTI1 = DIN1 // 128
NGRP = 8
NE = 8
HID1 = 2048
RK = 16
TOWN = 2048
NCHO = TOWN // TC

ENGS = ("sync", "vector", "scalar", "tensor", "gpsimd")


class Ctl:

    def __init__(self, nc, ctx):
        self.nc = nc
        self.ops = {e: [] for e in ENGS}
        self.s_tick = ctx.enter_context(nc.semaphore("tick"))
        self.ntick = 0
        self.last_wait = {e: 0 for e in ENGS}
        self.dsem = {}
        self.dcnt = {}
        for q in ("sync", "gpsimd", "scalar"):
            self.dsem[q] = ctx.enter_context(nc.semaphore(f"dma_{q}"))
            self.dcnt[q] = 0
        self.last_dwait = {e: {q: 0 for q in self.dsem} for e in ENGS}

    def op(self, eng, fn, tick=False):
        wv = self.ntick if self.ntick > self.last_wait[eng] else 0
        if wv:
            self.last_wait[eng] = wv
        ent = {"kind": "op", "fn": fn, "wv": wv, "tickid": None, "dwv": None}
        if tick:
            self.ntick += 1
            self.last_wait[eng] = self.ntick
            ent["tickid"] = self.ntick
        self.ops[eng].append(ent)

    def tick_last(self, eng):
        lst = self.ops[eng]
        assert lst and lst[-1]["kind"] == "op" and lst[-1]["tickid"] is None
        self.ntick += 1
        self.last_wait[eng] = self.ntick
        lst[-1]["tickid"] = self.ntick

    def fence(self, eng):
        self.ops[eng].append({"kind": "wait", "wv": self.ntick})
        self.last_wait[eng] = self.ntick

    def dma(self, q, fn):
        self.dcnt[q] += 16
        wv = self.ntick if self.ntick > self.last_wait[q] else 0
        if wv:
            self.last_wait[q] = wv
        self.ops[q].append({"kind": "dma", "fn": fn, "wv": wv,
                            "tickid": None, "dwv": None, "q": q})

    def dwait(self, eng, q):
        cnt = self.dcnt[q]
        if cnt <= self.last_dwait[eng][q]:
            return
        self.last_dwait[eng][q] = cnt
        self.ops[eng].append({"kind": "dwait", "q": q, "cnt": cnt})

    def mark(self, q):
        return self.dcnt[q]

    def dwait_cnt(self, eng, q, cnt):
        if cnt <= self.last_dwait[eng][q]:
            return
        self.last_dwait[eng][q] = cnt
        self.ops[eng].append({"kind": "dwait", "q": q, "cnt": cnt})

    def barrier(self, dmas=True):
        if dmas:
            for e in ENGS:
                for q in self.dsem:
                    self.dwait(e, q)
        for e in ENGS:
            lst = self.ops[e]
            j = len(lst) - 1
            while j >= 0 and lst[j]["kind"] == "dwait":
                j -= 1
            if j >= 0 and lst[j]["kind"] == "op" and lst[j]["tickid"] is None:
                self.ntick += 1
                lst[j]["tickid"] = self.ntick
                self.last_wait[e] = self.ntick
            else:
                self.op(e, lambda h: h.nop(), tick=True)
        wv = self.ntick
        for e in ENGS:
            self.last_wait[e] = wv
            self.ops[e].append({"kind": "wait", "wv": wv})

    def emit(self, block):
        tsem = self.s_tick
        dsem = self.dsem
        for name in ENGS:
            oplist = self.ops[name]

            def body(h, oplist=oplist):
                for ent in oplist:
                    k = ent["kind"]
                    if k == "dwait":
                        h.wait_ge(dsem[ent["q"]], ent["cnt"])
                    elif k == "wait":
                        h.wait_ge(tsem, ent["wv"])
                    elif k == "dma":
                        if ent["wv"]:
                            h.wait_ge(tsem, ent["wv"])
                        ent["fn"](h).then_inc(dsem[ent["q"]], 16)
                    else:
                        if ent["wv"]:
                            h.wait_ge(tsem, ent["wv"])
                        ins = ent["fn"](h)
                        if ent["tickid"] is not None:
                            ins.then_inc(tsem, 1)
            getattr(block, name)(body)
        assert self.ntick < 60000, f"tick overflow {self.ntick}"
        for q, cval in self.dcnt.items():
            assert cval < 60000, f"dma sem overflow {q}={cval}"


def sbt(nc, ctx, name, shape, dtype):
    return ctx.enter_context(nc.sbuf_tensor(name, shape, dtype))


def pst(nc, ctx, name, shape, dtype=F32):
    return ctx.enter_context(nc.psum_tensor(name, shape, dtype))


def emit_ln_stats(nc, c, xt, sqt, ones_col, p_s, p_q, r, eps2=None, W=TC):
    c.op("scalar", lambda h: h.activation(out=sqt[:, 0:ND * W], in_=xt[:, 0:ND * W],
                                          func=AF.Square), tick=True)

    def mm_s(h):
        for d in range(ND):
            ins = h.matmul(out=p_s[:, 0:W], lhsT=ones_col[:, 0:1],
                           rhs=xt[:, d * W:(d + 1) * W],
                           start=(d == 0), stop=(d == ND - 1))
        return ins
    c.op("tensor", mm_s, tick=True)

    def mm_q(h):
        for d in range(ND):
            ins = h.matmul(out=p_q[:, 0:W], lhsT=ones_col[:, 0:1],
                           rhs=sqt[:, d * W:(d + 1) * W],
                           start=(d == 0), stop=(d == ND - 1))
        return ins
    c.op("tensor", mm_q, tick=True)
    c.op("vector", lambda h: h.tensor_scalar(out=r["m"][:, 0:W], in0=p_s[0:1, :],
         scalar1=1.0 / D, scalar2=None, op0=ALU.mult))
    c.op("vector", lambda h: h.tensor_scalar(out=r["e2"][:, 0:W], in0=p_q[0:1, :],
         scalar1=1.0 / D, scalar2=None, op0=ALU.mult))
    c.op("vector", lambda h: h.tensor_tensor(out=r["m2"][:, 0:W], in0=r["m"][:, 0:W],
         in1=r["m"][:, 0:W], op=ALU.mult))
    c.op("vector", lambda h: h.tensor_tensor(out=r["var"][:, 0:W], in0=r["e2"][:, 0:W],
         in1=r["m2"][:, 0:W], op=ALU.subtract))
    c.op("vector", lambda h: h.tensor_scalar(out=r["vp"][:, 0:W], in0=r["var"][:, 0:W],
         scalar1=1e-5, scalar2=None, op0=ALU.add))
    c.op("vector", lambda h: h.reciprocal(out=r["rv"][:, 0:W], in_=r["vp"][:, 0:W]),
         tick=(eps2 is None))
    if eps2 is not None:
        c.op("vector", lambda h: h.tensor_tensor(out=r["q2"][:, 0:W], in0=r["var"][:, 0:W],
             in1=r["rv"][:, 0:W], op=ALU.mult))
        c.op("vector", lambda h, e=eps2: h.tensor_scalar(out=r["qe"][:, 0:W],
             in0=r["q2"][:, 0:W], scalar1=e, scalar2=None, op0=ALU.add))
        c.op("vector", lambda h: h.reciprocal(out=r["rq"][:, 0:W], in_=r["qe"][:, 0:W]),
             tick=True)
        c.op("scalar", lambda h: h.sqrt(out=r["g2"][:, 0:W], in_=r["rq"][:, 0:W]))
    c.op("scalar", lambda h: h.sqrt(out=r["g1"][:, 0:W], in_=r["rv"][:, 0:W]),
         tick=True)
    c.op("vector", lambda h: h.tensor_tensor(out=r["mg1"][:, 0:W], in0=r["m"][:, 0:W],
         in1=r["g1"][:, 0:W], op=ALU.mult), tick=(eps2 is None))
    if eps2 is not None:
        c.op("vector", lambda h: h.tensor_tensor(out=r["g12"][:, 0:W], in0=r["g1"][:, 0:W],
             in1=r["g2"][:, 0:W], op=ALU.mult))
        c.op("vector", lambda h: h.tensor_tensor(out=r["mg12"][:, 0:W], in0=r["m"][:, 0:W],
             in1=r["g12"][:, 0:W], op=ALU.mult), tick=True)


def make_rows(nc, ctx, prefix):
    rt = [sbt(nc, ctx, f"{prefix}_{i}", [1, TC], F32) for i in range(6)]
    alias = {"m": 0, "e2": 1, "m2": 2, "var": 3, "vp": 1, "rv": 2, "q2": 4,
             "qe": 3, "rq": 4, "g1": 5, "g2": 3, "mg1": 2, "g12": 4,
             "mg12": 1}
    return {n: rt[i] for n, i in alias.items()}


def softplus_chain(nc, c, out_sb, psum_in, bias_col, w_sb, aw_sb, e_sb, l_sb, P):
    c.op("vector", lambda h: h.tensor_scalar(out=w_sb[:P, :], in0=psum_in,
         scalar1=bias_col, scalar2=None, op0=ALU.add), tick=True)
    c.op("scalar", lambda h: h.activation(out=aw_sb[:P, :], in_=w_sb[:P, :],
         func=AF.Abs))
    c.op("scalar", lambda h: h.activation(out=e_sb[:P, :], in_=aw_sb[:P, :],
         func=AF.Exp, scale=-1.0))
    c.op("scalar", lambda h: h.activation(out=l_sb[:P, :], in_=e_sb[:P, :],
         func=AF.Ln, bias=1.0), tick=True)
    c.op("vector", lambda h: h.tensor_scalar(out=w_sb[:P, :], in0=w_sb[:P, :],
         scalar1=0.0, scalar2=None, op0=ALU.max))
    c.op("vector", lambda h: h.tensor_tensor(out=out_sb, in0=w_sb[:P, :],
         in1=l_sb[:P, :], op=ALU.add), tick=True)


def emit_l0passA(nc, ctx, c, x_dram, S0_dram, sp, Wd, Cst, shared, dbg_h=None):
    w0in = [sbt(nc, ctx, f"w0in{k}", [128, 4384], BF16) for k in range(ND)]
    convw = sbt(nc, ctx, "cvw0", [128, 72], F32)
    convb = sbt(nc, ctx, "cvb0", [128, 18], F32)
    dtb = sbt(nc, ctx, "dtb0", [32, 1], F32)
    projbh = sbt(nc, ctx, "pjbh0", [128, ND], F32)
    nwc = sbt(nc, ctx, "nwc0", [128, ND], F32)
    xt = sbt(nc, ctx, "a0xt", [128, ND * TC], F32)
    sqt = sbt(nc, ctx, "a0sq", [128, ND * TC], F32)
    h_sb = sbt(nc, ctx, "a0h", [128, ND * TC], BF16)
    xbc = [sbt(nc, ctx, f"a0xbc{i}", [128, 3 + TC], F32) for i in range(NTI0)]
    halo = sbt(nc, ctx, "a0halo", [128, 54], F32)
    ct = sbt(nc, ctx, "a0ct", [128, TC], F32)
    stg = [sbt(nc, ctx, f"a0stg{i}", [128, TC], BF16) for i in range(2)]
    s0p = [sbt(nc, ctx, f"a0s0p{i}", [128, TC], F32) for i in range(2)]
    dts = sbt(nc, ctx, "a0dts", [32, TC], F32)
    sp_w = sbt(nc, ctx, "a0spw", [32, TC], F32)
    sp_a = sbt(nc, ctx, "a0spa", [32, TC], F32)
    sp_e = sbt(nc, ctx, "a0spe", [32, TC], F32)
    hrow = sbt(nc, ctx, "a0hrow", [1, TC], F32)
    hrow2 = sbt(nc, ctx, "a0hrow2", [1, TC], F32)
    rows = make_rows(nc, ctx, "a0r")
    pS = pst(nc, ctx, "a0pS", [1, TC])
    pQ = pst(nc, ctx, "a0pQ", [1, TC])
    pB1 = pst(nc, ctx, "a0pB1", [128, TC])
    pB2 = pst(nc, ctx, "a0pB2", [128, TC])
    pIP = [pst(nc, ctx, f"a0pIP{i}", [128, TC]) for i in range(2)]
    pDT = pB2[0:32, :]
    Bb, Cb = shared["Bb"], shared["Cb"]
    ones_col, ones_row = Cst["ones_col"], Cst["ones_row"]

    for k in range(ND):
        c.dma("gpsimd", lambda h, k=k: h.dma_start(
            out=w0in[k][:], in_=Wd["inWT"][k * 128:(k + 1) * 128, :]))
    for nm, dst in [("convw", convw), ("convb", convb), ("dtb", dtb),
                    ("nwc", nwc)]:
        c.dma("sync", lambda h, nm=nm, dst=dst: h.dma_start(
            out=dst[:], in_=Wd[nm][:, :]))
    c.dma("sync", lambda h: h.dma_start(out=projbh[:], in_=Wd["projb"][:, :]))
    for e in ("scalar", "vector", "tensor"):
        c.dwait(e, "sync")
        c.dwait(e, "gpsimd")
    c.op("scalar", lambda h: h.mul(out=projbh[:, :], in_=projbh[:, :], mul=0.5),
         tick=True)
    c.op("vector", lambda h: h.memset(halo[:, :], 0.0), tick=True)

    for tc in range(NCH):
        t0 = tc * TC
        c.dma("sync", lambda h, t0=t0: h.dma_start(
            out=xt[:, :].rearrange("p (d t) -> p d t", d=ND),
            in_=x_dram[:, :, t0:t0 + TC].transpose([1, 0, 2])))
        for e in ("scalar", "vector", "tensor"):
            c.dwait(e, "sync")
        emit_ln_stats(nc, c, xt, sqt, ones_col, pS, pQ, rows, eps2=1e-4)
        c.op("tensor", lambda h: h.matmul(out=pB1[:, :], lhsT=ones_row[0:1, :],
             rhs=rows["g12"][:, :], start=True, stop=True), tick=True)
        c.op("tensor", lambda h: h.matmul(out=pB2[:, :], lhsT=ones_row[0:1, :],
             rhs=rows["mg12"][:, :], start=True, stop=True), tick=True)
        for d in range(ND):
            sl = slice(d * TC, (d + 1) * TC)
            c.op("vector", lambda h, sl=sl: h.tensor_tensor(
                out=sqt[:, sl], in0=xt[:, sl], in1=pB1[:, :], op=ALU.mult))
            c.op("vector", lambda h, sl=sl: h.tensor_tensor(
                out=sqt[:, sl], in0=sqt[:, sl], in1=pB2[:, :], op=ALU.subtract))
            c.op("vector", lambda h, d=d, sl=sl: h.tensor_scalar(
                out=h_sb[:, sl], in0=sqt[:, sl], scalar1=nwc[:, d:d + 1],
                scalar2=None, op0=ALU.mult))
        c.tick_last("vector")
        if dbg_h is not None:
            c.dma("sync", lambda h, t0=t0: h.dma_start(
                out=dbg_h[:, :, t0:t0 + TC].transpose([1, 0, 2]),
                in_=h_sb[:, :].rearrange("p (d t) -> p d t", d=ND)))
        c.op("vector", lambda h: h.tensor_scalar(out=hrow[:, :],
             in0=rows["g1"][:, :], scalar1=0.5, scalar2=None, op0=ALU.mult))
        c.op("vector", lambda h: h.tensor_scalar(out=hrow2[:, :],
             in0=rows["mg1"][:, :], scalar1=0.5, scalar2=None, op0=ALU.mult),
             tick=True)
        c.op("tensor", lambda h: h.matmul(out=pB1[:, :], lhsT=ones_row[0:1, :],
             rhs=hrow[:, :], start=True, stop=True), tick=True)
        c.op("tensor", lambda h: h.matmul(out=pB2[:, :], lhsT=ones_row[0:1, :],
             rhs=hrow2[:, :], start=True, stop=True), tick=True)
        for d in range(ND):
            sl = slice(d * TC, (d + 1) * TC)
            st = s0p[d % 2]
            if d >= 2:
                c.dwait("vector", "sync")
            c.op("vector", lambda h, sl=sl, st=st: h.tensor_tensor(
                out=st[:, :], in0=xt[:, sl], in1=pB1[:, :], op=ALU.mult))
            c.op("vector", lambda h, st=st: h.tensor_tensor(
                out=st[:, :], in0=st[:, :], in1=pB2[:, :], op=ALU.subtract))
            c.op("vector", lambda h, d=d, st=st: h.tensor_scalar(
                out=st[:, :], in0=st[:, :], scalar1=projbh[:, d:d + 1],
                scalar2=None, op0=ALU.add), tick=True)
            c.dma("sync", lambda h, d=d, st=st, t0=t0: h.dma_start(
                out=S0_dram[d, :, t0:t0 + TC], in_=st[:, :]))
        def cons0(of, t0):
            pp = pIP[(of + 1) % 2]
            if of < 16:
                stz = stg[of % 2]
                if of >= 2:
                    c.dwait("scalar", "sync")
                c.op("scalar", lambda h, pp=pp, stz=stz: h.activation(
                    out=stz[:, :], in_=pp[:, :], func=AF.Silu), tick=True)
                c.dma("sync", lambda h, of=of, stz=stz, t0=t0: h.dma_start(
                    out=sp["z"][of, :, t0:t0 + TC], in_=stz[:, :]))
            else:
                ti = of - 16
                c.op("scalar", lambda h, ti=ti, pp=pp: h.copy(
                    out=xbc[ti][:, 3:3 + TC], in_=pp[:, :]), tick=True)

        for of in range(35):
            M = 32 if of == 34 else 128
            pp = pDT if of == 34 else pIP[(of + 1) % 2]

            def mms(h, of=of, M=M, pp=pp):
                for k in range(ND):
                    ins = h.matmul(out=pp[0:M, :] if M != 128 else pp[:, :],
                                   lhsT=w0in[k][:, of * 128:of * 128 + M],
                                   rhs=h_sb[:, k * TC:(k + 1) * TC],
                                   start=(k == 0), stop=(k == ND - 1))
                return ins
            c.op("tensor", mms, tick=True)
            if of >= 1 and of < 34:
                cons0(of - 1, t0)
            if of == 34:
                cons0(33, t0)
        for ti in range(NTI0):
            c.op("vector", lambda h, ti=ti: h.tensor_copy(
                out=xbc[ti][:, 0:3], in_=halo[:, ti * 3:ti * 3 + 3]))
        c.tick_last("vector")
        softplus_chain(nc, c, dts[:, :], pDT, dtb[:, 0:1], sp_w, sp_a, sp_e,
                       sp_a, 32)
        c.dma("sync", lambda h, t0=t0: h.dma_start(
            out=sp["dts"][:, t0:t0 + TC], in_=dts[:, :]))
        for ti in range(NTI0):
            c.op("vector", lambda h, ti=ti: h.tensor_scalar(
                out=ct[:, :], in0=xbc[ti][:, 0:TC],
                scalar1=convw[:, ti * 4:ti * 4 + 1],
                scalar2=convb[:, ti:ti + 1], op0=ALU.mult, op1=ALU.add))
            for k in range(1, 4):
                c.op("vector", lambda h, ti=ti, k=k: h.scalar_tensor_tensor(
                    out=ct[:, :], in0=xbc[ti][:, k:k + TC],
                    scalar=convw[:, ti * 4 + k:ti * 4 + k + 1], in1=ct[:, :],
                    op0=ALU.mult, op1=ALU.add))
            c.op("vector", lambda h, ti=ti: h.tensor_copy(
                out=halo[:, ti * 3:ti * 3 + 3], in_=xbc[ti][:, TC:TC + 3]),
                tick=True)
            if ti < 16:
                stz = stg[ti % 2]
                c.dwait("scalar", "sync")
                c.op("scalar", lambda h, stz=stz: h.activation(
                    out=stz[:, :], in_=ct[:, :], func=AF.Silu), tick=True)
                c.dma("sync", lambda h, ti=ti, stz=stz, t0=t0: h.dma_start(
                    out=sp["xs"][ti, :, t0:t0 + TC], in_=stz[:, :]))
            elif ti == 16:
                c.op("scalar", lambda h, t0=t0: h.activation(
                    out=Bb[:, t0:t0 + TC], in_=ct[:, :], func=AF.Silu),
                    tick=True)
            else:
                c.op("scalar", lambda h, t0=t0: h.activation(
                    out=Cb[:, t0:t0 + TC], in_=ct[:, :], func=AF.Silu),
                    tick=True)
        c.barrier()


def emit_l0passB(nc, ctx, c, S0_dram, sp, Wd, Cst, shared, dbg_y=None, dbg_W=None, dbg_ssd=None, dbg2=None):
    NPO = DIN0 // 128
    POT = [sbt(nc, ctx, f"pot{k}", [128, D], BF16) for k in range(NPO)]
    Bb, Cb = shared["Bb"], shared["Cb"]
    ones_col, ones_row = Cst["ones_col"], Cst["ones_row"]
    id128, idbf, tri, ones32 = Cst["id128"], Cst["idbf"], Cst["tri"], Cst["ones32"]
    pB1 = pst(nc, ctx, "b0pB1", [128, TC])
    pIP = [pst(nc, ctx, f"b0pIP{i}", [128, TC]) for i in range(2)]
    pS = pst(nc, ctx, "b0pS", [1, TC])
    bankA = pst(nc, ctx, "b0bankA", [128, TC])
    bankB = pst(nc, ctx, "b0bankB", [128, 224])
    pWB = bankA[:, 0:128]
    pSC = bankA[:, 128:256]
    pY = bankA[:, 256:384]
    pEC = bankA[:, 384:416]
    pXTt = pst(nc, ctx, "b0pXT", [128, 2 * HD], BF16)
    pTR = bankB[:, 128:224]
    pBTt = pst(nc, ctx, "b0pBT", [128, LC], BF16)
    pBT = pBTt[:, :]

    with ExitStack() as pctx:
        owr = [sbt(nc, pctx, f"owr{k}", [128, DIN0], BF16) for k in range(ND)]
        pjt = [sbt(nc, pctx, f"pjt{k}", [128, D], BF16) for k in range(ND)]
        for k in range(ND):
            c.dma("gpsimd", lambda h, k=k: h.dma_start(
                out=owr[k][:], in_=Wd["outWraw"][k * 128:(k + 1) * 128, :]))
            c.dma("gpsimd", lambda h, k=k: h.dma_start(
                out=pjt[k][:], in_=Wd["projT"][k * 128:(k + 1) * 128, :]))
        c.dwait("tensor", "gpsimd")
        for m in range(NPO):
            for nh in range(2):
                def mmp(h, m=m, nh=nh):
                    for k in range(ND):
                        ins = h.matmul(
                            out=pIP[nh][:, 0:TC],
                            lhsT=owr[k][:, m * 128:(m + 1) * 128],
                            rhs=pjt[k][:, nh * TC:(nh + 1) * TC],
                            start=(k == 0), stop=(k == ND - 1))
                    return ins
                c.op("tensor", mmp, tick=True)
                c.op("vector", lambda h, m=m, nh=nh: h.tensor_copy(
                    out=POT[m][:, nh * TC:(nh + 1) * TC], in_=pIP[nh][:, 0:TC]),
                    tick=True)
        c.barrier()

    xsb = sbt(nc, ctx, "b0xsb", [128, 16 * TC], BF16)
    zb = sbt(nc, ctx, "b0zb", [128, 16 * TC], BF16)
    dts = sbt(nc, ctx, "b0dts", [32, TC], F32)
    W_ = sbt(nc, ctx, "b0W", [32, TC], F32)
    V_ = sbt(nc, ctx, "b0V", [32, TC], F32)
    wflat = sbt(nc, ctx, "b0wfl", [1, NH * LC], F32)
    dtT = sbt(nc, ctx, "b0dtT", [128, 32], F32)
    negb = sbt(nc, ctx, "b0negb", [128, 32], F32)
    dect = sbt(nc, ctx, "b0dec", [128, 32], F32)
    combT = sbt(nc, ctx, "b0comb", [128, 32], F32)
    echb = sbt(nc, ctx, "b0echb", [128, 32], F32)
    scr32 = sbt(nc, ctx, "b0scr32", [128, 32], F32)
    scoresM = sbt(nc, ctx, "b0scm", [128, LC], F32)
    BT = sbt(nc, ctx, "b0BT", [128, LC], BF16)
    Ea = sbt(nc, ctx, "b0Ea", [128, 2 * LC], F32)
    Ep = sbt(nc, ctx, "b0Ep", [128, 2 * LC], F32)
    dtxT = sbt(nc, ctx, "b0dtx", [128, 2 * HD], BF16)
    dtx3 = sbt(nc, ctx, "b0dtx3", [128, 2 * HD], BF16)
    Mt = sbt(nc, ctx, "b0Mt", [128, 2 * LC], BF16)
    Cpp = sbt(nc, ctx, "b0Cpp", [128, 2 * LC], BF16)
    y_sb = sbt(nc, ctx, "b0y", [128, 16 * TC], F32)
    yn_sb = sbt(nc, ctx, "b0yn", [128, 16 * TC], BF16)
    zsil = sbt(nc, ctx, "b0zs", [128, TC], F32)
    sq1 = sbt(nc, ctx, "b0sq1", [128, TC], F32)
    s0a = sbt(nc, ctx, "b0s0a", [128, TC], F32)
    s0b = sbt(nc, ctx, "b0s0b", [128, TC], F32)
    grms = sbt(nc, ctx, "b0grms", [1, TC], F32)
    rtmp = sbt(nc, ctx, "b0rtmp", [1, TC], F32)
    aheads = sbt(nc, ctx, "b0ah", [32, 1], F32)
    abc = sbt(nc, ctx, "b0abc", [128, 32], F32)
    S0f = sbt(nc, ctx, "b0Sf", [128, NH * HD], F32)
    S0bf = sbt(nc, ctx, "b0Sb", [128, NH * HD], BF16)
    alog = sbt(nc, ctx, "b0alog", [32, 1], F32)
    drep = sbt(nc, ctx, "b0drep", [128, 16], F32)
    gnw = sbt(nc, ctx, "b0gnw", [128, 16], F32)

    c.dma("sync", lambda h: h.dma_start(out=alog[:], in_=Wd["alog"][:, :]))
    c.dma("sync", lambda h: h.dma_start(out=drep[:], in_=Wd["drep"][:, :]))
    c.dma("sync", lambda h: h.dma_start(out=gnw[:], in_=Wd["gnw"][:, :]))
    for e in ("scalar", "vector", "tensor"):
        c.dwait(e, "sync")
    c.op("scalar", lambda h: h.activation(out=aheads[:, :], in_=alog[:, :],
         func=AF.Exp), tick=True)
    c.op("vector", lambda h: h.tensor_scalar(out=aheads[:, :], in0=aheads[:, :],
         scalar1=-1.0, scalar2=None, op0=ALU.mult), tick=True)
    c.dma("gpsimd", lambda h: h.dma_start(out=wflat[0:1, 0:32],
         in_=aheads[:, :], allow_slow_non_contiguous=True))
    c.dwait("tensor", "gpsimd")
    c.op("tensor", lambda h: h.matmul(out=pEC, lhsT=ones_row[0:1, :],
         rhs=wflat[0:1, 0:32], start=True, stop=True), tick=True)
    c.op("vector", lambda h: h.tensor_copy(out=abc[:, :], in_=pEC), tick=True)
    c.op("vector", lambda h: h.memset(S0f[:, :], 0.0))
    c.op("vector", lambda h: h.memset(S0bf[:, :], 0.0), tick=True)

    for tc in range(NCH):
        t0 = tc * TC
        c.dma("sync", lambda h, t0=t0: h.dma_start(
            out=xsb[:, :].rearrange("p (d t) -> p d t", d=16),
            in_=sp["xs"][:, :, t0:t0 + TC].transpose([1, 0, 2])))
        c.dma("sync", lambda h, t0=t0: h.dma_start(
            out=zb[:, :].rearrange("p (d t) -> p d t", d=16),
            in_=sp["z"][:, :, t0:t0 + TC].transpose([1, 0, 2])))
        c.dma("sync", lambda h, t0=t0: h.dma_start(
            out=dts[:, :], in_=sp["dts"][:, t0:t0 + TC]))
        for e in ("scalar", "vector", "tensor", "gpsimd"):
            c.dwait(e, "sync")
        for lc in range(NLC):
            l0 = lc * LC
            c.op("vector", lambda h, l0=l0: h.tensor_tensor_scan(
                out=W_[:, l0:l0 + LC], data0=ones32[:, 0:LC],
                data1=dts[:, l0:l0 + LC], initial=0.0,
                op0=ALU.mult, op1=ALU.add), tick=True)
            c.fence("vector")
            c.op("vector", lambda h, l0=l0: h.tensor_scalar(
                out=V_[:, l0:l0 + LC], in0=W_[:, l0:l0 + LC], scalar1=-1.0,
                scalar2=W_[:, l0 + LC - 1:l0 + LC], op0=ALU.mult, op1=ALU.add),
                tick=True)
        for lc in range(NLC):
            l0 = lc * LC

            def trs(h, l0=l0):
                h.transpose(out=pTR[:, 0:32], in_=dts[:, l0:l0 + LC],
                            identity=id128[0:32, 0:32])
                h.transpose(out=pTR[:, 32:64], in_=W_[:, l0:l0 + LC],
                            identity=id128[0:32, 0:32])
                return h.transpose(out=pTR[:, 64:96], in_=V_[:, l0:l0 + LC],
                                   identity=id128[0:32, 0:32])
            c.op("tensor", trs, tick=True)
            c.op("vector", lambda h: h.tensor_copy(out=dtT[:, :],
                 in_=pTR[:, 0:32]))
            c.op("vector", lambda h: h.scalar_tensor_tensor(
                out=negb[:, :], in0=abc[:, :], scalar=-1.0, in1=pTR[:, 32:64],
                op0=ALU.mult, op1=ALU.mult))
            c.op("vector", lambda h: h.tensor_tensor(
                out=scr32[:, :], in0=abc[:, :], in1=pTR[:, 64:96],
                op=ALU.mult), tick=True)
            c.op("scalar", lambda h: h.activation(out=dect[:, :],
                 in_=scr32[:, :], func=AF.Exp), tick=True)
            c.op("vector", lambda h: h.tensor_tensor(out=combT[:, :],
                 in0=dtT[:, :], in1=dect[:, :], op=ALU.mult), tick=True)
            c.dma("gpsimd", lambda h, l0=l0: h.dma_start(
                out=wflat[0:1, :].rearrange("a (b c) -> a b c", b=NH),
                in_=W_[:, l0:l0 + LC]))
            c.dwait("tensor", "gpsimd")
            c.op("tensor", lambda h: h.matmul(out=pEC,
                 lhsT=ones_row[0:1, :], rhs=wflat[0:1, LC - 1::LC],
                 start=True, stop=True), tick=True)
            c.op("vector", lambda h: h.tensor_tensor(out=scr32[:, :],
                 in0=abc[:, :], in1=pEC, op=ALU.mult), tick=True)
            c.op("scalar", lambda h: h.activation(out=echb[:, :],
                 in_=scr32[:, :], func=AF.Exp), tick=True)
            c.op("tensor", lambda h, t0=t0, l0=l0: h.matmul(out=pSC,
                 lhsT=Bb[:, t0 + l0:t0 + l0 + LC], rhs=Cb[:, t0 + l0:t0 + l0 + LC],
                 start=True, stop=True), tick=True)
            c.op("vector", lambda h: h.tensor_tensor(out=scoresM[:, :],
                 in0=pSC, in1=tri[:, :], op=ALU.mult), tick=True)
            c.op("tensor", lambda h, t0=t0, l0=l0: h.transpose(out=pBT,
                 in_=Bb[:, t0 + l0:t0 + l0 + LC], identity=idbf[:, :]),
                 tick=True)
            c.op("vector", lambda h: h.tensor_copy(out=BT[:, :], in_=pBT),
                 tick=True)
            if dbg_ssd is not None and tc == 0 and lc == 0:
                c.op("vector", lambda h: h.tensor_copy(out=zsil[:, 128:160],
                     in_=pTR[:, 64:96]), tick=True)
                c.dma("sync", lambda h: h.dma_start(out=dbg2[0:128, 0:32], in_=dtT[:, :]))
                c.dma("sync", lambda h: h.dma_start(out=dbg2[0:128, 32:64], in_=dect[:, :]))
                c.dma("sync", lambda h: h.dma_start(out=dbg2[0:128, 64:96], in_=combT[:, :]))
                c.dma("sync", lambda h: h.dma_start(out=dbg2[0:128, 96:128], in_=zsil[:, 128:160]))
                c.dma("sync", lambda h: h.dma_start(out=dbg2[0:128, 128:160], in_=negb[:, :]))
                c.dma("sync", lambda h: h.dma_start(out=dbg2[0:128, 160:192], in_=echb[:, :]))
                c.dwait("vector", "sync")
                c.dwait("scalar", "sync")
            for hq in range(NH // 2):
                hd0 = 2 * hq

                def pe1(h, hd0=hd0, hq=hq, l0=l0):
                    h.matmul(out=bankA[:, 0:256], lhsT=ones_row[0:1, :],
                             rhs=wflat[0:1, hd0 * LC:(hd0 + 2) * LC],
                             start=True, stop=True)
                    return h.transpose(
                        out=pXTt[:, 0:2 * HD],
                        in_=xsb[0:128, hq * TC + l0:hq * TC + l0 + LC],
                        identity=Cst["idbf"][:, :])
                c.op("tensor", pe1, tick=True)
                for s_ in range(2):
                    hd = hd0 + s_
                    esl = slice(s_ * LC, (s_ + 1) * LC)
                    c.op("scalar", lambda h, hd=hd, esl=esl, s_=s_:
                         h.activation(
                             out=Ea[:, esl],
                             in_=bankA[:, s_ * 128:(s_ + 1) * 128],
                             func=AF.Exp,
                             scale=abc[:, hd:hd + 1], bias=negb[:, hd:hd + 1]))
                    c.op("scalar", lambda h, hd=hd, esl=esl, s_=s_:
                         h.activation(
                             out=Ep[:, esl],
                             in_=bankA[:, s_ * 128:(s_ + 1) * 128],
                             func=AF.Exp,
                             scale=abc[:, hd:hd + 1]))
                c.tick_last("scalar")
                for s_ in range(2):
                    hd = hd0 + s_
                    xsl = slice(s_ * HD, (s_ + 1) * HD)
                    esl = slice(s_ * LC, (s_ + 1) * LC)
                    c.op("vector", lambda h, hd=hd, xsl=xsl: h.tensor_scalar(
                        out=dtxT[:, xsl], in0=pXTt[:, xsl],
                        scalar1=dtT[:, hd:hd + 1],
                        scalar2=None, op0=ALU.mult))
                    c.op("vector", lambda h, hd=hd, xsl=xsl: h.tensor_scalar(
                        out=dtx3[:, xsl], in0=pXTt[:, xsl],
                        scalar1=combT[:, hd:hd + 1],
                        scalar2=None, op0=ALU.mult))
                    c.op("vector", lambda h, esl=esl: h.scalar_tensor_tensor(
                        out=Mt[:, esl], in0=Ea[:, esl], scalar=BIG,
                        in1=scoresM[:, :],
                        op0=ALU.min, op1=ALU.mult))
                    c.op("vector", lambda h, t0=t0, l0=l0, esl=esl:
                         h.tensor_tensor(
                             out=Cpp[:, esl],
                             in0=Cb[:, t0 + l0:t0 + l0 + LC], in1=Ep[:, esl],
                             op=ALU.mult))
                c.tick_last("vector")

                def pe2(h, hd0=hd0):
                    h.matmul(out=pY[0:64], lhsT=dtxT[:, 0:HD],
                             rhs=Mt[:, 0:LC], start=True, stop=False)
                    h.matmul(out=pY[0:64],
                             lhsT=S0bf[:, hd0 * HD:(hd0 + 1) * HD],
                             rhs=Cpp[:, 0:LC],
                             start=False, stop=True)
                    h.matmul(out=pY[64:128], lhsT=dtxT[:, HD:2 * HD],
                             rhs=Mt[:, LC:2 * LC], start=True, stop=False)
                    h.matmul(out=pY[64:128],
                             lhsT=S0bf[:, (hd0 + 1) * HD:(hd0 + 2) * HD],
                             rhs=Cpp[:, LC:2 * LC],
                             start=False, stop=True)
                    return h.matmul(out=bankB[:, 0:128], lhsT=BT[:, :],
                                    rhs=dtx3[:, 0:2 * HD],
                                    start=True, stop=True)
                c.op("tensor", pe2, tick=True)
                if dbg_ssd is not None and tc == 0 and lc == 0 and hd == 0:
                    c.op("vector", lambda h: h.tensor_copy(out=zsil[:, 0:128],
                         in_=pY[0:128]), tick=True)
                    c.dma("sync", lambda h: h.dma_start(out=dbg_ssd[0:128, 0:128], in_=zsil[:, 0:128]))
                    c.dma("sync", lambda h: h.dma_start(out=dbg_ssd[0:128, 128:256], in_=scoresM[:, :]))
                    c.dma("gpsimd", lambda h: h.dma_start(out=dbg_ssd[0:128, 256:384], in_=Mt[:, :]))
                    c.dma("gpsimd", lambda h: h.dma_start(out=dbg_ssd[0:128, 384:448], in_=dtxT[:, :]))
                    c.dma("gpsimd", lambda h: h.dma_start(out=dbg_ssd[0:128, 448:512], in_=dtx3[:, :]))
                c.op("vector", lambda h, hd0=hd0: h.scalar_tensor_tensor(
                    out=S0f[:, hd0 * HD:(hd0 + 1) * HD],
                    in0=S0f[:, hd0 * HD:(hd0 + 1) * HD],
                    scalar=echb[:, hd0:hd0 + 1], in1=bankB[:, 0:64],
                    op0=ALU.mult, op1=ALU.add))
                c.op("vector", lambda h, hd0=hd0: h.scalar_tensor_tensor(
                    out=S0f[:, (hd0 + 1) * HD:(hd0 + 2) * HD],
                    in0=S0f[:, (hd0 + 1) * HD:(hd0 + 2) * HD],
                    scalar=echb[:, hd0 + 1:hd0 + 2], in1=bankB[:, 64:128],
                    op0=ALU.mult, op1=ALU.add))
                c.op("vector", lambda h, hd0=hd0: h.tensor_copy(
                    out=S0bf[:, hd0 * HD:(hd0 + 2) * HD],
                    in_=S0f[:, hd0 * HD:(hd0 + 2) * HD]))
                c.op("vector", lambda h, hq=hq, l0=l0: h.scalar_tensor_tensor(
                    out=y_sb[0:64, hq * TC + l0:hq * TC + l0 + LC],
                    in0=xsb[0:64, hq * TC + l0:hq * TC + l0 + LC],
                    scalar=drep[0:64, hq:hq + 1], in1=pY[0:64],
                    op0=ALU.mult, op1=ALU.add))
                c.op("vector", lambda h, hq=hq, l0=l0: h.scalar_tensor_tensor(
                    out=y_sb[64:128, hq * TC + l0:hq * TC + l0 + LC],
                    in0=xsb[64:128, hq * TC + l0:hq * TC + l0 + LC],
                    scalar=drep[64:128, hq:hq + 1], in1=pY[64:128],
                    op0=ALU.mult, op1=ALU.add), tick=True)
        if dbg_y is not None:
            c.dma("sync", lambda h, t0=t0: h.dma_start(
                out=dbg_y[:, :, t0:t0 + TC].transpose([1, 0, 2]),
                in_=y_sb[:, :].rearrange("p (d t) -> p d t", d=16)))
            c.dma("sync", lambda h, t0=t0: h.dma_start(
                out=dbg_W[:, t0:t0 + TC], in_=W_[:, :]))
            c.dwait("vector", "sync")
            c.dwait("scalar", "sync")
        for of in range(16):
            sl = slice(of * TC, (of + 1) * TC)
            c.op("vector", lambda h, sl=sl: h.tensor_tensor(
                out=y_sb[:, sl], in0=y_sb[:, sl], in1=zb[:, sl],
                op=ALU.mult), tick=True)
            c.op("scalar", lambda h, sl=sl: h.activation(
                out=sq1[:, :], in_=y_sb[:, sl], func=AF.Square), tick=True)
            c.op("tensor", lambda h, of=of: h.matmul(
                out=pS[:, :], lhsT=ones_col[:, 0:1], rhs=sq1[:, :],
                start=(of == 0), stop=(of == 15)), tick=(of == 15))
        c.op("vector", lambda h: h.tensor_scalar(out=rtmp[:, :], in0=pS[0:1, :],
             scalar1=1.0 / DIN0, scalar2=1e-5, op0=ALU.mult, op1=ALU.add))
        c.op("vector", lambda h: h.reciprocal(out=grms[:, :], in_=rtmp[:, :]),
             tick=True)
        c.op("scalar", lambda h: h.sqrt(out=grms[:, :], in_=grms[:, :]),
             tick=True)
        c.op("tensor", lambda h: h.matmul(out=pB1[:, :], lhsT=ones_row[0:1, :],
             rhs=grms[:, :], start=True, stop=True), tick=True)
        for of in range(16):
            sl = slice(of * TC, (of + 1) * TC)
            c.op("vector", lambda h, of=of, sl=sl: h.scalar_tensor_tensor(
                out=yn_sb[:, sl], in0=y_sb[:, sl], scalar=gnw[:, of:of + 1],
                in1=pB1[:, :], op0=ALU.mult, op1=ALU.mult))
        c.tick_last("vector")
        for of in range(ND):
            def mmo(h, of=of):
                for k in range(NPO):
                    ins = h.matmul(out=pIP[of % 2][:, :],
                                   lhsT=POT[k][:, of * 128:(of + 1) * 128],
                                   rhs=yn_sb[:, k * TC:(k + 1) * TC],
                                   start=(k == 0), stop=(k == NPO - 1))
                return ins
            c.op("tensor", mmo, tick=True)
            st = s0a if of % 2 == 0 else s0b
            if of >= 2:
                c.dwait("vector", "gpsimd")
            c.op("vector", lambda h, of=of, st=st: h.tensor_copy(
                out=st[:, :], in_=pIP[of % 2][:, :]), tick=True)
            c.dma("gpsimd", lambda h, of=of, st=st, t0=t0: h.dma_start(
                out=S0_dram[of, :, t0:t0 + TC], in_=st[:, :],
                accum_op=ALU.add))
        c.barrier()


def emit_l0ffn(nc, ctx, c, x_dram, S0_dram, Wd, Cst):
    TCF = 256
    NCF = TOWN // TCF
    NOF = HID0 // 128
    wgb = [sbt(nc, ctx, f"wgb{k}", [128, HID0], BF16) for k in range(ND)]
    wub = [sbt(nc, ctx, f"wub{k}", [128, HID0], BF16) for k in range(ND)]
    wdb = [sbt(nc, ctx, f"wdb{k}", [128, D], BF16) for k in range(NOF)]
    for k in range(ND):
        c.dma("gpsimd", lambda h, k=k: h.dma_start(
            out=wgb[k][:], in_=Wd["wgT"][k * 128:(k + 1) * 128, :]))
        c.dma("gpsimd", lambda h, k=k: h.dma_start(
            out=wub[k][:], in_=Wd["wuT"][k * 128:(k + 1) * 128, :]))
    gumk = c.mark("gpsimd")
    for k in range(NOF):
        c.dma("gpsimd", lambda h, k=k: h.dma_start(
            out=wdb[k][:], in_=Wd["wdT"][k * 128:(k + 1) * 128, :]))
    xt = sbt(nc, ctx, "f0xt", [128, ND * TCF], F32)
    sqt = sbt(nc, ctx, "f0sq", [128, ND * TCF], F32)
    xnb = sbt(nc, ctx, "f0xn", [128, ND * TCF], BF16)
    gub = sbt(nc, ctx, "f0gu", [128, NOF * TCF], BF16)
    sgt = sbt(nc, ctx, "f0sg", [128, TCF], BF16)
    f0t = [sbt(nc, ctx, f"f0o{i}", [128, TCF], F32) for i in range(2)]
    rows = make_rows(nc, ctx, "f0r")
    pS = pst(nc, ctx, "f0pS", [1, TCF])
    pQ = pst(nc, ctx, "f0pQ", [1, TCF])
    pB1 = pst(nc, ctx, "f0pB1", [128, TCF])
    pB2 = pst(nc, ctx, "f0pB2", [128, TCF])
    pIP = [pst(nc, ctx, f"f0pIP{i}", [128, TCF]) for i in range(2)]
    ones_col, ones_row = Cst["ones_col"], Cst["ones_row"]
    c.dwait_cnt("tensor", "gpsimd", gumk)
    c.dwait_cnt("vector", "gpsimd", gumk)
    c.dwait_cnt("scalar", "gpsimd", gumk)
    for fc in range(NCF):
        t0 = fc * TCF
        c.dma("sync", lambda h, t0=t0: h.dma_start(
            out=xt[:, :].rearrange("p (d t) -> p d t", d=ND),
            in_=x_dram[:, :, t0:t0 + TCF].transpose([1, 0, 2])))
        c.dwait("scalar", "sync")
        c.dwait("tensor", "sync")
        c.dwait("vector", "sync")
        emit_ln_stats(nc, c, xt, sqt, ones_col, pS, pQ, rows, eps2=None, W=TCF)
        c.op("tensor", lambda h: h.matmul(out=pB1[:, 0:TCF], lhsT=ones_row[0:1, :],
             rhs=rows["g1"][:, 0:TCF], start=True, stop=True), tick=True)
        c.op("tensor", lambda h: h.matmul(out=pB2[:, 0:TCF], lhsT=ones_row[0:1, :],
             rhs=rows["mg1"][:, 0:TCF], start=True, stop=True), tick=True)
        for d in range(ND):
            sl = slice(d * TCF, (d + 1) * TCF)
            c.op("vector", lambda h, sl=sl: h.tensor_tensor(
                out=sqt[:, sl], in0=xt[:, sl], in1=pB1[:, 0:TCF], op=ALU.mult))
            c.op("vector", lambda h, sl=sl: h.tensor_tensor(
                out=xnb[:, sl], in0=sqt[:, sl], in1=pB2[:, 0:TCF],
                op=ALU.subtract))
        c.tick_last("vector")
        def mmg(h, of):
            for k in range(ND):
                ins = h.matmul(out=pIP[0][:, 0:TCF],
                               lhsT=wgb[k][:, of * 128:(of + 1) * 128],
                               rhs=xnb[:, k * TCF:(k + 1) * TCF],
                               start=(k == 0), stop=(k == ND - 1))
            return ins

        def mmu(h, of):
            for k in range(ND):
                ins = h.matmul(out=pIP[1][:, 0:TCF],
                               lhsT=wub[k][:, of * 128:(of + 1) * 128],
                               rhs=xnb[:, k * TCF:(k + 1) * TCF],
                               start=(k == 0), stop=(k == ND - 1))
            return ins
        for of in range(NOF):
            c.op("tensor", lambda h, of=of: mmg(h, of), tick=True)
            c.op("tensor", lambda h, of=of: mmu(h, of), tick=True)
            c.op("scalar", lambda h: h.activation(
                out=sgt[:, :], in_=pIP[0][:, 0:TCF], func=AF.Silu, scale=QS),
                tick=True)
            c.op("vector", lambda h, of=of: h.scalar_tensor_tensor(
                out=gub[:, of * TCF:(of + 1) * TCF], in0=pIP[1][:, 0:TCF],
                scalar=QS, in1=sgt[:, :], op0=ALU.mult, op1=ALU.mult),
                tick=True)
        def ev_dn(of2, t0):
            ft = f0t[of2 % 2]
            if of2 >= 2:
                c.dwait("vector", "gpsimd")
            c.op("vector", lambda h, of2=of2, ft=ft: h.tensor_scalar(
                out=ft[:, :], in0=pIP[of2 % 2][:, 0:TCF], scalar1=QS,
                scalar2=None, op0=ALU.mult), tick=True)
            c.dma("gpsimd", lambda h, of2=of2, ft=ft, t0=t0: h.dma_start(
                out=S0_dram[of2, :, t0:t0 + TCF], in_=ft[:, :],
                accum_op=ALU.add))
        c.dwait("tensor", "gpsimd")
        for of2 in range(ND):
            def mmd(h, of2=of2):
                for k in range(NOF):
                    ins = h.matmul(out=pIP[of2 % 2][:, 0:TCF],
                                   lhsT=wdb[k][:, of2 * 128:(of2 + 1) * 128],
                                   rhs=gub[:, k * TCF:(k + 1) * TCF],
                                   start=(k == 0), stop=(k == NOF - 1))
                return ins
            c.op("tensor", mmd, tick=True)
            if of2 >= 1:
                ev_dn(of2 - 1, t0)
        ev_dn(ND - 1, t0)
        c.barrier()


def load_consts(nc, ctx, c):
    Cst = {}
    decls = [("ones_col", [128, 1], F32), ("ones_row", [1, 128], F32),
             ("id128", [128, 128], F32), ("idbf", [128, 128], BF16),
             ("tri", [128, 128], F32), ("ones32", [32, 128], F32),
             ("negones", [1, TC], F32), ("idbf64", [128, 64], BF16),
             ("ones_row_bf", [1, 128], BF16)]
    for nm, shape, dt in decls:
        dr = nc.dram_tensor("c_" + nm, shape, dt, kind="ExternalInput").ap()
        t = sbt(nc, ctx, "cs_" + nm, shape, dt)
        c.dma("sync", lambda h, t=t, dr=dr: h.dma_start(out=t[:], in_=dr[:, :]))
        Cst[nm] = t
    return Cst


def host_consts():
    return {
        "c_ones_col": np.ones((128, 1), np.float32),
        "c_ones_row": np.ones((1, 128), np.float32),
        "c_id128": np.eye(128, dtype=np.float32),
        "c_idbf": np.eye(128).astype(ml_dtypes.bfloat16),
        "c_tri": np.triu(np.ones((128, 128), np.float32)),
        "c_ones32": np.ones((32, 128), np.float32),
        "c_negones": -np.ones((1, TC), np.float32),
        "c_idbf64": np.vstack([np.eye(64), np.eye(64)]).astype(ml_dtypes.bfloat16),
        "c_ones_row_bf": np.ones((1, 128), ml_dtypes.bfloat16),
    }


def col_form(v, nt=None):
    v = np.asarray(v, np.float32)
    n = v.shape[0]
    nt = n // 128
    return np.ascontiguousarray(v.reshape(nt, 128).T)


def prep_l0_weights(inp, rev):
    d = "bwd" if rev else "fwd"
    W = {}
    W["inWT"] = np.ascontiguousarray(inp[f"L0_attn_{d}_in_W"].T)
    W["outWraw"] = np.ascontiguousarray(inp[f"L0_attn_{d}_out_W"])
    pj = inp["L0_attn_proj_W"]
    W["projT"] = np.ascontiguousarray((pj[:, 1024:] if rev else pj[:, :1024]).T)
    cw = inp[f"L0_attn_{d}_conv_w"]
    W["convw"] = np.ascontiguousarray(
        cw.reshape(18, 128, 4).transpose(1, 0, 2).reshape(128, 72))
    W["convb"] = col_form(inp[f"L0_attn_{d}_conv_b"])
    W["dtb"] = np.asarray(inp[f"L0_attn_{d}_dt_bias"], np.float32).reshape(32, 1)
    W["alog"] = np.asarray(inp[f"L0_attn_{d}_A_log"], np.float32).reshape(32, 1)
    W["drep"] = col_form(np.repeat(np.asarray(inp[f"L0_attn_{d}_D"], np.float32), HD))
    W["gnw"] = col_form(inp[f"L0_attn_{d}_gnorm_w"])
    W["projb"] = col_form(inp["L0_attn_proj_b"])
    W["nwc"] = col_form(inp["L0_attn_norm_w"])
    W["wgT"] = np.ascontiguousarray(inp["L0_ffn_w_gate"].T)
    W["wuT"] = np.ascontiguousarray(inp["L0_ffn_w_up"].T)
    W["wdT"] = np.ascontiguousarray(inp["L0_ffn_w_down"].T)
    return W


L0W_SHAPES = [("inWT", [D, 4384], F32), ("outWraw", [D, DIN0], F32),
              ("projT", [D, D], F32), ("convw", [128, 72], F32),
              ("convb", [128, 18], F32), ("dtb", [32, 1], F32),
              ("alog", [32, 1], F32), ("drep", [128, 16], F32),
              ("gnw", [128, 16], F32), ("projb", [128, ND], F32),
              ("nwc", [128, ND], F32),
              ("wgT", [D, HID0], I32), ("wuT", [D, HID0], I32),
              ("wdT", [HID0, D], I32)]


def declare_l0_scratch(nc, dbg=False):
    kind = "ExternalOutput" if dbg else "Internal"
    return {
        "xs": nc.dram_tensor("sp_xs", [16, 128, T], BF16, kind=kind).ap(),
        "z": nc.dram_tensor("sp_z", [16, 128, T], BF16, kind=kind).ap(),
        "dts": nc.dram_tensor("sp_dts", [32, T], F32, kind=kind).ap(),
    }


def emit_l0(nc, ctx, c, x_dram, S0_dram, Wd, Cst, parts="abf", dbg=False):
    sp = declare_l0_scratch(nc, dbg)
    dbg_h = (nc.dram_tensor("dbg_h", [ND, 128, T], BF16,
                            kind="ExternalOutput").ap() if dbg else None)
    dbg_y = (nc.dram_tensor("dbg_y", [16, 128, T], F32,
                            kind="ExternalOutput").ap() if dbg else None)
    dbg_W = (nc.dram_tensor("dbg_W", [32, T], F32,
                            kind="ExternalOutput").ap() if dbg else None)
    dbg_ssd = (nc.dram_tensor("dbg_ssd", [128, 512], F32,
                              kind="ExternalOutput").ap() if dbg else None)
    dbg2 = (nc.dram_tensor("dbg2", [128, 256], F32,
                           kind="ExternalOutput").ap() if dbg else None)
    with ExitStack() as mctx:
        shared = {
            "Bb": sbt(nc, mctx, "l0Bb", [128, T], BF16),
            "Cb": sbt(nc, mctx, "l0Cb", [128, T], BF16),
        }
        with ExitStack() as actx:
            emit_l0passA(nc, actx, c, x_dram, S0_dram, sp, Wd, Cst, shared,
                         dbg_h)
            c.barrier()
        if "b" in parts:
            with ExitStack() as bctx:
                emit_l0passB(nc, bctx, c, S0_dram, sp, Wd, Cst, shared, dbg_y, dbg_W, dbg_ssd, dbg2)
                c.barrier()
    if "f" in parts:
        with ExitStack() as fctx:
            emit_l0ffn(nc, fctx, c, x_dram, S0_dram, Wd, Cst)
            c.barrier()


def build_l0(parts="abf", dbg=False):
    nc = bass.Bass("TRN2", target_bir_lowering=False)
    x_dram = nc.dram_tensor("x", [ND, 128, T], F32, kind="ExternalInput").ap()
    S0_dram = nc.dram_tensor("S0", [ND, 128, T], F32, kind="ExternalOutput").ap()
    Wd = {}
    for nm, shape, dt in L0W_SHAPES:
        Wd[nm] = nc.dram_tensor("w_" + nm, shape, dt, kind="ExternalInput").ap()
    with ExitStack() as ctx:
        c = Ctl(nc, ctx)
        Cst = load_consts(nc, ctx, c)
        emit_l0(nc, ctx, c, x_dram, S0_dram, Wd, Cst, parts, dbg)
        with nc.Block() as block:
            c.emit(block)
    return nc


def emit_combine(nc, ctx, c, x_dram, Sown, Sprev, x1_dram, rev_prev=False,
                 masks=None, G=None):
    xt = sbt(nc, ctx, "cbx", [128, TC], F32)
    at = sbt(nc, ctx, "cba", [128, TC], F32)
    bt = sbt(nc, ctx, "cbb", [128, TC], F32)
    g0 = sbt(nc, ctx, "cbg0", [128, TC], F32)
    g1t = sbt(nc, ctx, "cbg1", [128, TC], F32)
    ot = [sbt(nc, ctx, f"cbo{i}", [128, TC], F32) for i in range(2)]
    for tc in range(NCH):
        t0 = tc * TC
        tm = (NCH - 1 - tc) * TC
        for d in range(ND):
            c.dwait("vector", "sync")
            c.dma("sync", lambda h, d=d, t0=t0: h.dma_start(
                out=xt[:, :], in_=x_dram[d, :, t0:t0 + TC]))
            c.dma("sync", lambda h, d=d, t0=t0: h.dma_start(
                out=at[:, :], in_=Sown[d, :, t0:t0 + TC]))
            if masks is None:
                c.dma("sync", lambda h, d=d, t0=t0: h.dma_start(
                    out=bt[:, :], in_=Sprev[d, :, t0:t0 + TC]))
            else:
                c.dma("sync", lambda h, d=d, tm=tm: h.dma_start(
                    out=g0[:, :], in_=G[0][d, :, tm:tm + TC]))
                c.dma("sync", lambda h, d=d, tm=tm: h.dma_start(
                    out=g1t[:, :], in_=G[1][d, :, tm:tm + TC]))
            c.dwait("vector", "sync")
            o = ot[d % 2]
            c.op("vector", lambda h, o=o: h.tensor_tensor(
                out=o[:, :], in0=xt[:, :], in1=at[:, :], op=ALU.add))
            if masks is None:
                c.op("vector", lambda h, o=o: h.tensor_tensor(
                    out=o[:, :], in0=o[:, :], in1=bt[:, :], op=ALU.add),
                    tick=True)
            else:
                c.op("vector", lambda h, o=o: h.scalar_tensor_tensor(
                    out=o[:, :], in0=g0[:, ::-1], scalar=masks[:, 0:1],
                    in1=o[:, :], op0=ALU.mult, op1=ALU.add))
                c.op("vector", lambda h, o=o: h.scalar_tensor_tensor(
                    out=o[:, :], in0=g1t[:, ::-1], scalar=masks[:, 1:2],
                    in1=o[:, :], op0=ALU.mult, op1=ALU.add), tick=True)
            c.dma("sync", lambda h, d=d, t0=t0, o=o: h.dma_start(
                out=x1_dram[d, :, t0:t0 + TC], in_=o[:, :]))
    c.barrier()


def emit_l1passA(nc, ctx, c, x1_dram, sp1, Wd, Cst, shared1):
    w1in = [sbt(nc, ctx, f"w1in{k}", [128, 2 * DIN1], BF16) for k in range(ND)]
    convw = sbt(nc, ctx, "cvw1", [128, 64], F32)
    convb = sbt(nc, ctx, "cvb1", [128, 16], F32)
    lngr = sbt(nc, ctx, "lngr", [1, D], F32)
    lnbr = sbt(nc, ctx, "lnbr", [1, D], F32)
    lngc = sbt(nc, ctx, "lngc", [128, ND], F32)
    xt = sbt(nc, ctx, "a1xt", [128, ND * TC], F32)
    sqt = sbt(nc, ctx, "a1sq", [128, ND * TC], F32)
    xn2b = sbt(nc, ctx, "a1xn2", [128, ND * TC], BF16)
    xnst = sbt(nc, ctx, "a1xnst", [128, TC], BF16)
    xnf = sbt(nc, ctx, "a1xnf", [128, TC], F32)
    wgf = sbt(nc, ctx, "a1wgf", [128, ND * NE], F32)
    scst = sbt(nc, ctx, "a1scst", [8, TC], F32)
    wgsum = sbt(nc, ctx, "a1wgsum", [8, 1], F32)
    xbc = [sbt(nc, ctx, f"a1xbc{i}", [128, 3 + TC], F32) for i in range(NTI1)]
    halo = sbt(nc, ctx, "a1halo", [128, 48], F32)
    ct = sbt(nc, ctx, "a1ct", [128, TC], F32)
    stg = [sbt(nc, ctx, f"a1stg{i}", [128, TC], BF16) for i in range(2)]
    stf = [sbt(nc, ctx, f"a1stf{i}", [128, TC], F32) for i in range(2)]
    rows = make_rows(nc, ctx, "a1r")
    pS = pst(nc, ctx, "a1pS", [1, TC])
    pQ = pst(nc, ctx, "a1pQ", [1, TC])
    pB1 = pst(nc, ctx, "a1pB1", [128, TC])
    pB2 = pst(nc, ctx, "a1pB2", [128, TC])
    pIP = [pst(nc, ctx, f"a1pIP{i}", [128, TC]) for i in range(2)]
    pSC8 = pst(nc, ctx, "a1pSC8", [8, TC])
    ones_col, ones_row = Cst["ones_col"], Cst["ones_row"]

    for k in range(ND):
        c.dma("gpsimd", lambda h, k=k: h.dma_start(
            out=w1in[k][:], in_=Wd["in1T"][k * 128:(k + 1) * 128, :]))
    c.dma("sync", lambda h: h.dma_start(out=wgf[:], in_=Wd["WgT"][:, :]))
    for nm, dst in [("conv1w", convw), ("conv1b", convb),
                    ("lngr", lngr), ("lnbr", lnbr), ("lngc", lngc)]:
        c.dma("sync", lambda h, nm=nm, dst=dst: h.dma_start(
            out=dst[:], in_=Wd[nm][:, :]))
    for e in ("scalar", "vector", "tensor"):
        c.dwait(e, "sync")
        c.dwait(e, "gpsimd")
    c.op("vector", lambda h: h.memset(halo[:, :], 0.0), tick=True)

    def mmwsum(h):
        for d in range(ND):
            ins = h.matmul(out=pSC8[:, 0:1], lhsT=wgf[:, d * NE:(d + 1) * NE],
                           rhs=Cst["ones_col"][:, 0:1],
                           start=(d == 0), stop=(d == ND - 1))
        return ins
    c.op("tensor", mmwsum, tick=True)
    c.op("vector", lambda h: h.tensor_copy(out=wgsum[:, :],
         in_=pSC8[:, 0:1]), tick=True)

    for tc in range(NCH):
        t0 = tc * TC
        c.dma("sync", lambda h, t0=t0: h.dma_start(
            out=xt[:, :].rearrange("p (d t) -> p d t", d=ND),
            in_=x1_dram[:, :, t0:t0 + TC].transpose([1, 0, 2])))
        for e in ("scalar", "vector", "tensor"):
            c.dwait(e, "sync")
        emit_ln_stats(nc, c, xt, sqt, ones_col, pS, pQ, rows, eps2=1e-5)
        if tc < NCHO:
            c.op("tensor", lambda h: h.matmul(out=pB1[:, :], lhsT=ones_row[0:1, :],
                 rhs=rows["g1"][:, :], start=True, stop=True), tick=True)
            c.op("tensor", lambda h: h.matmul(out=pB2[:, :], lhsT=ones_row[0:1, :],
                 rhs=rows["mg1"][:, :], start=True, stop=True), tick=True)
            for d in range(ND):
                sl = slice(d * TC, (d + 1) * TC)
                c.dwait("vector", "sync")
                c.op("vector", lambda h, sl=sl: h.tensor_tensor(
                    out=sqt[:, sl], in0=xt[:, sl], in1=pB1[:, :], op=ALU.mult))
                c.op("vector", lambda h, sl=sl: h.tensor_tensor(
                    out=xnst[:, :], in0=sqt[:, sl], in1=pB2[:, :],
                    op=ALU.subtract), tick=True)
                c.dma("sync", lambda h, d=d, t0=t0: h.dma_start(
                    out=sp1["xn"][d, :, t0:t0 + TC], in_=xnst[:, :]))
            def mmwx(h):
                for d in range(ND):
                    ins = h.matmul(out=pSC8[:, :],
                                   lhsT=wgf[:, d * NE:(d + 1) * NE],
                                   rhs=xt[:, d * TC:(d + 1) * TC],
                                   start=(d == 0), stop=(d == ND - 1))
                return ins
            c.op("tensor", mmwx, tick=True)
            c.op("scalar", lambda h: h.copy(out=scst[:, :],
                 in_=pSC8[:, :]), tick=True)
            c.fence("vector")
            c.op("vector", lambda h: h.tensor_tensor(out=scst[:, :],
                 in0=scst[:, :], in1=pB1[0:8, :], op=ALU.mult), tick=True)
            c.op("vector", lambda h: h.tensor_scalar(out=xnf[0:8, :],
                 in0=pB2[0:8, :], scalar1=wgsum[:, 0:1], scalar2=None,
                 op0=ALU.mult), tick=True)
            c.fence("vector")
            c.op("vector", lambda h: h.tensor_tensor(out=scst[:, :],
                 in0=scst[:, :], in1=xnf[0:8, :], op=ALU.subtract), tick=True)
            c.dma("sync", lambda h, t0=t0: h.dma_start(
                out=sp1["scores"][:, t0:t0 + TC], in_=scst[:, :]))
        c.op("tensor", lambda h: h.matmul(out=pB1[:, :], lhsT=ones_row[0:1, :],
             rhs=rows["g12"][:, :], start=True, stop=True), tick=True)
        for d in range(ND):
            sl = slice(d * TC, (d + 1) * TC)

            def mmr(h, d=d):
                h.matmul(out=pB2[:, :], lhsT=lngr[0:1, d * 128:(d + 1) * 128],
                         rhs=rows["mg12"][:, :], start=True, stop=False)
                return h.matmul(out=pB2[:, :],
                                lhsT=lnbr[0:1, d * 128:(d + 1) * 128],
                                rhs=Cst["negones"][0:1, :],
                                start=False, stop=True)
            c.op("tensor", mmr, tick=True)
            c.op("vector", lambda h, sl=sl: h.tensor_tensor(
                out=sqt[:, sl], in0=xt[:, sl], in1=pB1[:, :], op=ALU.mult))
            c.op("vector", lambda h, d=d, sl=sl: h.scalar_tensor_tensor(
                out=xn2b[:, sl], in0=sqt[:, sl], scalar=lngc[:, d:d + 1],
                in1=pB2[:, :], op0=ALU.mult, op1=ALU.subtract), tick=True)
        def cons1(of, t0):
            if of < 16:
                c.op("scalar", lambda h, of=of: h.copy(
                    out=xbc[of][:, 3:3 + TC], in_=pIP[of % 2][:, :]), tick=True)
            else:
                stz = stg[of % 2]
                c.dwait("scalar", "sync")
                c.op("scalar", lambda h, of=of, stz=stz: h.activation(
                    out=stz[:, :], in_=pIP[of % 2][:, :], func=AF.Silu),
                    tick=True)
                c.dma("sync", lambda h, of=of, stz=stz, t0=t0: h.dma_start(
                    out=sp1["z"][of - 16, :, t0:t0 + TC], in_=stz[:, :]))

        for of in range(32):
            def mms(h, of=of):
                for k in range(ND):
                    ins = h.matmul(out=pIP[of % 2][:, :],
                                   lhsT=w1in[k][:, of * 128:(of + 1) * 128],
                                   rhs=xn2b[:, k * TC:(k + 1) * TC],
                                   start=(k == 0), stop=(k == ND - 1))
                return ins
            c.op("tensor", mms, tick=True)
            if of >= 1:
                cons1(of - 1, t0)
        cons1(31, t0)
        for ti in range(NTI1):
            c.op("vector", lambda h, ti=ti: h.tensor_copy(
                out=xbc[ti][:, 0:3], in_=halo[:, ti * 3:ti * 3 + 3]))
        c.tick_last("vector")
        for ti in range(NTI1):
            c.op("vector", lambda h, ti=ti: h.tensor_scalar(
                out=ct[:, :], in0=xbc[ti][:, 0:TC],
                scalar1=convw[:, ti * 4:ti * 4 + 1],
                scalar2=convb[:, ti:ti + 1], op0=ALU.mult, op1=ALU.add))
            for k in range(1, 4):
                c.op("vector", lambda h, ti=ti, k=k: h.scalar_tensor_tensor(
                    out=ct[:, :], in0=xbc[ti][:, k:k + TC],
                    scalar=convw[:, ti * 4 + k:ti * 4 + k + 1], in1=ct[:, :],
                    op0=ALU.mult, op1=ALU.add))
            c.op("vector", lambda h, ti=ti: h.tensor_copy(
                out=halo[:, ti * 3:ti * 3 + 3], in_=xbc[ti][:, TC:TC + 3]),
                tick=True)
            stz = stf[ti % 2]
            c.dwait("scalar", "sync")
            c.op("scalar", lambda h, stz=stz: h.activation(
                out=stz[:, :], in_=ct[:, :], func=AF.Silu), tick=True)
            c.dma("sync", lambda h, ti=ti, stz=stz, t0=t0: h.dma_start(
                out=sp1["xc"][ti, :, t0:t0 + TC], in_=stz[:, :]))
        c.barrier()


def emit_l1passA2(nc, ctx, c, sp1, Wd, Cst, shared1):
    xpd = [sbt(nc, ctx, f"xpd{k}", [128, DTR], BF16) for k in range(NTI1)]
    xpb = [sbt(nc, ctx, f"xpb{k}", [128, NST], BF16) for k in range(NTI1)]
    xpc = [sbt(nc, ctx, f"xpc{k}", [128, NST], BF16) for k in range(NTI1)]
    dtw = sbt(nc, ctx, "dtw1", [64, DIN1], BF16)
    dtb1 = sbt(nc, ctx, "dtb1", [128, 16], F32)
    xcb = sbt(nc, ctx, "a2xcb", [128, NTI1 * TC], BF16)
    dtfm = sbt(nc, ctx, "a2dtfm", [64, TC], BF16)
    stf = [sbt(nc, ctx, f"a2stf{i}", [128, TC], F32) for i in range(2)]
    sp_w = sbt(nc, ctx, "a2spw", [128, TC], F32)
    sp_a = sbt(nc, ctx, "a2spa", [128, TC], F32)
    sp_e = sbt(nc, ctx, "a2spe", [128, TC], F32)
    pB1 = pst(nc, ctx, "a2pB1", [128, TC])
    pB2 = pst(nc, ctx, "a2pB2", [128, TC])
    pIP = [pst(nc, ctx, f"a2pIP{i}", [128, TC]) for i in range(2)]
    pDT = pB2[0:64, :]
    pBC = pB1[0:16, :]
    Bf, Cf = shared1["Bf"], shared1["Cf"]
    for k in range(NTI1):
        c.dma("gpsimd", lambda h, k=k: h.dma_start(
            out=xpd[k][:], in_=Wd["xpdT"][k * 128:(k + 1) * 128, :]))
        c.dma("gpsimd", lambda h, k=k: h.dma_start(
            out=xpb[k][:], in_=Wd["xpbT"][k * 128:(k + 1) * 128, :]))
        c.dma("gpsimd", lambda h, k=k: h.dma_start(
            out=xpc[k][:], in_=Wd["xpcT"][k * 128:(k + 1) * 128, :]))
    c.dma("gpsimd", lambda h: h.dma_start(out=dtw[:], in_=Wd["dtwT"][:, :]))
    c.dma("sync", lambda h: h.dma_start(out=dtb1[:], in_=Wd["dtb1"][:, :]))
    for e in ("scalar", "vector", "tensor"):
        c.dwait(e, "sync")
        c.dwait(e, "gpsimd")
    for tc in range(NCH):
        t0 = tc * TC
        c.dma("gpsimd", lambda h, t0=t0: h.dma_start(
            out=xcb[:, :].rearrange("p (d t) -> p d t", d=NTI1),
            in_=sp1["xc"][:, :, t0:t0 + TC].transpose([1, 0, 2])))
        for e in ("scalar", "vector", "tensor"):
            c.dwait(e, "gpsimd")
            c.dwait(e, "sync")
        def mmdt(h):
            for k in range(NTI1):
                ins = h.matmul(out=pDT, lhsT=xpd[k][:, :],
                               rhs=xcb[:, k * TC:(k + 1) * TC],
                               start=(k == 0), stop=(k == NTI1 - 1))
            return ins
        c.op("tensor", mmdt, tick=True)
        c.op("vector", lambda h: h.tensor_copy(out=dtfm[:, :], in_=pDT),
             tick=True)

        def mmb(h):
            for k in range(NTI1):
                ins = h.matmul(out=pBC, lhsT=xpb[k][:, :],
                               rhs=xcb[:, k * TC:(k + 1) * TC],
                               start=(k == 0), stop=(k == NTI1 - 1))
            return ins
        c.op("tensor", mmb, tick=True)
        c.op("vector", lambda h, t0=t0: h.tensor_copy(
            out=Bf[:, t0:t0 + TC], in_=pBC), tick=True)

        def mmc(h):
            for k in range(NTI1):
                ins = h.matmul(out=pBC, lhsT=xpc[k][:, :],
                               rhs=xcb[:, k * TC:(k + 1) * TC],
                               start=(k == 0), stop=(k == NTI1 - 1))
            return ins
        c.op("tensor", mmc, tick=True)
        c.op("vector", lambda h, t0=t0: h.tensor_copy(
            out=Cf[:, t0:t0 + TC], in_=pBC), tick=True)
        for of in range(NTI1):
            c.op("tensor", lambda h, of=of: h.matmul(
                out=pIP[of % 2][:, :], lhsT=dtw[:, of * 128:(of + 1) * 128],
                rhs=dtfm[:, :], start=True, stop=True), tick=True)
            softplus_chain(nc, c, sp_w[:, :], pIP[of % 2][:, :],
                           dtb1[:, of:of + 1], sp_w, sp_a, sp_e, sp_a, 128)
            c.dwait("vector", "sync")
            stz = stf[of % 2]
            c.op("vector", lambda h, stz=stz: h.tensor_copy(
                out=stz[:, :], in_=sp_w[:, :]), tick=True)
            c.dma("sync", lambda h, of=of, stz=stz, t0=t0: h.dma_start(
                out=sp1["delta"][of, :, t0:t0 + TC], in_=stz[:, :]))
        c.barrier()


def emit_l1passB(nc, ctx, c, sp1, S1ab, Wd, Cst, shared1):
    w1out = [sbt(nc, ctx, f"w1out{k}", [128, D], BF16) for k in range(NTI1)]
    A1c = sbt(nc, ctx, "A1c", [128, NTI1 * NST], F32)
    D1c = sbt(nc, ctx, "D1c", [128, 16], F32)
    zbg = sbt(nc, ctx, "b1zbg", [128, 4 * TC], BF16)
    Bbc = sbt(nc, ctx, "b1Bbc", [128, NST * TC], BF16)
    Cbc = sbt(nc, ctx, "b1Cbc", [128, NST * TC], BF16)
    bflat = sbt(nc, ctx, "b1bfl", [1, 4 * TC], BF16)
    dA = [sbt(nc, ctx, f"b1dA{i}", [128, NGRP * TC], F32) for i in range(2)]
    dBu = sbt(nc, ctx, "b1dBu", [128, NGRP * TC], F32)
    hb = sbt(nc, ctx, "b1h", [128, NGRP * TC], F32)
    delg = sbt(nc, ctx, "b1delg", [128, 4 * TC], F32)
    xcg = sbt(nc, ctx, "b1xcg", [128, 4 * TC], F32)
    du = sbt(nc, ctx, "b1du", [128, TC], F32)
    ycon = sbt(nc, ctx, "b1ycon", [128, TC], F32)
    yg = sbt(nc, ctx, "b1yg", [128, NTI1 * TC], BF16)
    s1t = [sbt(nc, ctx, f"b1s1{i}", [128, TC], F32) for i in range(2)]
    S1st = sbt(nc, ctx, "b1state", [128, 256], F32)
    pB1 = pst(nc, ctx, "b1pB1", [128, TC])
    pIP = [pst(nc, ctx, f"b1pIP{i}", [128, TC]) for i in range(2)]
    Bf, Cf = shared1["Bf"], shared1["Cf"]
    ones_row = Cst["ones_row"]

    for k in range(NTI1):
        c.dma("gpsimd", lambda h, k=k: h.dma_start(
            out=w1out[k][:], in_=Wd["out1T"][k * 128:(k + 1) * 128, :]))
    c.dma("sync", lambda h: h.dma_start(out=A1c[:], in_=Wd["A1col"][:, :]))
    c.dma("sync", lambda h: h.dma_start(out=D1c[:], in_=Wd["D1col"][:, :]))
    for e in ("scalar", "vector", "tensor"):
        c.dwait(e, "sync")
        c.dwait(e, "gpsimd")
    c.op("scalar", lambda h: h.activation(out=A1c[:, :], in_=A1c[:, :],
         func=AF.Exp), tick=True)
    c.op("vector", lambda h: h.tensor_scalar(out=A1c[:, :], in0=A1c[:, :],
         scalar1=-1.0, scalar2=None, op0=ALU.mult), tick=True)

    for tc in range(NCH):
        t0 = tc * TC
        for srcf, dstb in ((Bf, Bbc), (Cf, Cbc)):
            for g4 in range(4):
                c.dma("gpsimd", lambda h, srcf=srcf, g4=g4, t0=t0:
                      h.dma_start(
                          out=bflat[0:1, :].rearrange("a (b t) -> a b t", b=4),
                          in_=srcf[g4 * 4:(g4 + 1) * 4, t0:t0 + TC]))
                c.dwait("tensor", "gpsimd")
                for j in range(4):
                    n = g4 * 4 + j
                    c.op("tensor", lambda h, j=j: h.matmul(
                        out=pB1[:, :], lhsT=Cst["ones_row_bf"][0:1, :],
                        rhs=bflat[0:1, j * TC:(j + 1) * TC],
                        start=True, stop=True), tick=True)
                    c.op("scalar", lambda h, n=n, dstb=dstb: h.copy(
                        out=dstb[:, n * TC:(n + 1) * TC], in_=pB1[:, :]),
                        tick=True)
        for dg in range(4):
            c.dwait("vector", "sync")
            c.dwait("scalar", "sync")
            c.dma("sync", lambda h, dg=dg, t0=t0: h.dma_start(
                out=delg[:, :].rearrange("p (d t) -> p d t", d=4),
                in_=sp1["delta"][dg * 4:(dg + 1) * 4, :, t0:t0 + TC].transpose([1, 0, 2])))
            c.dma("sync", lambda h, dg=dg, t0=t0: h.dma_start(
                out=xcg[:, :].rearrange("p (d t) -> p d t", d=4),
                in_=sp1["xc"][dg * 4:(dg + 1) * 4, :, t0:t0 + TC].transpose([1, 0, 2])))
            c.dma("sync", lambda h, dg=dg, t0=t0: h.dma_start(
                out=zbg[:, :].rearrange("p (d t) -> p d t", d=4),
                in_=sp1["z"][dg * 4:(dg + 1) * 4, :, t0:t0 + TC].transpose([1, 0, 2])))
            for e in ("scalar", "vector"):
                c.dwait(e, "sync")
            def emit_exps(dl, grp, pg, tick):
                dsl = slice(dl * TC, (dl + 1) * TC)
                dtile = dg * 4 + dl
                for j in range(NGRP):
                    n = grp * NGRP + j
                    c.op("scalar", lambda h, dsl=dsl, j=j, dtile=dtile, n=n,
                         pg=pg:
                         h.activation(out=dA[pg][:, j * TC:(j + 1) * TC],
                                      in_=delg[:, dsl], func=AF.Exp,
                                      scale=A1c[:, dtile * NST + n:dtile * NST + n + 1]))
                if tick:
                    c.tick_last("scalar")

            units = [(dl, grp) for dl in range(4) for grp in range(2)]
            emit_exps(0, 0, 0, True)
            for ui, (dl, grp) in enumerate(units):
                dtile = dg * 4 + dl
                dsl = slice(dl * TC, (dl + 1) * TC)
                pg = ui % 2
                if grp == 0:
                    c.op("vector", lambda h, dsl=dsl: h.tensor_tensor(
                        out=du[:, :], in0=delg[:, dsl], in1=xcg[:, dsl],
                        op=ALU.mult), tick=True)
                bsl = slice(grp * NGRP * TC, (grp + 1) * NGRP * TC)
                c.op("vector", lambda h, bsl=bsl: h.tensor_tensor(
                    out=dBu[:, :].rearrange("p (g t) -> p g t", g=NGRP),
                    in0=du[:, :].unsqueeze(1).broadcast_to([128, NGRP, TC]),
                    in1=Bbc[:, bsl].rearrange("p (g t) -> p g t", g=NGRP),
                    op=ALU.mult), tick=True)
                c.fence("vector")
                if ui + 1 < len(units):
                    emit_exps(*units[ui + 1], 1 - pg, False)
                for j in range(NGRP):
                    n = grp * NGRP + j
                    if tc == 0:
                        ini = 0.0
                    else:
                        ini = S1st[:, dtile * NST + n:dtile * NST + n + 1]
                    c.op("vector", lambda h, j=j, ini=ini, pg=pg:
                         h.tensor_tensor_scan(
                             out=hb[:, j * TC:(j + 1) * TC],
                             data0=dA[pg][:, j * TC:(j + 1) * TC],
                             data1=dBu[:, j * TC:(j + 1) * TC],
                             initial=ini, op0=ALU.mult, op1=ALU.add))
                c.tick_last("vector")
                if ui + 1 < len(units):
                    c.tick_last("scalar")
                c.op("vector", lambda h, dtile=dtile, grp=grp:
                     h.tensor_copy(
                         out=S1st[:, dtile * NST + grp * NGRP:
                                  dtile * NST + (grp + 1) * NGRP],
                         in_=hb[:, TC - 1::TC]), tick=True)
                c.op("vector", lambda h, bsl=bsl: h.tensor_tensor(
                    out=hb[:, :].rearrange("p (g t) -> p g t", g=NGRP),
                    in0=hb[:, :].rearrange("p (g t) -> p g t", g=NGRP),
                    in1=Cbc[:, bsl].rearrange("p (g t) -> p g t", g=NGRP),
                    op=ALU.mult), tick=True)
                t4 = dBu
                c.op("vector", lambda h: h.tensor_tensor(
                    out=t4[:, 0:4 * TC], in0=hb[:, 0:4 * TC],
                    in1=hb[:, 4 * TC:8 * TC], op=ALU.add))
                c.op("vector", lambda h: h.tensor_tensor(
                    out=t4[:, 0:2 * TC], in0=t4[:, 0:2 * TC],
                    in1=t4[:, 2 * TC:4 * TC], op=ALU.add))
                if grp == 0:
                    c.op("vector", lambda h: h.tensor_tensor(
                        out=ycon[:, :], in0=t4[:, 0:TC], in1=t4[:, TC:2 * TC],
                        op=ALU.add), tick=True)
                else:
                    c.op("vector", lambda h, t4=t4: h.tensor_tensor(
                        out=t4[:, 2 * TC:3 * TC], in0=t4[:, 0:TC],
                        in1=t4[:, TC:2 * TC], op=ALU.add))
                    c.op("vector", lambda h, t4=t4: h.tensor_tensor(
                        out=ycon[:, :], in0=ycon[:, :],
                        in1=t4[:, 2 * TC:3 * TC], op=ALU.add), tick=True)
                if grp == 1:
                    c.op("vector", lambda h, dsl=dsl, dtile=dtile:
                         h.scalar_tensor_tensor(
                             out=ycon[:, :], in0=xcg[:, dsl],
                             scalar=D1c[:, dtile:dtile + 1], in1=ycon[:, :],
                             op0=ALU.mult, op1=ALU.add))
                    c.op("vector", lambda h, dtile=dtile, dsl=dsl:
                         h.tensor_tensor(
                             out=yg[:, dtile * TC:(dtile + 1) * TC],
                             in0=ycon[:, :], in1=zbg[:, dsl], op=ALU.mult),
                        tick=True)
        for of in range(ND):
            def mmo(h, of=of):
                for k in range(NTI1):
                    ins = h.matmul(out=pIP[of % 2][:, :],
                                   lhsT=w1out[k][:, of * 128:(of + 1) * 128],
                                   rhs=yg[:, k * TC:(k + 1) * TC],
                                   start=(k == 0), stop=(k == NTI1 - 1))
                return ins
            c.op("tensor", mmo, tick=True)
            st = s1t[of % 2]
            c.dwait("vector", "sync")
            c.op("vector", lambda h, of=of, st=st: h.tensor_scalar(
                out=st[:, :], in0=pIP[of % 2][:, :], scalar1=0.5,
                scalar2=None, op0=ALU.mult), tick=True)
            tgt, toff = (S1ab[0], t0) if t0 < TOWN else (S1ab[1], t0 - TOWN)
            c.dma("sync", lambda h, of=of, st=st, tgt=tgt, toff=toff:
                  h.dma_start(out=tgt[of, :, toff:toff + TC], in_=st[:, :]))
        c.barrier()


def emit_moe(nc, ctx, c, sp1, S1a_dram, Wd, Cst, dbg_g=None, dbg_g2=None, dbg_g3=None):
    NHOF = HID1 // 128
    gates = sbt(nc, ctx, "mgates", [8, TOWN], F32)
    gflat = sbt(nc, ctx, "mgflat", [1, TOWN], F32)
    scf = sbt(nc, ctx, "mscf", [8, TC], F32)
    scT = sbt(nc, ctx, "mscT", [128, 8], F32)
    m1 = sbt(nc, ctx, "mm1", [128, 1], F32)
    nm1 = sbt(nc, ctx, "mnm1", [128, 1], F32)
    m2 = sbt(nc, ctx, "mm2", [128, 1], F32)
    Et = sbt(nc, ctx, "mEt", [128, 8], F32)
    eqt = sbt(nc, ctx, "meqt", [128, 8], F32)
    mskd = sbt(nc, ctx, "mmskd", [128, 8], F32)
    ssum = sbt(nc, ctx, "mssum", [128, 1], F32)
    gtT = sbt(nc, ctx, "mgtT", [128, 8], F32)
    W1b = [sbt(nc, ctx, f"mW1{k}", [128, HID1], BF16) for k in range(ND)]
    W3b = [sbt(nc, ctx, f"mW3{k}", [128, HID1], BF16) for k in range(ND)]
    W2b = [sbt(nc, ctx, f"mW2{k}", [128, D], BF16) for k in range(NHOF)]
    A1b = [sbt(nc, ctx, f"mA1{k}", [128, RK], BF16) for k in range(ND)]
    A3b = [sbt(nc, ctx, f"mA3{k}", [128, RK], BF16) for k in range(ND)]
    A2b = [sbt(nc, ctx, f"mA2{k}", [128, RK], BF16) for k in range(NHOF)]
    B1b = sbt(nc, ctx, "mB1", [RK, HID1], BF16)
    B3b = sbt(nc, ctx, "mB3", [RK, HID1], BF16)
    B2b = sbt(nc, ctx, "mB2", [RK, D], BF16)
    b1c = sbt(nc, ctx, "mb1c", [128, NHOF], F32)
    b3c = sbt(nc, ctx, "mb3c", [128, NHOF], F32)
    b2c = sbt(nc, ctx, "mb2c", [128, ND], F32)
    sh1 = sbt(nc, ctx, "msh1", [128, NHOF * TC], BF16)
    gsb = sbt(nc, ctx, "mgsb", [128, NHOF * TC], BF16)
    l1b = sbt(nc, ctx, "ml1", [RK, TC], BF16)
    l3b = sbt(nc, ctx, "ml3", [RK, TC], BF16)
    l2b = sbt(nc, ctx, "ml2", [RK, TC], BF16)
    gb = sbt(nc, ctx, "mgb", [128, TC], F32)
    yw = [sbt(nc, ctx, f"myw{i}", [128, TC], F32) for i in range(2)]
    xnfull = sbt(nc, ctx, "mxnf", [128, ND * TOWN], BF16)
    pT8 = pst(nc, ctx, "mpT8", [128, 8])
    pT8b = pst(nc, ctx, "mpT8b", [8, 128])
    pL = pst(nc, ctx, "mpL", [RK, TC])
    pH = [pst(nc, ctx, f"mpH{i}", [128, TC]) for i in range(2)]
    pGB = pst(nc, ctx, "mpGB", [128, TC])
    ones_row, id128 = Cst["ones_row"], Cst["id128"]

    c.dma("sync", lambda h: h.dma_start(
        out=xnfull[:, :].rearrange("p (d t) -> p d t", d=ND),
        in_=sp1["xn"][:, :, 0:TOWN].transpose([1, 0, 2])))
    for e in ("scalar", "vector", "tensor"):
        c.dwait(e, "sync")
        c.dwait(e, "gpsimd")
    for tco in range(NCHO):
        t0 = tco * TC
        c.dma("sync", lambda h, t0=t0: h.dma_start(
            out=scf[:, :], in_=sp1["scores"][:, t0:t0 + TC]))
        for e_ in ("vector", "tensor"):
            c.dwait(e_, "sync")
        if dbg_g is not None:
            c.dma("sync", lambda h, tco=tco: h.dma_start(
                out=dbg_g[:, tco * TC:(tco + 1) * TC], in_=scf[:, :]))
            c.dwait("vector", "sync")
        for g in range(TC // 128):
            c.op("tensor", lambda h, g=g: h.transpose(
                out=pT8[:, :], in_=scf[:, g * 128:(g + 1) * 128],
                identity=id128[0:8, 0:8]), tick=True)

            def V(fn):
                c.op("vector", fn, tick=True)
                c.fence("vector")
            V(lambda h: h.reduce_max(out=m1[:, :], in_=pT8[:, :],
              axis=mybir.AxisListType.X))
            V(lambda h: h.tensor_scalar(out=mskd[:, :], in0=pT8[:, :],
              scalar1=m1[:, 0:1], scalar2=None, op0=ALU.subtract))
            c.fence("scalar")
            c.op("scalar", lambda h: h.activation(out=Et[:, :], in_=mskd[:, :],
                 func=AF.Exp), tick=True)
            c.fence("vector")
            V(lambda h: h.tensor_scalar(out=eqt[:, :], in0=pT8[:, :],
              scalar1=m1[:, 0:1], scalar2=None, op0=ALU.is_ge))
            V(lambda h: h.scalar_tensor_tensor(out=mskd[:, :],
              in0=eqt[:, :], scalar=-BIG, in1=pT8[:, :], op0=ALU.mult,
              op1=ALU.add))
            V(lambda h: h.reduce_max(out=m2[:, :], in_=mskd[:, :],
              axis=mybir.AxisListType.X))
            V(lambda h: h.tensor_scalar(out=eqt[:, :], in0=pT8[:, :],
              scalar1=m2[:, 0:1], scalar2=None, op0=ALU.is_ge))
            V(lambda h: h.tensor_tensor(out=Et[:, :], in0=Et[:, :],
              in1=eqt[:, :], op=ALU.mult))
            V(lambda h: h.reduce_sum(out=ssum[:, :], in_=Et[:, :],
              axis=mybir.AxisListType.X))
            V(lambda h: h.reciprocal(out=ssum[:, :], in_=ssum[:, :]))
            V(lambda h: h.tensor_scalar(out=gtT[:, :], in0=Et[:, :],
              scalar1=ssum[:, 0:1], scalar2=None, op0=ALU.mult))
            if dbg_g is not None and tco == 0 and g == 0:
                c.dma("sync", lambda h: h.dma_start(out=dbg_g3[0:128, 0:8], in_=mskd[:, :]))
                c.dma("sync", lambda h: h.dma_start(out=dbg_g3[0:128, 8:16], in_=gtT[:, :]))
                c.dma("sync", lambda h: h.dma_start(out=dbg_g3[0:128, 16:24], in_=Et[:, :]))
                c.dma("sync", lambda h: h.dma_start(out=dbg_g3[0:128, 24:25], in_=m1[:, :], allow_slow_non_contiguous=True))
                c.dma("sync", lambda h: h.dma_start(out=dbg_g3[0:128, 26:27], in_=m2[:, :], allow_slow_non_contiguous=True))
                c.dma("sync", lambda h: h.dma_start(out=dbg_g3[0:128, 28:30], in_=eqt[:, 0:2], allow_slow_non_contiguous=True))
                c.dwait("vector", "sync")
                c.dwait("tensor", "sync")
            c.op("tensor", lambda h: h.transpose(
                out=pT8b[:, :], in_=gtT[:, :], identity=id128[:, :]), tick=True)
            c.op("vector", lambda h, t0=t0, g=g: h.tensor_copy(
                out=gates[:, t0 + g * 128:t0 + (g + 1) * 128],
                in_=pT8b[:, :]), tick=True)
    if dbg_g is not None:
        c.dma("sync", lambda h: h.dma_start(out=dbg_g2[:, :], in_=gates[:, :]))
        c.dwait("vector", "sync")
    def batchA(e):
        for k in range(ND):
            c.dma("gpsimd", lambda h, e=e, k=k: h.dma_start(
                out=W1b[k][:], in_=Wd["W1T"][e, k * 128:(k + 1) * 128, :]))
            c.dma("gpsimd", lambda h, e=e, k=k: h.dma_start(
                out=W3b[k][:], in_=Wd["W3T"][e, k * 128:(k + 1) * 128, :]))
            c.dma("gpsimd", lambda h, e=e, k=k: h.dma_start(
                out=A1b[k][:], in_=Wd["A1T"][e, k * 128:(k + 1) * 128, :]))
            c.dma("gpsimd", lambda h, e=e, k=k: h.dma_start(
                out=A3b[k][:], in_=Wd["A3T"][e, k * 128:(k + 1) * 128, :]))
        c.dma("gpsimd", lambda h, e=e: h.dma_start(
            out=B1b[:], in_=Wd["B1T"][e, :, :]))
        c.dma("gpsimd", lambda h, e=e: h.dma_start(
            out=B3b[:], in_=Wd["B3T"][e, :, :]))
        c.dma("sync", lambda h, e=e: h.dma_start(
            out=b1c[:], in_=Wd["b1c"][e, :, :]))
        c.dma("sync", lambda h, e=e: h.dma_start(
            out=b3c[:], in_=Wd["b3c"][e, :, :]))
        c.dma("sync", lambda h, e=e: h.dma_start(
            out=gflat[:], in_=gates[e:e + 1, :]))

    def batchB(e):
        for k in range(NHOF):
            c.dma("gpsimd", lambda h, e=e, k=k: h.dma_start(
                out=W2b[k][:], in_=Wd["W2T"][e, k * 128:(k + 1) * 128, :]))
            c.dma("gpsimd", lambda h, e=e, k=k: h.dma_start(
                out=A2b[k][:], in_=Wd["A2T"][e, k * 128:(k + 1) * 128, :]))
        c.dma("gpsimd", lambda h, e=e: h.dma_start(
            out=B2b[:], in_=Wd["B2T"][e, :, :]))
        c.dma("sync", lambda h, e=e: h.dma_start(
            out=b2c[:], in_=Wd["b2c"][e, :, :]))
        return (c.mark("sync"), c.mark("gpsimd"))

    batchA(0)
    bmkB = None
    for e in range(NE):
        for eng in ("scalar", "vector", "tensor"):
            c.dwait(eng, "sync")
            c.dwait(eng, "gpsimd")
        for tco in range(NCHO):
            t0 = tco * TC
            xsl = [slice(k * TOWN + t0, k * TOWN + t0 + TC) for k in range(ND)]
            c.op("tensor", lambda h, t0=t0: h.matmul(
                out=pGB[:, :], lhsT=ones_row[0:1, :],
                rhs=gflat[0:1, t0:t0 + TC], start=True, stop=True), tick=True)
            c.op("scalar", lambda h: h.copy(out=gb[:, :], in_=pGB[:, :]),
                 tick=True)
            def mml1(h, xsl=tuple(xsl)):
                for k in range(ND):
                    ins = h.matmul(out=pL[:, :], lhsT=A1b[k][:, :],
                                   rhs=xnfull[:, xsl[k]],
                                   start=(k == 0), stop=(k == ND - 1))
                return ins
            c.op("tensor", mml1, tick=True)
            c.op("vector", lambda h: h.tensor_copy(out=l1b[:, :], in_=pL[:, :]),
                 tick=True)

            def mml3(h, xsl=tuple(xsl)):
                for k in range(ND):
                    ins = h.matmul(out=pL[:, :], lhsT=A3b[k][:, :],
                                   rhs=xnfull[:, xsl[k]],
                                   start=(k == 0), stop=(k == ND - 1))
                return ins
            c.op("tensor", mml3, tick=True)
            c.op("vector", lambda h: h.tensor_copy(out=l3b[:, :], in_=pL[:, :]),
                 tick=True)
            for of in range(NHOF):
                def mmh1(h, of=of, xsl=tuple(xsl)):
                    for k in range(ND):
                        h.matmul(out=pH[0][:, :],
                                 lhsT=W1b[k][:, of * 128:(of + 1) * 128],
                                 rhs=xnfull[:, xsl[k]],
                                 start=(k == 0), stop=False)
                    return h.matmul(out=pH[0][:, :],
                                    lhsT=B1b[:, of * 128:(of + 1) * 128],
                                    rhs=l1b[:, :], start=False, stop=True)
                c.op("tensor", mmh1, tick=True)
                if of == 0 and tco == 0:
                    bmkB = batchB(e)
                c.op("scalar", lambda h, of=of: h.activation(
                    out=sh1[:, of * TC:(of + 1) * TC], in_=pH[0][:, :],
                    func=AF.Silu, bias=b1c[:, of:of + 1]), tick=True)

                def mmh3(h, of=of, xsl=tuple(xsl)):
                    for k in range(ND):
                        h.matmul(out=pH[1][:, :],
                                 lhsT=W3b[k][:, of * 128:(of + 1) * 128],
                                 rhs=xnfull[:, xsl[k]],
                                 start=(k == 0), stop=False)
                    return h.matmul(out=pH[1][:, :],
                                    lhsT=B3b[:, of * 128:(of + 1) * 128],
                                    rhs=l3b[:, :], start=False, stop=True)
                c.op("tensor", mmh3, tick=True)
                c.op("vector", lambda h, of=of: h.scalar_tensor_tensor(
                    out=gsb[:, of * TC:(of + 1) * TC], in0=pH[1][:, :],
                    scalar=b3c[:, of:of + 1],
                    in1=sh1[:, of * TC:(of + 1) * TC],
                    op0=ALU.add, op1=ALU.mult), tick=True)
            def mml2(h):
                for k in range(NHOF):
                    ins = h.matmul(out=pL[:, :], lhsT=A2b[k][:, :],
                                   rhs=gsb[:, k * TC:(k + 1) * TC],
                                   start=(k == 0), stop=(k == NHOF - 1))
                return ins
            c.op("tensor", mml2, tick=True)
            c.op("vector", lambda h: h.tensor_copy(out=l2b[:, :], in_=pL[:, :]),
                 tick=True)
            c.dwait_cnt("tensor", "sync", bmkB[0])
            c.dwait_cnt("tensor", "gpsimd", bmkB[1])
            c.dwait_cnt("vector", "sync", bmkB[0])
            for of2 in range(ND):
                def mmy(h, of2=of2):
                    for k in range(NHOF):
                        h.matmul(out=pH[of2 % 2][:, :],
                                 lhsT=W2b[k][:, of2 * 128:(of2 + 1) * 128],
                                 rhs=gsb[:, k * TC:(k + 1) * TC],
                                 start=(k == 0), stop=False)
                    return h.matmul(out=pH[of2 % 2][:, :],
                                    lhsT=B2b[:, of2 * 128:(of2 + 1) * 128],
                                    rhs=l2b[:, :], start=False, stop=True)
                c.op("tensor", mmy, tick=True)
                if of2 == 0 and tco == NCHO - 1 and e + 1 < NE:
                    batchA(e + 1)
                st = yw[of2 % 2]
                c.dwait("vector", "gpsimd")
                c.op("vector", lambda h, of2=of2, st=st: h.scalar_tensor_tensor(
                    out=st[:, :], in0=pH[of2 % 2][:, :],
                    scalar=b2c[:, of2:of2 + 1], in1=gb[:, :],
                    op0=ALU.add, op1=ALU.mult), tick=True)
                c.dma("gpsimd", lambda h, of2=of2, st=st, t0=t0: h.dma_start(
                    out=S1a_dram[of2, :, t0:t0 + TC], in_=st[:, :],
                    accum_op=ALU.add))


def declare_l1_scratch(nc, dbg=False):
    kind = "ExternalOutput" if dbg else "Internal"
    return {
        "xn": nc.dram_tensor("sp1_xn", [ND, 128, T], BF16, kind=kind).ap(),
        "z": nc.dram_tensor("sp1_z", [16, 128, T], BF16, kind=kind).ap(),
        "xc": nc.dram_tensor("sp1_xc", [16, 128, T], F32, kind=kind).ap(),
        "delta": nc.dram_tensor("sp1_dl", [16, 128, T], F32, kind=kind).ap(),
        "scores": nc.dram_tensor("sp1_sc", [8, TOWN], F32, kind=kind).ap(),
    }


def emit_l1(nc, ctx, c, x1_dram, S1ab, Wd, Cst, parts="abm", dbg=False):
    sp1 = declare_l1_scratch(nc, dbg)
    with ExitStack() as mctx:
        shared1 = {
            "Bf": sbt(nc, mctx, "l1Bf", [NST, T], BF16),
            "Cf": sbt(nc, mctx, "l1Cf", [NST, T], BF16),
        }
        with ExitStack() as actx:
            emit_l1passA(nc, actx, c, x1_dram, sp1, Wd, Cst, shared1)
            c.barrier()
        with ExitStack() as a2ctx:
            emit_l1passA2(nc, a2ctx, c, sp1, Wd, Cst, shared1)
            c.barrier()
        if "b" in parts:
            with ExitStack() as bctx:
                emit_l1passB(nc, bctx, c, sp1, S1ab, Wd, Cst, shared1)
                c.barrier()
    if "m" in parts:
        dbg_g = (nc.dram_tensor("dbg_g", [8, TOWN], F32,
                                kind="ExternalOutput").ap() if dbg else None)
        dbg_g2 = (nc.dram_tensor("dbg_g2", [8, TOWN], F32,
                                 kind="ExternalOutput").ap() if dbg else None)
        dbg_g3 = (nc.dram_tensor("dbg_g3", [128, 32], F32,
                                 kind="ExternalOutput").ap() if dbg else None)
        with ExitStack() as mctx2:
            emit_moe(nc, mctx2, c, sp1, S1ab[0], Wd, Cst, dbg_g, dbg_g2,
                     dbg_g3)
            c.barrier()


def prep_l1_weights(inp, rev):
    d = "b" if rev else "f"
    W = {}
    W["in1T"] = np.ascontiguousarray(inp[f"L1_attn_{d}_in_w"].T)
    xp = inp[f"L1_attn_{d}_xproj_w"]
    W["xpdT"] = np.ascontiguousarray(xp[:DTR].T)
    W["xpbT"] = np.ascontiguousarray(xp[DTR:DTR + NST].T)
    W["xpcT"] = np.ascontiguousarray(xp[DTR + NST:].T)
    W["dtwT"] = np.ascontiguousarray(inp[f"L1_attn_{d}_dt_w"].T)
    cw = inp[f"L1_attn_{d}_conv_w"]
    W["conv1w"] = np.ascontiguousarray(
        cw.reshape(16, 128, 4).transpose(1, 0, 2).reshape(128, 64))
    W["conv1b"] = col_form(inp[f"L1_attn_{d}_conv_b"])
    W["dtb1"] = col_form(inp[f"L1_attn_{d}_dt_b"])
    W["lngr"] = np.asarray(inp["L1_attn_ln_g"], np.float32).reshape(1, D)
    W["lnbr"] = np.asarray(inp["L1_attn_ln_b"], np.float32).reshape(1, D)
    W["lngc"] = col_form(inp["L1_attn_ln_g"])
    W["out1T"] = np.ascontiguousarray(inp[f"L1_attn_{d}_out_w"].T)
    al = np.asarray(inp[f"L1_attn_{d}_A_log"], np.float32)
    W["A1col"] = np.ascontiguousarray(
        al.reshape(16, 128, NST).transpose(1, 0, 2).reshape(128, 16 * NST))
    W["D1col"] = col_form(inp[f"L1_attn_{d}_D_skip"])
    W["WgT"] = np.ascontiguousarray(
        inp["L1_ffn_Wg"].T.reshape(ND, 128, NE).reshape(128 * ND, NE)
    ).reshape(D, NE)
    W["WgT"] = np.ascontiguousarray(
        inp["L1_ffn_Wg"].T.reshape(ND, 128, NE).transpose(1, 0, 2).reshape(128, ND * NE))
    W["W1T"] = np.ascontiguousarray(np.transpose(inp["L1_ffn_W1"], (0, 2, 1)))
    W["W3T"] = np.ascontiguousarray(np.transpose(inp["L1_ffn_W3"], (0, 2, 1)))
    W["W2T"] = np.ascontiguousarray(np.transpose(inp["L1_ffn_W2"], (0, 2, 1)))
    W["A1T"] = np.ascontiguousarray(np.transpose(inp["L1_ffn_A1"], (0, 2, 1)))
    W["A3T"] = np.ascontiguousarray(np.transpose(inp["L1_ffn_A3"], (0, 2, 1)))
    W["A2T"] = np.ascontiguousarray(np.transpose(inp["L1_ffn_A2"], (0, 2, 1)))
    W["B1T"] = np.ascontiguousarray(np.transpose(inp["L1_ffn_B1"], (0, 2, 1)))
    W["B3T"] = np.ascontiguousarray(np.transpose(inp["L1_ffn_B3"], (0, 2, 1)))
    W["B2T"] = np.ascontiguousarray(np.transpose(inp["L1_ffn_B2"], (0, 2, 1)))
    W["b1c"] = np.ascontiguousarray(
        inp["L1_ffn_b1"].reshape(NE, 16, 128).transpose(0, 2, 1))
    W["b3c"] = np.ascontiguousarray(
        inp["L1_ffn_b3"].reshape(NE, 16, 128).transpose(0, 2, 1))
    W["b2c"] = np.ascontiguousarray(
        inp["L1_ffn_b2"].reshape(NE, ND, 128).transpose(0, 2, 1))
    return W


L1W_SHAPES = [
    ("in1T", [D, 2 * DIN1], F32), ("xpdT", [DIN1, DTR], F32),
    ("xpbT", [DIN1, NST], F32), ("xpcT", [DIN1, NST], F32),
    ("dtwT", [DTR, DIN1], F32), ("conv1w", [128, 64], F32),
    ("conv1b", [128, 16], F32), ("dtb1", [128, 16], F32),
    ("lngr", [1, D], F32), ("lnbr", [1, D], F32), ("lngc", [128, ND], F32),
    ("out1T", [DIN1, D], F32), ("A1col", [128, 16 * NST], F32),
    ("D1col", [128, 16], F32), ("WgT", [128, ND * NE], F32),
    ("W1T", [NE, D, HID1], F32), ("W3T", [NE, D, HID1], F32),
    ("W2T", [NE, HID1, D], F32), ("A1T", [NE, D, RK], F32),
    ("A3T", [NE, D, RK], F32), ("A2T", [NE, HID1, RK], F32),
    ("B1T", [NE, RK, HID1], F32), ("B3T", [NE, RK, HID1], F32),
    ("B2T", [NE, RK, D], F32), ("b1c", [NE, 128, 16], F32),
    ("b3c", [NE, 128, 16], F32), ("b2c", [NE, 128, ND], F32),
]


def emit_final(nc, ctx, c, x1_dram, Sown, Sprev, out_dram, masks=None, G=None):
    xt = sbt(nc, ctx, "fnx", [128, TC], F32)
    at = sbt(nc, ctx, "fna", [128, TC], F32)
    bt = sbt(nc, ctx, "fnb", [128, TC], F32)
    g0 = sbt(nc, ctx, "fng0", [128, TC], F32)
    g1t = sbt(nc, ctx, "fng1", [128, TC], F32)
    ot = [sbt(nc, ctx, f"fno{i}", [128, TC], F32) for i in range(2)]
    for tco in range(NCHO):
        t0 = tco * TC
        tm = (NCH - 1 - tco) * TC - TOWN
        for d in range(ND):
            c.dwait("vector", "sync")
            c.dma("sync", lambda h, d=d, t0=t0: h.dma_start(
                out=xt[:, :], in_=x1_dram[d, :, t0:t0 + TC]))
            c.dma("sync", lambda h, d=d, t0=t0: h.dma_start(
                out=at[:, :], in_=Sown[d, :, t0:t0 + TC]))
            if masks is None:
                c.dma("sync", lambda h, d=d, t0=t0: h.dma_start(
                    out=bt[:, :], in_=Sprev[d, :, t0:t0 + TC]))
            else:
                c.dma("sync", lambda h, d=d, tm=tm: h.dma_start(
                    out=g0[:, :], in_=G[0][d, :, tm:tm + TC]))
                c.dma("sync", lambda h, d=d, tm=tm: h.dma_start(
                    out=g1t[:, :], in_=G[1][d, :, tm:tm + TC]))
            c.dwait("vector", "sync")
            o = ot[d % 2]
            c.op("vector", lambda h, o=o: h.tensor_tensor(
                out=o[:, :], in0=xt[:, :], in1=at[:, :], op=ALU.add))
            if masks is None:
                c.op("vector", lambda h, o=o: h.tensor_tensor(
                    out=o[:, :], in0=o[:, :], in1=bt[:, :], op=ALU.add),
                    tick=True)
            else:
                c.op("vector", lambda h, o=o: h.scalar_tensor_tensor(
                    out=o[:, :], in0=g0[:, ::-1], scalar=masks[:, 0:1],
                    in1=o[:, :], op0=ALU.mult, op1=ALU.add))
                c.op("vector", lambda h, o=o: h.scalar_tensor_tensor(
                    out=o[:, :], in0=g1t[:, ::-1], scalar=masks[:, 1:2],
                    in1=o[:, :], op0=ALU.mult, op1=ALU.add), tick=True)
            c.dma("sync", lambda h, d=d, t0=t0, o=o: h.dma_start(
                out=out_dram[d, :, t0:t0 + TC], in_=o[:, :]))
    c.barrier()


def build_l1(parts="abm", dbg=False):
    nc = bass.Bass("TRN2", target_bir_lowering=False)
    x_dram = nc.dram_tensor("x", [ND, 128, T], F32, kind="ExternalInput").ap()
    S0o = nc.dram_tensor("S0own", [ND, 128, T], F32, kind="ExternalInput").ap()
    S0p = nc.dram_tensor("S0prev", [ND, 128, T], F32, kind="ExternalInput").ap()
    x1_dram = nc.dram_tensor("x1", [ND, 128, T], F32, kind="ExternalOutput").ap()
    S1a_d = nc.dram_tensor("S1a", [ND, 128, TOWN], F32,
                           kind="ExternalOutput").ap()
    S1b_d = nc.dram_tensor("S1b", [ND, 128, TOWN], F32,
                           kind="ExternalOutput").ap()
    Wd = {}
    for nm, shape, dt in L1W_SHAPES:
        Wd[nm] = nc.dram_tensor("w_" + nm, shape, dt, kind="ExternalInput").ap()
    with ExitStack() as ctx:
        c = Ctl(nc, ctx)
        Cst = load_consts(nc, ctx, c)
        with ExitStack() as cctx:
            emit_combine(nc, cctx, c, x_dram, S0o, S0p, x1_dram)
        emit_l1(nc, ctx, c, x1_dram, (S1a_d, S1b_d), Wd, Cst, parts, dbg)
        with nc.Block() as block:
            c.emit(block)
    return nc


def build_fin():
    nc = bass.Bass("TRN2", target_bir_lowering=False)
    x1_dram = nc.dram_tensor("x1", [ND, 128, T], F32, kind="ExternalInput").ap()
    S1o = nc.dram_tensor("S1own", [ND, 128, T], F32, kind="ExternalInput").ap()
    S1p = nc.dram_tensor("S1prev", [ND, 128, T], F32, kind="ExternalInput").ap()
    out_dram = nc.dram_tensor("out", [ND, 128, TOWN], F32,
                              kind="ExternalOutput").ap()
    with ExitStack() as ctx:
        c = Ctl(nc, ctx)
        with ExitStack() as fctx:
            emit_final(nc, fctx, c, x1_dram, S1o, S1p, out_dram)
        with nc.Block() as block:
            c.emit(block)
    return nc


def _host_inputs_common(inputs, c):
    rev = c % 2
    m = {}
    m.update(host_consts())
    W0 = prep_l0_weights(inputs, rev)
    for nm, shape, dt in L0W_SHAPES:
        m["w_" + nm] = W0[nm]
    return m


def kernel_unfused(inputs):
    x = np.asarray(inputs["L0_attn_x"])
    B = x.shape[0]
    ncores = 8
    xs = []
    for c in range(ncores):
        b, rev = c // 2, c % 2
        xb = np.ascontiguousarray(x[b].T) if not rev else \
            np.ascontiguousarray(x[b].T[:, ::-1])
        xs.append(xb.reshape(ND, 128, T))
    cst = host_consts()
    nc0 = build_l0()
    maps0 = []
    for c in range(ncores):
        W0 = prep_l0_weights(inputs, c % 2)
        m = {"x": xs[c]}
        for nm, shape, dt in L0W_SHAPES:
            m["w_" + nm] = W0[nm]
        m.update(cst)
        maps0.append(m)
    res0 = run_bass_kernel_spmd(nc0, maps0, core_ids=list(range(ncores)))
    S0 = [np.asarray(res0.results[c]["S0"]) for c in range(ncores)]
    nc1 = build_l1()
    maps1 = []
    for c in range(ncores):
        part = c ^ 1
        sp = np.ascontiguousarray(
            S0[part].reshape(D, T)[:, ::-1]).reshape(ND, 128, T)
        W1 = prep_l1_weights(inputs, c % 2)
        m = {"x": xs[c], "S0own": S0[c], "S0prev": sp}
        for nm, shape, dt in L1W_SHAPES:
            m["w_" + nm] = W1[nm]
        m.update(cst)
        maps1.append(m)
    res1 = run_bass_kernel_spmd(nc1, maps1, core_ids=list(range(ncores)))
    x1 = [np.asarray(res1.results[c]["x1"]) for c in range(ncores)]
    S1 = [np.concatenate([np.asarray(res1.results[c]["S1a"]),
                          np.asarray(res1.results[c]["S1b"])], axis=2)
          for c in range(ncores)]
    nc2 = build_fin()
    maps2 = []
    for c in range(ncores):
        part = c ^ 1
        sp = np.ascontiguousarray(
            S1[part].reshape(D, T)[:, ::-1]).reshape(ND, 128, T)
        maps2.append({"x1": x1[c], "S1own": S1[c], "S1prev": sp})
    res2 = run_bass_kernel_spmd(nc2, maps2, core_ids=list(range(ncores)))
    outs = [np.asarray(res2.results[c]["out"]).reshape(D, TOWN)
            for c in range(ncores)]
    full = np.empty((B, T, D), np.float32)
    for b in range(B):
        full[b, :TOWN] = outs[2 * b].T
        full[b, TOWN:] = outs[2 * b + 1].T[::-1]
    return full


def build_full():
    nc = bass.Bass("TRN2", target_bir_lowering=False)
    x_dram = nc.dram_tensor("x", [ND, 128, T], F32, kind="ExternalInput").ap()
    pmask = nc.dram_tensor("pmask", [128, 2], F32, kind="ExternalInput").ap()
    out_dram = nc.dram_tensor("out", [ND, 128, TOWN], F32,
                              kind="ExternalOutput").ap()
    S0 = nc.dram_tensor("S0i", [ND, 128, T], F32, kind="Internal").ap()
    S1a = nc.dram_tensor("S1ai", [ND, 128, TOWN], F32, kind="Internal").ap()
    S1b = nc.dram_tensor("S1bi", [ND, 128, TOWN], F32, kind="Internal").ap()
    x1 = nc.dram_tensor("x1i", [ND, 128, T], F32, kind="Internal").ap()
    G0 = nc.dram_tensor("G0i", [2 * ND, 128, T], F32, kind="Internal").ap()
    G1 = nc.dram_tensor("G1i", [2 * ND, 128, TOWN], F32,
                        kind="Internal").ap()
    Wd = {}
    for nm, shape, dt in L0W_SHAPES + L1W_SHAPES:
        Wd[nm] = nc.dram_tensor("w_" + nm, shape, dt, kind="ExternalInput").ap()
    groups = [[0, 1], [2, 3], [4, 5], [6, 7]]
    with ExitStack() as ctx:
        c = Ctl(nc, ctx)
        Cst = load_consts(nc, ctx, c)
        pm = sbt(nc, ctx, "pmt", [128, 2], F32)
        c.dma("sync", lambda h: h.dma_start(out=pm[:], in_=pmask[:, :]))
        emit_l0(nc, ctx, c, x_dram, S0, Wd, Cst)
        c.barrier()
        c.op("gpsimd", lambda h: h.collective_compute(
            "AllGather", ALU.bypass, replica_groups=groups,
            ins=[S0.opt()], outs=[G0.opt()]), tick=True)
        c.barrier()
        with ExitStack() as cctx:
            emit_combine(nc, cctx, c, x_dram, S0, None, x1, masks=pm,
                         G=(G0[0:ND], G0[ND:2 * ND]))
        emit_l1(nc, ctx, c, x1, (S1a, S1b), Wd, Cst)
        c.barrier()
        c.op("gpsimd", lambda h: h.collective_compute(
            "AllGather", ALU.bypass, replica_groups=groups,
            ins=[S1b.opt()], outs=[G1.opt()]), tick=True)
        c.barrier()
        with ExitStack() as fctx:
            emit_final(nc, fctx, c, x1, S1a, None, out_dram, masks=pm,
                       G=(G1[0:ND], G1[ND:2 * ND]))
        c.barrier()
        with nc.Block() as block:
            c.emit(block)
    return nc


def kernel_fused(inputs):
    x = np.asarray(inputs["L0_attn_x"])
    B = x.shape[0]
    ncores = 8
    nc = build_full()
    cst = host_consts()
    maps = []
    for c in range(ncores):
        b, rev = c // 2, c % 2
        xb = np.ascontiguousarray(x[b].T) if not rev else \
            np.ascontiguousarray(x[b].T[:, ::-1])
        m = {"x": xb.reshape(ND, 128, T)}
        m["pmask"] = np.ascontiguousarray(np.broadcast_to(
            np.array([[1.0 if rev == 1 else 0.0,
                       1.0 if rev == 0 else 0.0]], np.float32), (128, 2)))
        W0 = prep_l0_weights(inputs, rev)
        for nm, shape, dt in L0W_SHAPES:
            m["w_" + nm] = W0[nm]
        W1 = prep_l1_weights(inputs, rev)
        for nm, shape, dt in L1W_SHAPES:
            m["w_" + nm] = W1[nm]
        m.update(cst)
        maps.append(m)
    res = run_bass_kernel_spmd(nc, maps, core_ids=list(range(ncores)))
    outs = [np.asarray(res.results[c]["out"]).reshape(D, TOWN)
            for c in range(ncores)]
    full = np.empty((B, T, D), np.float32)
    for b in range(B):
        full[b, :TOWN] = outs[2 * b].T
        full[b, TOWN:] = outs[2 * b + 1].T[::-1]
    return full


def kernel(**inputs):
    return kernel_fused(inputs)


# version 23
# speedup vs baseline: 1.0120x; 1.0120x over previous
import numpy as np
import ml_dtypes
from contextlib import ExitStack
import concourse.bass as bass
import concourse.mybir as mybir
from concourse.bass_utils import run_bass_kernel_spmd

F32 = mybir.dt.float32
BF16 = mybir.dt.bfloat16
I32 = mybir.dt.int32
AF = mybir.ActivationFunctionType
ALU = mybir.AluOpType

T = 4096
TC = 512
NCH = T // TC
LC = 128
NLC = TC // LC
D = 1024
ND = D // 128
BIG = 1.0e9
DIN0 = 2048
NH = 32
HD = 64
DST = 128
CONVD = 2304
NTI0 = CONVD // 128
HID0 = 3072
QS = 1.0 / 127.0
DIN1 = 2048
NST = 16
DTR = 64
NTI1 = DIN1 // 128
NGRP = 8
NE = 8
HID1 = 2048
RK = 16
TOWN = 2048
NCHO = TOWN // TC

ENGS = ("sync", "vector", "scalar", "tensor", "gpsimd")


class Ctl:

    def __init__(self, nc, ctx):
        self.nc = nc
        self.ops = {e: [] for e in ENGS}
        self.s_tick = ctx.enter_context(nc.semaphore("tick"))
        self.ntick = 0
        self.last_wait = {e: 0 for e in ENGS}
        self.dsem = {}
        self.dcnt = {}
        for q in ("sync", "gpsimd", "scalar"):
            self.dsem[q] = ctx.enter_context(nc.semaphore(f"dma_{q}"))
            self.dcnt[q] = 0
        self.last_dwait = {e: {q: 0 for q in self.dsem} for e in ENGS}

    def op(self, eng, fn, tick=False):
        wv = self.ntick if self.ntick > self.last_wait[eng] else 0
        if wv:
            self.last_wait[eng] = wv
        ent = {"kind": "op", "fn": fn, "wv": wv, "tickid": None, "dwv": None}
        if tick:
            self.ntick += 1
            self.last_wait[eng] = self.ntick
            ent["tickid"] = self.ntick
        self.ops[eng].append(ent)

    def tick_last(self, eng):
        lst = self.ops[eng]
        assert lst and lst[-1]["kind"] == "op" and lst[-1]["tickid"] is None
        self.ntick += 1
        self.last_wait[eng] = self.ntick
        lst[-1]["tickid"] = self.ntick

    def fence(self, eng):
        self.ops[eng].append({"kind": "wait", "wv": self.ntick})
        self.last_wait[eng] = self.ntick

    def dma(self, q, fn):
        self.dcnt[q] += 16
        wv = self.ntick if self.ntick > self.last_wait[q] else 0
        if wv:
            self.last_wait[q] = wv
        self.ops[q].append({"kind": "dma", "fn": fn, "wv": wv,
                            "tickid": None, "dwv": None, "q": q})

    def dwait(self, eng, q):
        cnt = self.dcnt[q]
        if cnt <= self.last_dwait[eng][q]:
            return
        self.last_dwait[eng][q] = cnt
        self.ops[eng].append({"kind": "dwait", "q": q, "cnt": cnt})

    def mark(self, q):
        return self.dcnt[q]

    def dwait_cnt(self, eng, q, cnt):
        if cnt <= self.last_dwait[eng][q]:
            return
        self.last_dwait[eng][q] = cnt
        self.ops[eng].append({"kind": "dwait", "q": q, "cnt": cnt})

    def barrier(self, dmas=True):
        if dmas:
            for e in ENGS:
                for q in self.dsem:
                    self.dwait(e, q)
        for e in ENGS:
            lst = self.ops[e]
            j = len(lst) - 1
            while j >= 0 and lst[j]["kind"] == "dwait":
                j -= 1
            if j >= 0 and lst[j]["kind"] == "op" and lst[j]["tickid"] is None:
                self.ntick += 1
                lst[j]["tickid"] = self.ntick
                self.last_wait[e] = self.ntick
            else:
                self.op(e, lambda h: h.nop(), tick=True)
        wv = self.ntick
        for e in ENGS:
            self.last_wait[e] = wv
            self.ops[e].append({"kind": "wait", "wv": wv})

    def emit(self, block):
        tsem = self.s_tick
        dsem = self.dsem
        for name in ENGS:
            oplist = self.ops[name]

            def body(h, oplist=oplist):
                for ent in oplist:
                    k = ent["kind"]
                    if k == "dwait":
                        h.wait_ge(dsem[ent["q"]], ent["cnt"])
                    elif k == "wait":
                        h.wait_ge(tsem, ent["wv"])
                    elif k == "dma":
                        if ent["wv"]:
                            h.wait_ge(tsem, ent["wv"])
                        ent["fn"](h).then_inc(dsem[ent["q"]], 16)
                    else:
                        if ent["wv"]:
                            h.wait_ge(tsem, ent["wv"])
                        ins = ent["fn"](h)
                        if ent["tickid"] is not None:
                            ins.then_inc(tsem, 1)
            getattr(block, name)(body)
        assert self.ntick < 60000, f"tick overflow {self.ntick}"
        for q, cval in self.dcnt.items():
            assert cval < 60000, f"dma sem overflow {q}={cval}"


def sbt(nc, ctx, name, shape, dtype):
    return ctx.enter_context(nc.sbuf_tensor(name, shape, dtype))


def pst(nc, ctx, name, shape, dtype=F32):
    return ctx.enter_context(nc.psum_tensor(name, shape, dtype))


def emit_ln_stats(nc, c, xt, sqt, ones_col, p_s, p_q, r, eps2=None, W=TC):
    c.op("scalar", lambda h: h.activation(out=sqt[:, 0:ND * W], in_=xt[:, 0:ND * W],
                                          func=AF.Square), tick=True)

    def mm_s(h):
        for d in range(ND):
            ins = h.matmul(out=p_s[:, 0:W], lhsT=ones_col[:, 0:1],
                           rhs=xt[:, d * W:(d + 1) * W],
                           start=(d == 0), stop=(d == ND - 1))
        return ins
    c.op("tensor", mm_s, tick=True)

    def mm_q(h):
        for d in range(ND):
            ins = h.matmul(out=p_q[:, 0:W], lhsT=ones_col[:, 0:1],
                           rhs=sqt[:, d * W:(d + 1) * W],
                           start=(d == 0), stop=(d == ND - 1))
        return ins
    c.op("tensor", mm_q, tick=True)
    c.op("vector", lambda h: h.tensor_scalar(out=r["m"][:, 0:W], in0=p_s[0:1, :],
         scalar1=1.0 / D, scalar2=None, op0=ALU.mult))
    c.op("vector", lambda h: h.tensor_scalar(out=r["e2"][:, 0:W], in0=p_q[0:1, :],
         scalar1=1.0 / D, scalar2=None, op0=ALU.mult))
    c.op("vector", lambda h: h.tensor_tensor(out=r["m2"][:, 0:W], in0=r["m"][:, 0:W],
         in1=r["m"][:, 0:W], op=ALU.mult))
    c.op("vector", lambda h: h.tensor_tensor(out=r["var"][:, 0:W], in0=r["e2"][:, 0:W],
         in1=r["m2"][:, 0:W], op=ALU.subtract))
    c.op("vector", lambda h: h.tensor_scalar(out=r["vp"][:, 0:W], in0=r["var"][:, 0:W],
         scalar1=1e-5, scalar2=None, op0=ALU.add))
    c.op("vector", lambda h: h.reciprocal(out=r["rv"][:, 0:W], in_=r["vp"][:, 0:W]),
         tick=(eps2 is None))
    if eps2 is not None:
        c.op("vector", lambda h: h.tensor_tensor(out=r["q2"][:, 0:W], in0=r["var"][:, 0:W],
             in1=r["rv"][:, 0:W], op=ALU.mult))
        c.op("vector", lambda h, e=eps2: h.tensor_scalar(out=r["qe"][:, 0:W],
             in0=r["q2"][:, 0:W], scalar1=e, scalar2=None, op0=ALU.add))
        c.op("vector", lambda h: h.reciprocal(out=r["rq"][:, 0:W], in_=r["qe"][:, 0:W]),
             tick=True)
        c.op("scalar", lambda h: h.sqrt(out=r["g2"][:, 0:W], in_=r["rq"][:, 0:W]))
    c.op("scalar", lambda h: h.sqrt(out=r["g1"][:, 0:W], in_=r["rv"][:, 0:W]),
         tick=True)
    c.op("vector", lambda h: h.tensor_tensor(out=r["mg1"][:, 0:W], in0=r["m"][:, 0:W],
         in1=r["g1"][:, 0:W], op=ALU.mult), tick=(eps2 is None))
    if eps2 is not None:
        c.op("vector", lambda h: h.tensor_tensor(out=r["g12"][:, 0:W], in0=r["g1"][:, 0:W],
             in1=r["g2"][:, 0:W], op=ALU.mult))
        c.op("vector", lambda h: h.tensor_tensor(out=r["mg12"][:, 0:W], in0=r["m"][:, 0:W],
             in1=r["g12"][:, 0:W], op=ALU.mult), tick=True)


def make_rows(nc, ctx, prefix):
    rt = [sbt(nc, ctx, f"{prefix}_{i}", [1, TC], F32) for i in range(6)]
    alias = {"m": 0, "e2": 1, "m2": 2, "var": 3, "vp": 1, "rv": 2, "q2": 4,
             "qe": 3, "rq": 4, "g1": 5, "g2": 3, "mg1": 2, "g12": 4,
             "mg12": 1}
    return {n: rt[i] for n, i in alias.items()}


def softplus_chain(nc, c, out_sb, psum_in, bias_col, w_sb, aw_sb, e_sb, l_sb, P):
    c.op("vector", lambda h: h.tensor_scalar(out=w_sb[:P, :], in0=psum_in,
         scalar1=bias_col, scalar2=None, op0=ALU.add), tick=True)
    c.op("scalar", lambda h: h.activation(out=aw_sb[:P, :], in_=w_sb[:P, :],
         func=AF.Abs))
    c.op("scalar", lambda h: h.activation(out=e_sb[:P, :], in_=aw_sb[:P, :],
         func=AF.Exp, scale=-1.0))
    c.op("scalar", lambda h: h.activation(out=l_sb[:P, :], in_=e_sb[:P, :],
         func=AF.Ln, bias=1.0), tick=True)
    c.op("vector", lambda h: h.tensor_scalar(out=w_sb[:P, :], in0=w_sb[:P, :],
         scalar1=0.0, scalar2=None, op0=ALU.max))
    c.op("vector", lambda h: h.tensor_tensor(out=out_sb, in0=w_sb[:P, :],
         in1=l_sb[:P, :], op=ALU.add), tick=True)


def emit_l0passA(nc, ctx, c, x_dram, S0_dram, sp, Wd, Cst, shared, dbg_h=None):
    w0in = [sbt(nc, ctx, f"w0in{k}", [128, 4384], BF16) for k in range(ND)]
    convw = sbt(nc, ctx, "cvw0", [128, 72], F32)
    convb = sbt(nc, ctx, "cvb0", [128, 18], F32)
    dtb = sbt(nc, ctx, "dtb0", [32, 1], F32)
    projbh = sbt(nc, ctx, "pjbh0", [128, ND], F32)
    nwc = sbt(nc, ctx, "nwc0", [128, ND], F32)
    xt = sbt(nc, ctx, "a0xt", [128, ND * TC], F32)
    sqt = sbt(nc, ctx, "a0sq", [128, ND * TC], F32)
    h_sb = sbt(nc, ctx, "a0h", [128, ND * TC], BF16)
    xbc = [sbt(nc, ctx, f"a0xbc{i}", [128, 3 + TC], F32) for i in range(NTI0)]
    halo = sbt(nc, ctx, "a0halo", [128, 54], F32)
    ct = sbt(nc, ctx, "a0ct", [128, TC], F32)
    stg = [sbt(nc, ctx, f"a0stg{i}", [128, TC], BF16) for i in range(2)]
    s0p = [sbt(nc, ctx, f"a0s0p{i}", [128, TC], F32) for i in range(2)]
    dts = sbt(nc, ctx, "a0dts", [32, TC], F32)
    sp_w = sbt(nc, ctx, "a0spw", [32, TC], F32)
    sp_a = sbt(nc, ctx, "a0spa", [32, TC], F32)
    sp_e = sbt(nc, ctx, "a0spe", [32, TC], F32)
    hrow = sbt(nc, ctx, "a0hrow", [1, TC], F32)
    hrow2 = sbt(nc, ctx, "a0hrow2", [1, TC], F32)
    rows = make_rows(nc, ctx, "a0r")
    pS = pst(nc, ctx, "a0pS", [1, TC])
    pQ = pst(nc, ctx, "a0pQ", [1, TC])
    pB1 = pst(nc, ctx, "a0pB1", [128, TC])
    pB2 = pst(nc, ctx, "a0pB2", [128, TC])
    pIP = [pst(nc, ctx, f"a0pIP{i}", [128, TC]) for i in range(2)]
    pDT = pB2[0:32, :]
    Bb, Cb = shared["Bb"], shared["Cb"]
    ones_col, ones_row = Cst["ones_col"], Cst["ones_row"]

    for k in range(ND):
        c.dma("gpsimd", lambda h, k=k: h.dma_start(
            out=w0in[k][:], in_=Wd["inWT"][k * 128:(k + 1) * 128, :]))
    for nm, dst in [("convw", convw), ("convb", convb), ("dtb", dtb),
                    ("nwc", nwc)]:
        c.dma("sync", lambda h, nm=nm, dst=dst: h.dma_start(
            out=dst[:], in_=Wd[nm][:, :]))
    c.dma("sync", lambda h: h.dma_start(out=projbh[:], in_=Wd["projb"][:, :]))
    for e in ("scalar", "vector", "tensor"):
        c.dwait(e, "sync")
        c.dwait(e, "gpsimd")
    c.op("scalar", lambda h: h.mul(out=projbh[:, :], in_=projbh[:, :], mul=0.5),
         tick=True)
    c.op("vector", lambda h: h.memset(halo[:, :], 0.0), tick=True)

    for tc in range(NCH):
        t0 = tc * TC
        c.dma("sync", lambda h, t0=t0: h.dma_start(
            out=xt[:, :].rearrange("p (d t) -> p d t", d=ND),
            in_=x_dram[:, :, t0:t0 + TC].transpose([1, 0, 2])))
        for e in ("scalar", "vector", "tensor"):
            c.dwait(e, "sync")
        emit_ln_stats(nc, c, xt, sqt, ones_col, pS, pQ, rows, eps2=1e-4)
        c.op("tensor", lambda h: h.matmul(out=pB1[:, :], lhsT=ones_row[0:1, :],
             rhs=rows["g12"][:, :], start=True, stop=True), tick=True)
        c.op("tensor", lambda h: h.matmul(out=pB2[:, :], lhsT=ones_row[0:1, :],
             rhs=rows["mg12"][:, :], start=True, stop=True), tick=True)
        for d in range(ND):
            sl = slice(d * TC, (d + 1) * TC)
            c.op("vector", lambda h, sl=sl: h.tensor_tensor(
                out=sqt[:, sl], in0=xt[:, sl], in1=pB1[:, :], op=ALU.mult))
            c.op("vector", lambda h, sl=sl: h.tensor_tensor(
                out=sqt[:, sl], in0=sqt[:, sl], in1=pB2[:, :], op=ALU.subtract))
            c.op("vector", lambda h, d=d, sl=sl: h.tensor_scalar(
                out=h_sb[:, sl], in0=sqt[:, sl], scalar1=nwc[:, d:d + 1],
                scalar2=None, op0=ALU.mult))
        c.tick_last("vector")
        if dbg_h is not None:
            c.dma("sync", lambda h, t0=t0: h.dma_start(
                out=dbg_h[:, :, t0:t0 + TC].transpose([1, 0, 2]),
                in_=h_sb[:, :].rearrange("p (d t) -> p d t", d=ND)))
        c.op("vector", lambda h: h.tensor_scalar(out=hrow[:, :],
             in0=rows["g1"][:, :], scalar1=0.5, scalar2=None, op0=ALU.mult))
        c.op("vector", lambda h: h.tensor_scalar(out=hrow2[:, :],
             in0=rows["mg1"][:, :], scalar1=0.5, scalar2=None, op0=ALU.mult),
             tick=True)
        c.op("tensor", lambda h: h.matmul(out=pB1[:, :], lhsT=ones_row[0:1, :],
             rhs=hrow[:, :], start=True, stop=True), tick=True)
        c.op("tensor", lambda h: h.matmul(out=pB2[:, :], lhsT=ones_row[0:1, :],
             rhs=hrow2[:, :], start=True, stop=True), tick=True)
        for d in range(ND):
            sl = slice(d * TC, (d + 1) * TC)
            st = s0p[d % 2]
            if d >= 2:
                c.dwait("vector", "sync")
            c.op("vector", lambda h, sl=sl, st=st: h.tensor_tensor(
                out=st[:, :], in0=xt[:, sl], in1=pB1[:, :], op=ALU.mult))
            c.op("vector", lambda h, st=st: h.tensor_tensor(
                out=st[:, :], in0=st[:, :], in1=pB2[:, :], op=ALU.subtract))
            c.op("vector", lambda h, d=d, st=st: h.tensor_scalar(
                out=st[:, :], in0=st[:, :], scalar1=projbh[:, d:d + 1],
                scalar2=None, op0=ALU.add), tick=True)
            c.dma("sync", lambda h, d=d, st=st, t0=t0: h.dma_start(
                out=S0_dram[d, :, t0:t0 + TC], in_=st[:, :]))
        def cons0(of, t0):
            pp = pIP[(of + 1) % 2]
            if of < 16:
                stz = stg[of % 2]
                if of >= 2:
                    c.dwait("scalar", "sync")
                c.op("scalar", lambda h, pp=pp, stz=stz: h.activation(
                    out=stz[:, :], in_=pp[:, :], func=AF.Silu), tick=True)
                c.dma("sync", lambda h, of=of, stz=stz, t0=t0: h.dma_start(
                    out=sp["z"][of, :, t0:t0 + TC], in_=stz[:, :]))
            else:
                ti = of - 16
                c.op("scalar", lambda h, ti=ti, pp=pp: h.copy(
                    out=xbc[ti][:, 3:3 + TC], in_=pp[:, :]), tick=True)

        for of in range(35):
            M = 32 if of == 34 else 128
            pp = pDT if of == 34 else pIP[(of + 1) % 2]

            def mms(h, of=of, M=M, pp=pp):
                for k in range(ND):
                    ins = h.matmul(out=pp[0:M, :] if M != 128 else pp[:, :],
                                   lhsT=w0in[k][:, of * 128:of * 128 + M],
                                   rhs=h_sb[:, k * TC:(k + 1) * TC],
                                   start=(k == 0), stop=(k == ND - 1))
                return ins
            c.op("tensor", mms, tick=True)
            if of >= 1 and of < 34:
                cons0(of - 1, t0)
            if of == 34:
                cons0(33, t0)
        for ti in range(NTI0):
            c.op("vector", lambda h, ti=ti: h.tensor_copy(
                out=xbc[ti][:, 0:3], in_=halo[:, ti * 3:ti * 3 + 3]))
        c.tick_last("vector")
        softplus_chain(nc, c, dts[:, :], pDT, dtb[:, 0:1], sp_w, sp_a, sp_e,
                       sp_a, 32)
        c.dma("sync", lambda h, t0=t0: h.dma_start(
            out=sp["dts"][:, t0:t0 + TC], in_=dts[:, :]))
        for ti in range(NTI0):
            c.op("vector", lambda h, ti=ti: h.tensor_scalar(
                out=ct[:, :], in0=xbc[ti][:, 0:TC],
                scalar1=convw[:, ti * 4:ti * 4 + 1],
                scalar2=convb[:, ti:ti + 1], op0=ALU.mult, op1=ALU.add))
            for k in range(1, 4):
                c.op("vector", lambda h, ti=ti, k=k: h.scalar_tensor_tensor(
                    out=ct[:, :], in0=xbc[ti][:, k:k + TC],
                    scalar=convw[:, ti * 4 + k:ti * 4 + k + 1], in1=ct[:, :],
                    op0=ALU.mult, op1=ALU.add))
            c.op("vector", lambda h, ti=ti: h.tensor_copy(
                out=halo[:, ti * 3:ti * 3 + 3], in_=xbc[ti][:, TC:TC + 3]),
                tick=True)
            if ti < 16:
                stz = stg[ti % 2]
                c.dwait("scalar", "sync")
                c.op("scalar", lambda h, stz=stz: h.activation(
                    out=stz[:, :], in_=ct[:, :], func=AF.Silu), tick=True)
                c.dma("sync", lambda h, ti=ti, stz=stz, t0=t0: h.dma_start(
                    out=sp["xs"][ti, :, t0:t0 + TC], in_=stz[:, :]))
            elif ti == 16:
                c.op("scalar", lambda h, t0=t0: h.activation(
                    out=Bb[:, t0:t0 + TC], in_=ct[:, :], func=AF.Silu),
                    tick=True)
            else:
                c.op("scalar", lambda h, t0=t0: h.activation(
                    out=Cb[:, t0:t0 + TC], in_=ct[:, :], func=AF.Silu),
                    tick=True)
        c.barrier()


def emit_l0passB(nc, ctx, c, S0_dram, sp, Wd, Cst, shared, dbg_y=None, dbg_W=None, dbg_ssd=None, dbg2=None):
    NPO = DIN0 // 128
    POT = [sbt(nc, ctx, f"pot{k}", [128, D], BF16) for k in range(NPO)]
    Bb, Cb = shared["Bb"], shared["Cb"]
    ones_col, ones_row = Cst["ones_col"], Cst["ones_row"]
    id128, idbf, tri, ones32 = Cst["id128"], Cst["idbf"], Cst["tri"], Cst["ones32"]
    pB1 = pst(nc, ctx, "b0pB1", [128, TC])
    pIP = [pst(nc, ctx, f"b0pIP{i}", [128, TC]) for i in range(2)]
    pS = pst(nc, ctx, "b0pS", [1, TC])
    bankA = pst(nc, ctx, "b0bankA", [128, TC])
    bankB = pst(nc, ctx, "b0bankB", [128, 224])
    pWB = bankA[:, 0:128]
    pSC = bankA[:, 128:256]
    pY = bankA[:, 256:384]
    pEC = bankA[:, 384:416]
    pXTt = pst(nc, ctx, "b0pXT", [128, 2 * HD], BF16)
    pTR = bankB[:, 128:224]
    pBTt = pst(nc, ctx, "b0pBT", [128, LC], BF16)
    pBT = pBTt[:, :]

    with ExitStack() as pctx:
        owr = [sbt(nc, pctx, f"owr{k}", [128, DIN0], BF16) for k in range(ND)]
        pjt = [sbt(nc, pctx, f"pjt{k}", [128, D], BF16) for k in range(ND)]
        for k in range(ND):
            c.dma("gpsimd", lambda h, k=k: h.dma_start(
                out=owr[k][:], in_=Wd["outWraw"][k * 128:(k + 1) * 128, :]))
            c.dma("gpsimd", lambda h, k=k: h.dma_start(
                out=pjt[k][:], in_=Wd["projT"][k * 128:(k + 1) * 128, :]))
        c.dwait("tensor", "gpsimd")
        for m in range(NPO):
            for nh in range(2):
                def mmp(h, m=m, nh=nh):
                    for k in range(ND):
                        ins = h.matmul(
                            out=pIP[nh][:, 0:TC],
                            lhsT=owr[k][:, m * 128:(m + 1) * 128],
                            rhs=pjt[k][:, nh * TC:(nh + 1) * TC],
                            start=(k == 0), stop=(k == ND - 1))
                    return ins
                c.op("tensor", mmp, tick=True)
                c.op("vector", lambda h, m=m, nh=nh: h.tensor_copy(
                    out=POT[m][:, nh * TC:(nh + 1) * TC], in_=pIP[nh][:, 0:TC]),
                    tick=True)
        c.barrier()

    xsb = sbt(nc, ctx, "b0xsb", [128, 16 * TC], BF16)
    zb = sbt(nc, ctx, "b0zb", [128, 16 * TC], BF16)
    dts = sbt(nc, ctx, "b0dts", [32, TC], F32)
    W_ = sbt(nc, ctx, "b0W", [32, TC], F32)
    V_ = sbt(nc, ctx, "b0V", [32, TC], F32)
    wflat = sbt(nc, ctx, "b0wfl", [1, NH * LC], F32)
    dtT = sbt(nc, ctx, "b0dtT", [128, 32], F32)
    negb = sbt(nc, ctx, "b0negb", [128, 32], F32)
    dect = sbt(nc, ctx, "b0dec", [128, 32], F32)
    combT = sbt(nc, ctx, "b0comb", [128, 32], F32)
    echb = sbt(nc, ctx, "b0echb", [128, 32], F32)
    scr32 = sbt(nc, ctx, "b0scr32", [128, 32], F32)
    scoresM = sbt(nc, ctx, "b0scm", [128, LC], F32)
    BT = sbt(nc, ctx, "b0BT", [128, LC], BF16)
    Ea = sbt(nc, ctx, "b0Ea", [128, 2 * LC], F32)
    Ep = sbt(nc, ctx, "b0Ep", [128, 2 * LC], F32)
    dtxT = sbt(nc, ctx, "b0dtx", [128, 2 * HD], BF16)
    dtx3 = sbt(nc, ctx, "b0dtx3", [128, 2 * HD], BF16)
    Mt = sbt(nc, ctx, "b0Mt", [128, 2 * LC], BF16)
    Cpp = sbt(nc, ctx, "b0Cpp", [128, 2 * LC], BF16)
    y_sb = sbt(nc, ctx, "b0y", [128, 16 * TC], F32)
    yn_sb = sbt(nc, ctx, "b0yn", [128, 16 * TC], BF16)
    zsil = sbt(nc, ctx, "b0zs", [128, TC], F32)
    sq1 = sbt(nc, ctx, "b0sq1", [128, TC], F32)
    s0a = sbt(nc, ctx, "b0s0a", [128, TC], F32)
    s0b = sbt(nc, ctx, "b0s0b", [128, TC], F32)
    grms = sbt(nc, ctx, "b0grms", [1, TC], F32)
    rtmp = sbt(nc, ctx, "b0rtmp", [1, TC], F32)
    aheads = sbt(nc, ctx, "b0ah", [32, 1], F32)
    abc = sbt(nc, ctx, "b0abc", [128, 32], F32)
    S0f = sbt(nc, ctx, "b0Sf", [128, NH * HD], F32)
    S0bf = sbt(nc, ctx, "b0Sb", [128, NH * HD], BF16)
    alog = sbt(nc, ctx, "b0alog", [32, 1], F32)
    drep = sbt(nc, ctx, "b0drep", [128, 16], F32)
    gnw = sbt(nc, ctx, "b0gnw", [128, 16], F32)

    c.dma("sync", lambda h: h.dma_start(out=alog[:], in_=Wd["alog"][:, :]))
    c.dma("sync", lambda h: h.dma_start(out=drep[:], in_=Wd["drep"][:, :]))
    c.dma("sync", lambda h: h.dma_start(out=gnw[:], in_=Wd["gnw"][:, :]))
    for e in ("scalar", "vector", "tensor"):
        c.dwait(e, "sync")
    c.op("scalar", lambda h: h.activation(out=aheads[:, :], in_=alog[:, :],
         func=AF.Exp), tick=True)
    c.op("vector", lambda h: h.tensor_scalar(out=aheads[:, :], in0=aheads[:, :],
         scalar1=-1.0, scalar2=None, op0=ALU.mult), tick=True)
    c.dma("gpsimd", lambda h: h.dma_start(out=wflat[0:1, 0:32],
         in_=aheads[:, :], allow_slow_non_contiguous=True))
    c.dwait("tensor", "gpsimd")
    c.op("tensor", lambda h: h.matmul(out=pEC, lhsT=ones_row[0:1, :],
         rhs=wflat[0:1, 0:32], start=True, stop=True), tick=True)
    c.op("vector", lambda h: h.tensor_copy(out=abc[:, :], in_=pEC), tick=True)
    c.op("vector", lambda h: h.memset(S0f[:, :], 0.0))
    c.op("vector", lambda h: h.memset(S0bf[:, :], 0.0), tick=True)

    for tc in range(NCH):
        t0 = tc * TC
        c.dma("sync", lambda h, t0=t0: h.dma_start(
            out=xsb[:, :].rearrange("p (d t) -> p d t", d=16),
            in_=sp["xs"][:, :, t0:t0 + TC].transpose([1, 0, 2])))
        c.dma("sync", lambda h, t0=t0: h.dma_start(
            out=zb[:, :].rearrange("p (d t) -> p d t", d=16),
            in_=sp["z"][:, :, t0:t0 + TC].transpose([1, 0, 2])))
        c.dma("sync", lambda h, t0=t0: h.dma_start(
            out=dts[:, :], in_=sp["dts"][:, t0:t0 + TC]))
        for e in ("scalar", "vector", "tensor", "gpsimd"):
            c.dwait(e, "sync")
        for lc in range(NLC):
            l0 = lc * LC
            c.op("vector", lambda h, l0=l0: h.tensor_tensor_scan(
                out=W_[:, l0:l0 + LC], data0=ones32[:, 0:LC],
                data1=dts[:, l0:l0 + LC], initial=0.0,
                op0=ALU.mult, op1=ALU.add), tick=True)
            c.fence("vector")
            c.op("vector", lambda h, l0=l0: h.tensor_scalar(
                out=V_[:, l0:l0 + LC], in0=W_[:, l0:l0 + LC], scalar1=-1.0,
                scalar2=W_[:, l0 + LC - 1:l0 + LC], op0=ALU.mult, op1=ALU.add),
                tick=True)
        for lc in range(NLC):
            l0 = lc * LC

            def trs(h, l0=l0):
                h.transpose(out=pTR[:, 0:32], in_=dts[:, l0:l0 + LC],
                            identity=id128[0:32, 0:32])
                h.transpose(out=pTR[:, 32:64], in_=W_[:, l0:l0 + LC],
                            identity=id128[0:32, 0:32])
                return h.transpose(out=pTR[:, 64:96], in_=V_[:, l0:l0 + LC],
                                   identity=id128[0:32, 0:32])
            c.op("tensor", trs, tick=True)
            c.op("vector", lambda h: h.tensor_copy(out=dtT[:, :],
                 in_=pTR[:, 0:32]))
            c.op("vector", lambda h: h.scalar_tensor_tensor(
                out=negb[:, :], in0=abc[:, :], scalar=-1.0, in1=pTR[:, 32:64],
                op0=ALU.mult, op1=ALU.mult))
            c.op("vector", lambda h: h.tensor_tensor(
                out=scr32[:, :], in0=abc[:, :], in1=pTR[:, 64:96],
                op=ALU.mult), tick=True)
            c.op("scalar", lambda h: h.activation(out=dect[:, :],
                 in_=scr32[:, :], func=AF.Exp), tick=True)
            c.op("vector", lambda h: h.tensor_tensor(out=combT[:, :],
                 in0=dtT[:, :], in1=dect[:, :], op=ALU.mult), tick=True)
            c.dma("gpsimd", lambda h, l0=l0: h.dma_start(
                out=wflat[0:1, :].rearrange("a (b c) -> a b c", b=NH),
                in_=W_[:, l0:l0 + LC]))
            c.dwait("tensor", "gpsimd")
            c.op("tensor", lambda h: h.matmul(out=pEC,
                 lhsT=ones_row[0:1, :], rhs=wflat[0:1, LC - 1::LC],
                 start=True, stop=True), tick=True)
            c.op("vector", lambda h: h.tensor_tensor(out=scr32[:, :],
                 in0=abc[:, :], in1=pEC, op=ALU.mult), tick=True)
            c.op("scalar", lambda h: h.activation(out=echb[:, :],
                 in_=scr32[:, :], func=AF.Exp), tick=True)
            c.op("tensor", lambda h, t0=t0, l0=l0: h.matmul(out=pSC,
                 lhsT=Bb[:, t0 + l0:t0 + l0 + LC], rhs=Cb[:, t0 + l0:t0 + l0 + LC],
                 start=True, stop=True), tick=True)
            c.op("vector", lambda h: h.tensor_tensor(out=scoresM[:, :],
                 in0=pSC, in1=tri[:, :], op=ALU.mult), tick=True)
            c.op("tensor", lambda h, t0=t0, l0=l0: h.transpose(out=pBT,
                 in_=Bb[:, t0 + l0:t0 + l0 + LC], identity=idbf[:, :]),
                 tick=True)
            c.op("vector", lambda h: h.tensor_copy(out=BT[:, :], in_=pBT),
                 tick=True)
            if dbg_ssd is not None and tc == 0 and lc == 0:
                c.op("vector", lambda h: h.tensor_copy(out=zsil[:, 128:160],
                     in_=pTR[:, 64:96]), tick=True)
                c.dma("sync", lambda h: h.dma_start(out=dbg2[0:128, 0:32], in_=dtT[:, :]))
                c.dma("sync", lambda h: h.dma_start(out=dbg2[0:128, 32:64], in_=dect[:, :]))
                c.dma("sync", lambda h: h.dma_start(out=dbg2[0:128, 64:96], in_=combT[:, :]))
                c.dma("sync", lambda h: h.dma_start(out=dbg2[0:128, 96:128], in_=zsil[:, 128:160]))
                c.dma("sync", lambda h: h.dma_start(out=dbg2[0:128, 128:160], in_=negb[:, :]))
                c.dma("sync", lambda h: h.dma_start(out=dbg2[0:128, 160:192], in_=echb[:, :]))
                c.dwait("vector", "sync")
                c.dwait("scalar", "sync")
            for hq in range(NH // 2):
                hd0 = 2 * hq

                def pe1(h, hd0=hd0, hq=hq, l0=l0):
                    h.matmul(out=bankA[:, 0:256], lhsT=ones_row[0:1, :],
                             rhs=wflat[0:1, hd0 * LC:(hd0 + 2) * LC],
                             start=True, stop=True)
                    return h.transpose(
                        out=pXTt[:, 0:2 * HD],
                        in_=xsb[0:128, hq * TC + l0:hq * TC + l0 + LC],
                        identity=Cst["idbf"][:, :])
                c.op("tensor", pe1, tick=True)
                for s_ in range(2):
                    hd = hd0 + s_
                    esl = slice(s_ * LC, (s_ + 1) * LC)
                    c.op("scalar", lambda h, hd=hd, esl=esl, s_=s_:
                         h.activation(
                             out=Ea[:, esl],
                             in_=bankA[:, s_ * 128:(s_ + 1) * 128],
                             func=AF.Exp,
                             scale=abc[:, hd:hd + 1], bias=negb[:, hd:hd + 1]))
                    c.op("scalar", lambda h, hd=hd, esl=esl, s_=s_:
                         h.activation(
                             out=Ep[:, esl],
                             in_=bankA[:, s_ * 128:(s_ + 1) * 128],
                             func=AF.Exp,
                             scale=abc[:, hd:hd + 1]))
                c.tick_last("scalar")
                for s_ in range(2):
                    hd = hd0 + s_
                    xsl = slice(s_ * HD, (s_ + 1) * HD)
                    esl = slice(s_ * LC, (s_ + 1) * LC)
                    c.op("vector", lambda h, hd=hd, xsl=xsl: h.tensor_scalar(
                        out=dtxT[:, xsl], in0=pXTt[:, xsl],
                        scalar1=dtT[:, hd:hd + 1],
                        scalar2=None, op0=ALU.mult))
                    c.op("vector", lambda h, hd=hd, xsl=xsl: h.tensor_scalar(
                        out=dtx3[:, xsl], in0=pXTt[:, xsl],
                        scalar1=combT[:, hd:hd + 1],
                        scalar2=None, op0=ALU.mult))
                    c.op("vector", lambda h, esl=esl: h.scalar_tensor_tensor(
                        out=Mt[:, esl], in0=Ea[:, esl], scalar=BIG,
                        in1=scoresM[:, :],
                        op0=ALU.min, op1=ALU.mult))
                    c.op("vector", lambda h, t0=t0, l0=l0, esl=esl:
                         h.tensor_tensor(
                             out=Cpp[:, esl],
                             in0=Cb[:, t0 + l0:t0 + l0 + LC], in1=Ep[:, esl],
                             op=ALU.mult))
                c.tick_last("vector")

                def pe2(h, hd0=hd0):
                    h.matmul(out=pY[0:64], lhsT=dtxT[:, 0:HD],
                             rhs=Mt[:, 0:LC], start=True, stop=False)
                    h.matmul(out=pY[0:64],
                             lhsT=S0bf[:, hd0 * HD:(hd0 + 1) * HD],
                             rhs=Cpp[:, 0:LC],
                             start=False, stop=True)
                    h.matmul(out=pY[64:128], lhsT=dtxT[:, HD:2 * HD],
                             rhs=Mt[:, LC:2 * LC], start=True, stop=False)
                    h.matmul(out=pY[64:128],
                             lhsT=S0bf[:, (hd0 + 1) * HD:(hd0 + 2) * HD],
                             rhs=Cpp[:, LC:2 * LC],
                             start=False, stop=True)
                    return h.matmul(out=bankB[:, 0:128], lhsT=BT[:, :],
                                    rhs=dtx3[:, 0:2 * HD],
                                    start=True, stop=True)
                c.op("tensor", pe2, tick=True)
                if dbg_ssd is not None and tc == 0 and lc == 0 and hd == 0:
                    c.op("vector", lambda h: h.tensor_copy(out=zsil[:, 0:128],
                         in_=pY[0:128]), tick=True)
                    c.dma("sync", lambda h: h.dma_start(out=dbg_ssd[0:128, 0:128], in_=zsil[:, 0:128]))
                    c.dma("sync", lambda h: h.dma_start(out=dbg_ssd[0:128, 128:256], in_=scoresM[:, :]))
                    c.dma("gpsimd", lambda h: h.dma_start(out=dbg_ssd[0:128, 256:384], in_=Mt[:, :]))
                    c.dma("gpsimd", lambda h: h.dma_start(out=dbg_ssd[0:128, 384:448], in_=dtxT[:, :]))
                    c.dma("gpsimd", lambda h: h.dma_start(out=dbg_ssd[0:128, 448:512], in_=dtx3[:, :]))
                c.op("vector", lambda h, hd0=hd0: h.scalar_tensor_tensor(
                    out=S0f[:, hd0 * HD:(hd0 + 1) * HD],
                    in0=S0f[:, hd0 * HD:(hd0 + 1) * HD],
                    scalar=echb[:, hd0:hd0 + 1], in1=bankB[:, 0:64],
                    op0=ALU.mult, op1=ALU.add))
                c.op("vector", lambda h, hd0=hd0: h.scalar_tensor_tensor(
                    out=S0f[:, (hd0 + 1) * HD:(hd0 + 2) * HD],
                    in0=S0f[:, (hd0 + 1) * HD:(hd0 + 2) * HD],
                    scalar=echb[:, hd0 + 1:hd0 + 2], in1=bankB[:, 64:128],
                    op0=ALU.mult, op1=ALU.add))
                c.op("vector", lambda h, hd0=hd0: h.tensor_copy(
                    out=S0bf[:, hd0 * HD:(hd0 + 2) * HD],
                    in_=S0f[:, hd0 * HD:(hd0 + 2) * HD]))
                c.op("vector", lambda h, hq=hq, l0=l0: h.scalar_tensor_tensor(
                    out=y_sb[0:64, hq * TC + l0:hq * TC + l0 + LC],
                    in0=xsb[0:64, hq * TC + l0:hq * TC + l0 + LC],
                    scalar=drep[0:64, hq:hq + 1], in1=pY[0:64],
                    op0=ALU.mult, op1=ALU.add))
                c.op("vector", lambda h, hq=hq, l0=l0: h.scalar_tensor_tensor(
                    out=y_sb[64:128, hq * TC + l0:hq * TC + l0 + LC],
                    in0=xsb[64:128, hq * TC + l0:hq * TC + l0 + LC],
                    scalar=drep[64:128, hq:hq + 1], in1=pY[64:128],
                    op0=ALU.mult, op1=ALU.add), tick=True)
        if dbg_y is not None:
            c.dma("sync", lambda h, t0=t0: h.dma_start(
                out=dbg_y[:, :, t0:t0 + TC].transpose([1, 0, 2]),
                in_=y_sb[:, :].rearrange("p (d t) -> p d t", d=16)))
            c.dma("sync", lambda h, t0=t0: h.dma_start(
                out=dbg_W[:, t0:t0 + TC], in_=W_[:, :]))
            c.dwait("vector", "sync")
            c.dwait("scalar", "sync")
        for of in range(16):
            sl = slice(of * TC, (of + 1) * TC)
            c.op("vector", lambda h, sl=sl: h.tensor_tensor(
                out=y_sb[:, sl], in0=y_sb[:, sl], in1=zb[:, sl],
                op=ALU.mult), tick=True)
            c.op("scalar", lambda h, sl=sl: h.activation(
                out=sq1[:, :], in_=y_sb[:, sl], func=AF.Square), tick=True)
            c.op("tensor", lambda h, of=of: h.matmul(
                out=pS[:, :], lhsT=ones_col[:, 0:1], rhs=sq1[:, :],
                start=(of == 0), stop=(of == 15)), tick=(of == 15))
        c.op("vector", lambda h: h.tensor_scalar(out=rtmp[:, :], in0=pS[0:1, :],
             scalar1=1.0 / DIN0, scalar2=1e-5, op0=ALU.mult, op1=ALU.add))
        c.op("vector", lambda h: h.reciprocal(out=grms[:, :], in_=rtmp[:, :]),
             tick=True)
        c.op("scalar", lambda h: h.sqrt(out=grms[:, :], in_=grms[:, :]),
             tick=True)
        c.op("tensor", lambda h: h.matmul(out=pB1[:, :], lhsT=ones_row[0:1, :],
             rhs=grms[:, :], start=True, stop=True), tick=True)
        for of in range(16):
            sl = slice(of * TC, (of + 1) * TC)
            c.op("vector", lambda h, of=of, sl=sl: h.scalar_tensor_tensor(
                out=yn_sb[:, sl], in0=y_sb[:, sl], scalar=gnw[:, of:of + 1],
                in1=pB1[:, :], op0=ALU.mult, op1=ALU.mult))
        c.tick_last("vector")
        def ev_op(of, t0):
            st = s0a if of % 2 == 0 else s0b
            if of >= 2:
                c.dwait("vector", "gpsimd")
            c.op("vector", lambda h, of=of, st=st: h.tensor_copy(
                out=st[:, :], in_=pIP[of % 2][:, :]), tick=True)
            c.dma("gpsimd", lambda h, of=of, st=st, t0=t0: h.dma_start(
                out=S0_dram[of, :, t0:t0 + TC], in_=st[:, :],
                accum_op=ALU.add))
        for of in range(ND):
            def mmo(h, of=of):
                for k in range(NPO):
                    ins = h.matmul(out=pIP[of % 2][:, :],
                                   lhsT=POT[k][:, of * 128:(of + 1) * 128],
                                   rhs=yn_sb[:, k * TC:(k + 1) * TC],
                                   start=(k == 0), stop=(k == NPO - 1))
                return ins
            c.op("tensor", mmo, tick=True)
            if of >= 1:
                ev_op(of - 1, t0)
        ev_op(ND - 1, t0)
        c.barrier()


def emit_l0ffn(nc, ctx, c, x_dram, S0_dram, Wd, Cst):
    TCF = 256
    NCF = TOWN // TCF
    NOF = HID0 // 128
    wgb = [sbt(nc, ctx, f"wgb{k}", [128, HID0], BF16) for k in range(ND)]
    wub = [sbt(nc, ctx, f"wub{k}", [128, HID0], BF16) for k in range(ND)]
    wdb = [sbt(nc, ctx, f"wdb{k}", [128, D], BF16) for k in range(NOF)]
    for k in range(ND):
        c.dma("gpsimd", lambda h, k=k: h.dma_start(
            out=wgb[k][:], in_=Wd["wgT"][k * 128:(k + 1) * 128, :]))
        c.dma("gpsimd", lambda h, k=k: h.dma_start(
            out=wub[k][:], in_=Wd["wuT"][k * 128:(k + 1) * 128, :]))
    gumk = c.mark("gpsimd")
    for k in range(NOF):
        c.dma("gpsimd", lambda h, k=k: h.dma_start(
            out=wdb[k][:], in_=Wd["wdT"][k * 128:(k + 1) * 128, :]))
    xt = sbt(nc, ctx, "f0xt", [128, ND * TCF], F32)
    sqt = sbt(nc, ctx, "f0sq", [128, ND * TCF], F32)
    xnb = sbt(nc, ctx, "f0xn", [128, ND * TCF], BF16)
    gub = sbt(nc, ctx, "f0gu", [128, NOF * TCF], BF16)
    sgt = sbt(nc, ctx, "f0sg", [128, TCF], BF16)
    f0t = [sbt(nc, ctx, f"f0o{i}", [128, TCF], F32) for i in range(2)]
    rows = make_rows(nc, ctx, "f0r")
    pS = pst(nc, ctx, "f0pS", [1, TCF])
    pQ = pst(nc, ctx, "f0pQ", [1, TCF])
    pB1 = pst(nc, ctx, "f0pB1", [128, TCF])
    pB2 = pst(nc, ctx, "f0pB2", [128, TCF])
    pIP = [pst(nc, ctx, f"f0pIP{i}", [128, TCF]) for i in range(2)]
    ones_col, ones_row = Cst["ones_col"], Cst["ones_row"]
    c.dwait_cnt("tensor", "gpsimd", gumk)
    c.dwait_cnt("vector", "gpsimd", gumk)
    c.dwait_cnt("scalar", "gpsimd", gumk)
    for fc in range(NCF):
        t0 = fc * TCF
        c.dma("sync", lambda h, t0=t0: h.dma_start(
            out=xt[:, :].rearrange("p (d t) -> p d t", d=ND),
            in_=x_dram[:, :, t0:t0 + TCF].transpose([1, 0, 2])))
        c.dwait("scalar", "sync")
        c.dwait("tensor", "sync")
        c.dwait("vector", "sync")
        emit_ln_stats(nc, c, xt, sqt, ones_col, pS, pQ, rows, eps2=None, W=TCF)
        c.op("tensor", lambda h: h.matmul(out=pB1[:, 0:TCF], lhsT=ones_row[0:1, :],
             rhs=rows["g1"][:, 0:TCF], start=True, stop=True), tick=True)
        c.op("tensor", lambda h: h.matmul(out=pB2[:, 0:TCF], lhsT=ones_row[0:1, :],
             rhs=rows["mg1"][:, 0:TCF], start=True, stop=True), tick=True)
        for d in range(ND):
            sl = slice(d * TCF, (d + 1) * TCF)
            c.op("vector", lambda h, sl=sl: h.tensor_tensor(
                out=sqt[:, sl], in0=xt[:, sl], in1=pB1[:, 0:TCF], op=ALU.mult))
            c.op("vector", lambda h, sl=sl: h.tensor_tensor(
                out=xnb[:, sl], in0=sqt[:, sl], in1=pB2[:, 0:TCF],
                op=ALU.subtract))
        c.tick_last("vector")
        def mmg(h, of):
            for k in range(ND):
                ins = h.matmul(out=pIP[0][:, 0:TCF],
                               lhsT=wgb[k][:, of * 128:(of + 1) * 128],
                               rhs=xnb[:, k * TCF:(k + 1) * TCF],
                               start=(k == 0), stop=(k == ND - 1))
            return ins

        def mmu(h, of):
            for k in range(ND):
                ins = h.matmul(out=pIP[1][:, 0:TCF],
                               lhsT=wub[k][:, of * 128:(of + 1) * 128],
                               rhs=xnb[:, k * TCF:(k + 1) * TCF],
                               start=(k == 0), stop=(k == ND - 1))
            return ins
        for of in range(NOF):
            c.op("tensor", lambda h, of=of: mmg(h, of), tick=True)
            c.op("tensor", lambda h, of=of: mmu(h, of), tick=True)
            c.op("scalar", lambda h: h.activation(
                out=sgt[:, :], in_=pIP[0][:, 0:TCF], func=AF.Silu, scale=QS),
                tick=True)
            c.op("vector", lambda h, of=of: h.scalar_tensor_tensor(
                out=gub[:, of * TCF:(of + 1) * TCF], in0=pIP[1][:, 0:TCF],
                scalar=QS, in1=sgt[:, :], op0=ALU.mult, op1=ALU.mult),
                tick=True)
        def ev_dn(of2, t0):
            ft = f0t[of2 % 2]
            if of2 >= 2:
                c.dwait("vector", "gpsimd")
            c.op("vector", lambda h, of2=of2, ft=ft: h.tensor_scalar(
                out=ft[:, :], in0=pIP[of2 % 2][:, 0:TCF], scalar1=QS,
                scalar2=None, op0=ALU.mult), tick=True)
            c.dma("gpsimd", lambda h, of2=of2, ft=ft, t0=t0: h.dma_start(
                out=S0_dram[of2, :, t0:t0 + TCF], in_=ft[:, :],
                accum_op=ALU.add))
        c.dwait("tensor", "gpsimd")
        for of2 in range(ND):
            def mmd(h, of2=of2):
                for k in range(NOF):
                    ins = h.matmul(out=pIP[of2 % 2][:, 0:TCF],
                                   lhsT=wdb[k][:, of2 * 128:(of2 + 1) * 128],
                                   rhs=gub[:, k * TCF:(k + 1) * TCF],
                                   start=(k == 0), stop=(k == NOF - 1))
                return ins
            c.op("tensor", mmd, tick=True)
            if of2 >= 1:
                ev_dn(of2 - 1, t0)
        ev_dn(ND - 1, t0)
        c.barrier()


def load_consts(nc, ctx, c):
    Cst = {}
    decls = [("ones_col", [128, 1], F32), ("ones_row", [1, 128], F32),
             ("id128", [128, 128], F32), ("idbf", [128, 128], BF16),
             ("tri", [128, 128], F32), ("ones32", [32, 128], F32),
             ("negones", [1, TC], F32), ("idbf64", [128, 64], BF16),
             ("ones_row_bf", [1, 128], BF16)]
    for nm, shape, dt in decls:
        dr = nc.dram_tensor("c_" + nm, shape, dt, kind="ExternalInput").ap()
        t = sbt(nc, ctx, "cs_" + nm, shape, dt)
        c.dma("sync", lambda h, t=t, dr=dr: h.dma_start(out=t[:], in_=dr[:, :]))
        Cst[nm] = t
    return Cst


def host_consts():
    return {
        "c_ones_col": np.ones((128, 1), np.float32),
        "c_ones_row": np.ones((1, 128), np.float32),
        "c_id128": np.eye(128, dtype=np.float32),
        "c_idbf": np.eye(128).astype(ml_dtypes.bfloat16),
        "c_tri": np.triu(np.ones((128, 128), np.float32)),
        "c_ones32": np.ones((32, 128), np.float32),
        "c_negones": -np.ones((1, TC), np.float32),
        "c_idbf64": np.vstack([np.eye(64), np.eye(64)]).astype(ml_dtypes.bfloat16),
        "c_ones_row_bf": np.ones((1, 128), ml_dtypes.bfloat16),
    }


def col_form(v, nt=None):
    v = np.asarray(v, np.float32)
    n = v.shape[0]
    nt = n // 128
    return np.ascontiguousarray(v.reshape(nt, 128).T)


def prep_l0_weights(inp, rev):
    d = "bwd" if rev else "fwd"
    W = {}
    W["inWT"] = np.ascontiguousarray(inp[f"L0_attn_{d}_in_W"].T)
    W["outWraw"] = np.ascontiguousarray(inp[f"L0_attn_{d}_out_W"])
    pj = inp["L0_attn_proj_W"]
    W["projT"] = np.ascontiguousarray((pj[:, 1024:] if rev else pj[:, :1024]).T)
    cw = inp[f"L0_attn_{d}_conv_w"]
    W["convw"] = np.ascontiguousarray(
        cw.reshape(18, 128, 4).transpose(1, 0, 2).reshape(128, 72))
    W["convb"] = col_form(inp[f"L0_attn_{d}_conv_b"])
    W["dtb"] = np.asarray(inp[f"L0_attn_{d}_dt_bias"], np.float32).reshape(32, 1)
    W["alog"] = np.asarray(inp[f"L0_attn_{d}_A_log"], np.float32).reshape(32, 1)
    W["drep"] = col_form(np.repeat(np.asarray(inp[f"L0_attn_{d}_D"], np.float32), HD))
    W["gnw"] = col_form(inp[f"L0_attn_{d}_gnorm_w"])
    W["projb"] = col_form(inp["L0_attn_proj_b"])
    W["nwc"] = col_form(inp["L0_attn_norm_w"])
    W["wgT"] = np.ascontiguousarray(inp["L0_ffn_w_gate"].T)
    W["wuT"] = np.ascontiguousarray(inp["L0_ffn_w_up"].T)
    W["wdT"] = np.ascontiguousarray(inp["L0_ffn_w_down"].T)
    return W


L0W_SHAPES = [("inWT", [D, 4384], F32), ("outWraw", [D, DIN0], F32),
              ("projT", [D, D], F32), ("convw", [128, 72], F32),
              ("convb", [128, 18], F32), ("dtb", [32, 1], F32),
              ("alog", [32, 1], F32), ("drep", [128, 16], F32),
              ("gnw", [128, 16], F32), ("projb", [128, ND], F32),
              ("nwc", [128, ND], F32),
              ("wgT", [D, HID0], I32), ("wuT", [D, HID0], I32),
              ("wdT", [HID0, D], I32)]


def declare_l0_scratch(nc, dbg=False):
    kind = "ExternalOutput" if dbg else "Internal"
    return {
        "xs": nc.dram_tensor("sp_xs", [16, 128, T], BF16, kind=kind).ap(),
        "z": nc.dram_tensor("sp_z", [16, 128, T], BF16, kind=kind).ap(),
        "dts": nc.dram_tensor("sp_dts", [32, T], F32, kind=kind).ap(),
    }


def emit_l0(nc, ctx, c, x_dram, S0_dram, Wd, Cst, parts="abf", dbg=False):
    sp = declare_l0_scratch(nc, dbg)
    dbg_h = (nc.dram_tensor("dbg_h", [ND, 128, T], BF16,
                            kind="ExternalOutput").ap() if dbg else None)
    dbg_y = (nc.dram_tensor("dbg_y", [16, 128, T], F32,
                            kind="ExternalOutput").ap() if dbg else None)
    dbg_W = (nc.dram_tensor("dbg_W", [32, T], F32,
                            kind="ExternalOutput").ap() if dbg else None)
    dbg_ssd = (nc.dram_tensor("dbg_ssd", [128, 512], F32,
                              kind="ExternalOutput").ap() if dbg else None)
    dbg2 = (nc.dram_tensor("dbg2", [128, 256], F32,
                           kind="ExternalOutput").ap() if dbg else None)
    with ExitStack() as mctx:
        shared = {
            "Bb": sbt(nc, mctx, "l0Bb", [128, T], BF16),
            "Cb": sbt(nc, mctx, "l0Cb", [128, T], BF16),
        }
        with ExitStack() as actx:
            emit_l0passA(nc, actx, c, x_dram, S0_dram, sp, Wd, Cst, shared,
                         dbg_h)
            c.barrier()
        if "b" in parts:
            with ExitStack() as bctx:
                emit_l0passB(nc, bctx, c, S0_dram, sp, Wd, Cst, shared, dbg_y, dbg_W, dbg_ssd, dbg2)
                c.barrier()
    if "f" in parts:
        with ExitStack() as fctx:
            emit_l0ffn(nc, fctx, c, x_dram, S0_dram, Wd, Cst)
            c.barrier()


def build_l0(parts="abf", dbg=False):
    nc = bass.Bass("TRN2", target_bir_lowering=False)
    x_dram = nc.dram_tensor("x", [ND, 128, T], F32, kind="ExternalInput").ap()
    S0_dram = nc.dram_tensor("S0", [ND, 128, T], F32, kind="ExternalOutput").ap()
    Wd = {}
    for nm, shape, dt in L0W_SHAPES:
        Wd[nm] = nc.dram_tensor("w_" + nm, shape, dt, kind="ExternalInput").ap()
    with ExitStack() as ctx:
        c = Ctl(nc, ctx)
        Cst = load_consts(nc, ctx, c)
        emit_l0(nc, ctx, c, x_dram, S0_dram, Wd, Cst, parts, dbg)
        with nc.Block() as block:
            c.emit(block)
    return nc


def emit_combine(nc, ctx, c, x_dram, Sown, Sprev, x1_dram, rev_prev=False,
                 masks=None, G=None):
    xt = sbt(nc, ctx, "cbx", [128, TC], F32)
    at = sbt(nc, ctx, "cba", [128, TC], F32)
    bt = sbt(nc, ctx, "cbb", [128, TC], F32)
    g0 = sbt(nc, ctx, "cbg0", [128, TC], F32)
    g1t = sbt(nc, ctx, "cbg1", [128, TC], F32)
    ot = [sbt(nc, ctx, f"cbo{i}", [128, TC], F32) for i in range(2)]
    for tc in range(NCH):
        t0 = tc * TC
        tm = (NCH - 1 - tc) * TC
        for d in range(ND):
            c.dwait("vector", "sync")
            c.dma("sync", lambda h, d=d, t0=t0: h.dma_start(
                out=xt[:, :], in_=x_dram[d, :, t0:t0 + TC]))
            c.dma("sync", lambda h, d=d, t0=t0: h.dma_start(
                out=at[:, :], in_=Sown[d, :, t0:t0 + TC]))
            if masks is None:
                c.dma("sync", lambda h, d=d, t0=t0: h.dma_start(
                    out=bt[:, :], in_=Sprev[d, :, t0:t0 + TC]))
            else:
                c.dma("sync", lambda h, d=d, tm=tm: h.dma_start(
                    out=g0[:, :], in_=G[0][d, :, tm:tm + TC]))
                c.dma("sync", lambda h, d=d, tm=tm: h.dma_start(
                    out=g1t[:, :], in_=G[1][d, :, tm:tm + TC]))
            c.dwait("vector", "sync")
            o = ot[d % 2]
            c.op("vector", lambda h, o=o: h.tensor_tensor(
                out=o[:, :], in0=xt[:, :], in1=at[:, :], op=ALU.add))
            if masks is None:
                c.op("vector", lambda h, o=o: h.tensor_tensor(
                    out=o[:, :], in0=o[:, :], in1=bt[:, :], op=ALU.add),
                    tick=True)
            else:
                c.op("vector", lambda h, o=o: h.scalar_tensor_tensor(
                    out=o[:, :], in0=g0[:, ::-1], scalar=masks[:, 0:1],
                    in1=o[:, :], op0=ALU.mult, op1=ALU.add))
                c.op("vector", lambda h, o=o: h.scalar_tensor_tensor(
                    out=o[:, :], in0=g1t[:, ::-1], scalar=masks[:, 1:2],
                    in1=o[:, :], op0=ALU.mult, op1=ALU.add), tick=True)
            c.dma("sync", lambda h, d=d, t0=t0, o=o: h.dma_start(
                out=x1_dram[d, :, t0:t0 + TC], in_=o[:, :]))
    c.barrier()


def emit_l1passA(nc, ctx, c, x1_dram, sp1, Wd, Cst, shared1):
    w1in = [sbt(nc, ctx, f"w1in{k}", [128, 2 * DIN1], BF16) for k in range(ND)]
    convw = sbt(nc, ctx, "cvw1", [128, 64], F32)
    convb = sbt(nc, ctx, "cvb1", [128, 16], F32)
    lngr = sbt(nc, ctx, "lngr", [1, D], F32)
    lnbr = sbt(nc, ctx, "lnbr", [1, D], F32)
    lngc = sbt(nc, ctx, "lngc", [128, ND], F32)
    xt = sbt(nc, ctx, "a1xt", [128, ND * TC], F32)
    sqt = sbt(nc, ctx, "a1sq", [128, ND * TC], F32)
    xn2b = sbt(nc, ctx, "a1xn2", [128, ND * TC], BF16)
    xnst = sbt(nc, ctx, "a1xnst", [128, TC], BF16)
    xnf = sbt(nc, ctx, "a1xnf", [128, TC], F32)
    wgf = sbt(nc, ctx, "a1wgf", [128, ND * NE], F32)
    scst = sbt(nc, ctx, "a1scst", [8, TC], F32)
    wgsum = sbt(nc, ctx, "a1wgsum", [8, 1], F32)
    xbc = [sbt(nc, ctx, f"a1xbc{i}", [128, 3 + TC], F32) for i in range(NTI1)]
    halo = sbt(nc, ctx, "a1halo", [128, 48], F32)
    ct = sbt(nc, ctx, "a1ct", [128, TC], F32)
    stg = [sbt(nc, ctx, f"a1stg{i}", [128, TC], BF16) for i in range(2)]
    stf = [sbt(nc, ctx, f"a1stf{i}", [128, TC], F32) for i in range(2)]
    rows = make_rows(nc, ctx, "a1r")
    pS = pst(nc, ctx, "a1pS", [1, TC])
    pQ = pst(nc, ctx, "a1pQ", [1, TC])
    pB1 = pst(nc, ctx, "a1pB1", [128, TC])
    pB2 = pst(nc, ctx, "a1pB2", [128, TC])
    pIP = [pst(nc, ctx, f"a1pIP{i}", [128, TC]) for i in range(2)]
    pSC8 = pst(nc, ctx, "a1pSC8", [8, TC])
    ones_col, ones_row = Cst["ones_col"], Cst["ones_row"]

    for k in range(ND):
        c.dma("gpsimd", lambda h, k=k: h.dma_start(
            out=w1in[k][:], in_=Wd["in1T"][k * 128:(k + 1) * 128, :]))
    c.dma("sync", lambda h: h.dma_start(out=wgf[:], in_=Wd["WgT"][:, :]))
    for nm, dst in [("conv1w", convw), ("conv1b", convb),
                    ("lngr", lngr), ("lnbr", lnbr), ("lngc", lngc)]:
        c.dma("sync", lambda h, nm=nm, dst=dst: h.dma_start(
            out=dst[:], in_=Wd[nm][:, :]))
    for e in ("scalar", "vector", "tensor"):
        c.dwait(e, "sync")
        c.dwait(e, "gpsimd")
    c.op("vector", lambda h: h.memset(halo[:, :], 0.0), tick=True)

    def mmwsum(h):
        for d in range(ND):
            ins = h.matmul(out=pSC8[:, 0:1], lhsT=wgf[:, d * NE:(d + 1) * NE],
                           rhs=Cst["ones_col"][:, 0:1],
                           start=(d == 0), stop=(d == ND - 1))
        return ins
    c.op("tensor", mmwsum, tick=True)
    c.op("vector", lambda h: h.tensor_copy(out=wgsum[:, :],
         in_=pSC8[:, 0:1]), tick=True)

    for tc in range(NCH):
        t0 = tc * TC
        c.dma("sync", lambda h, t0=t0: h.dma_start(
            out=xt[:, :].rearrange("p (d t) -> p d t", d=ND),
            in_=x1_dram[:, :, t0:t0 + TC].transpose([1, 0, 2])))
        for e in ("scalar", "vector", "tensor"):
            c.dwait(e, "sync")
        emit_ln_stats(nc, c, xt, sqt, ones_col, pS, pQ, rows, eps2=1e-5)
        if tc < NCHO:
            c.op("tensor", lambda h: h.matmul(out=pB1[:, :], lhsT=ones_row[0:1, :],
                 rhs=rows["g1"][:, :], start=True, stop=True), tick=True)
            c.op("tensor", lambda h: h.matmul(out=pB2[:, :], lhsT=ones_row[0:1, :],
                 rhs=rows["mg1"][:, :], start=True, stop=True), tick=True)
            for d in range(ND):
                sl = slice(d * TC, (d + 1) * TC)
                c.dwait("vector", "sync")
                c.op("vector", lambda h, sl=sl: h.tensor_tensor(
                    out=sqt[:, sl], in0=xt[:, sl], in1=pB1[:, :], op=ALU.mult))
                c.op("vector", lambda h, sl=sl: h.tensor_tensor(
                    out=xnst[:, :], in0=sqt[:, sl], in1=pB2[:, :],
                    op=ALU.subtract), tick=True)
                c.dma("sync", lambda h, d=d, t0=t0: h.dma_start(
                    out=sp1["xn"][d, :, t0:t0 + TC], in_=xnst[:, :]))
            def mmwx(h):
                for d in range(ND):
                    ins = h.matmul(out=pSC8[:, :],
                                   lhsT=wgf[:, d * NE:(d + 1) * NE],
                                   rhs=xt[:, d * TC:(d + 1) * TC],
                                   start=(d == 0), stop=(d == ND - 1))
                return ins
            c.op("tensor", mmwx, tick=True)
            c.op("scalar", lambda h: h.copy(out=scst[:, :],
                 in_=pSC8[:, :]), tick=True)
            c.fence("vector")
            c.op("vector", lambda h: h.tensor_tensor(out=scst[:, :],
                 in0=scst[:, :], in1=pB1[0:8, :], op=ALU.mult), tick=True)
            c.op("vector", lambda h: h.tensor_scalar(out=xnf[0:8, :],
                 in0=pB2[0:8, :], scalar1=wgsum[:, 0:1], scalar2=None,
                 op0=ALU.mult), tick=True)
            c.fence("vector")
            c.op("vector", lambda h: h.tensor_tensor(out=scst[:, :],
                 in0=scst[:, :], in1=xnf[0:8, :], op=ALU.subtract), tick=True)
            c.dma("sync", lambda h, t0=t0: h.dma_start(
                out=sp1["scores"][:, t0:t0 + TC], in_=scst[:, :]))
        c.op("tensor", lambda h: h.matmul(out=pB1[:, :], lhsT=ones_row[0:1, :],
             rhs=rows["g12"][:, :], start=True, stop=True), tick=True)
        for d in range(ND):
            sl = slice(d * TC, (d + 1) * TC)

            def mmr(h, d=d):
                h.matmul(out=pB2[:, :], lhsT=lngr[0:1, d * 128:(d + 1) * 128],
                         rhs=rows["mg12"][:, :], start=True, stop=False)
                return h.matmul(out=pB2[:, :],
                                lhsT=lnbr[0:1, d * 128:(d + 1) * 128],
                                rhs=Cst["negones"][0:1, :],
                                start=False, stop=True)
            c.op("tensor", mmr, tick=True)
            c.op("vector", lambda h, sl=sl: h.tensor_tensor(
                out=sqt[:, sl], in0=xt[:, sl], in1=pB1[:, :], op=ALU.mult))
            c.op("vector", lambda h, d=d, sl=sl: h.scalar_tensor_tensor(
                out=xn2b[:, sl], in0=sqt[:, sl], scalar=lngc[:, d:d + 1],
                in1=pB2[:, :], op0=ALU.mult, op1=ALU.subtract), tick=True)
        def cons1(of, t0):
            if of < 16:
                c.op("scalar", lambda h, of=of: h.copy(
                    out=xbc[of][:, 3:3 + TC], in_=pIP[of % 2][:, :]), tick=True)
            else:
                stz = stg[of % 2]
                c.dwait("scalar", "sync")
                c.op("scalar", lambda h, of=of, stz=stz: h.activation(
                    out=stz[:, :], in_=pIP[of % 2][:, :], func=AF.Silu),
                    tick=True)
                c.dma("sync", lambda h, of=of, stz=stz, t0=t0: h.dma_start(
                    out=sp1["z"][of - 16, :, t0:t0 + TC], in_=stz[:, :]))

        for of in range(32):
            def mms(h, of=of):
                for k in range(ND):
                    ins = h.matmul(out=pIP[of % 2][:, :],
                                   lhsT=w1in[k][:, of * 128:(of + 1) * 128],
                                   rhs=xn2b[:, k * TC:(k + 1) * TC],
                                   start=(k == 0), stop=(k == ND - 1))
                return ins
            c.op("tensor", mms, tick=True)
            if of >= 1:
                cons1(of - 1, t0)
        cons1(31, t0)
        for ti in range(NTI1):
            c.op("vector", lambda h, ti=ti: h.tensor_copy(
                out=xbc[ti][:, 0:3], in_=halo[:, ti * 3:ti * 3 + 3]))
        c.tick_last("vector")
        for ti in range(NTI1):
            c.op("vector", lambda h, ti=ti: h.tensor_scalar(
                out=ct[:, :], in0=xbc[ti][:, 0:TC],
                scalar1=convw[:, ti * 4:ti * 4 + 1],
                scalar2=convb[:, ti:ti + 1], op0=ALU.mult, op1=ALU.add))
            for k in range(1, 4):
                c.op("vector", lambda h, ti=ti, k=k: h.scalar_tensor_tensor(
                    out=ct[:, :], in0=xbc[ti][:, k:k + TC],
                    scalar=convw[:, ti * 4 + k:ti * 4 + k + 1], in1=ct[:, :],
                    op0=ALU.mult, op1=ALU.add))
            c.op("vector", lambda h, ti=ti: h.tensor_copy(
                out=halo[:, ti * 3:ti * 3 + 3], in_=xbc[ti][:, TC:TC + 3]),
                tick=True)
            stz = stf[ti % 2]
            c.dwait("scalar", "sync")
            c.op("scalar", lambda h, stz=stz: h.activation(
                out=stz[:, :], in_=ct[:, :], func=AF.Silu), tick=True)
            c.dma("sync", lambda h, ti=ti, stz=stz, t0=t0: h.dma_start(
                out=sp1["xc"][ti, :, t0:t0 + TC], in_=stz[:, :]))
        c.barrier()


def emit_l1passA2(nc, ctx, c, sp1, Wd, Cst, shared1):
    xpd = [sbt(nc, ctx, f"xpd{k}", [128, DTR], BF16) for k in range(NTI1)]
    xpb = [sbt(nc, ctx, f"xpb{k}", [128, NST], BF16) for k in range(NTI1)]
    xpc = [sbt(nc, ctx, f"xpc{k}", [128, NST], BF16) for k in range(NTI1)]
    dtw = sbt(nc, ctx, "dtw1", [64, DIN1], BF16)
    dtb1 = sbt(nc, ctx, "dtb1", [128, 16], F32)
    xcb = sbt(nc, ctx, "a2xcb", [128, NTI1 * TC], BF16)
    dtfm = sbt(nc, ctx, "a2dtfm", [64, TC], BF16)
    stf = [sbt(nc, ctx, f"a2stf{i}", [128, TC], F32) for i in range(2)]
    sp_w = sbt(nc, ctx, "a2spw", [128, TC], F32)
    sp_a = sbt(nc, ctx, "a2spa", [128, TC], F32)
    sp_e = sbt(nc, ctx, "a2spe", [128, TC], F32)
    pB1 = pst(nc, ctx, "a2pB1", [128, TC])
    pB2 = pst(nc, ctx, "a2pB2", [128, TC])
    pIP = [pst(nc, ctx, f"a2pIP{i}", [128, TC]) for i in range(2)]
    pDT = pB2[0:64, :]
    pBC = pB1[0:16, :]
    Bf, Cf = shared1["Bf"], shared1["Cf"]
    for k in range(NTI1):
        c.dma("gpsimd", lambda h, k=k: h.dma_start(
            out=xpd[k][:], in_=Wd["xpdT"][k * 128:(k + 1) * 128, :]))
        c.dma("gpsimd", lambda h, k=k: h.dma_start(
            out=xpb[k][:], in_=Wd["xpbT"][k * 128:(k + 1) * 128, :]))
        c.dma("gpsimd", lambda h, k=k: h.dma_start(
            out=xpc[k][:], in_=Wd["xpcT"][k * 128:(k + 1) * 128, :]))
    c.dma("gpsimd", lambda h: h.dma_start(out=dtw[:], in_=Wd["dtwT"][:, :]))
    c.dma("sync", lambda h: h.dma_start(out=dtb1[:], in_=Wd["dtb1"][:, :]))
    for e in ("scalar", "vector", "tensor"):
        c.dwait(e, "sync")
        c.dwait(e, "gpsimd")
    for tc in range(NCH):
        t0 = tc * TC
        c.dma("gpsimd", lambda h, t0=t0: h.dma_start(
            out=xcb[:, :].rearrange("p (d t) -> p d t", d=NTI1),
            in_=sp1["xc"][:, :, t0:t0 + TC].transpose([1, 0, 2])))
        for e in ("scalar", "vector", "tensor"):
            c.dwait(e, "gpsimd")
            c.dwait(e, "sync")
        def mmdt(h):
            for k in range(NTI1):
                ins = h.matmul(out=pDT, lhsT=xpd[k][:, :],
                               rhs=xcb[:, k * TC:(k + 1) * TC],
                               start=(k == 0), stop=(k == NTI1 - 1))
            return ins
        c.op("tensor", mmdt, tick=True)
        c.op("vector", lambda h: h.tensor_copy(out=dtfm[:, :], in_=pDT),
             tick=True)

        def mmb(h):
            for k in range(NTI1):
                ins = h.matmul(out=pBC, lhsT=xpb[k][:, :],
                               rhs=xcb[:, k * TC:(k + 1) * TC],
                               start=(k == 0), stop=(k == NTI1 - 1))
            return ins
        c.op("tensor", mmb, tick=True)
        c.op("vector", lambda h, t0=t0: h.tensor_copy(
            out=Bf[:, t0:t0 + TC], in_=pBC), tick=True)

        def mmc(h):
            for k in range(NTI1):
                ins = h.matmul(out=pBC, lhsT=xpc[k][:, :],
                               rhs=xcb[:, k * TC:(k + 1) * TC],
                               start=(k == 0), stop=(k == NTI1 - 1))
            return ins
        c.op("tensor", mmc, tick=True)
        c.op("vector", lambda h, t0=t0: h.tensor_copy(
            out=Cf[:, t0:t0 + TC], in_=pBC), tick=True)
        for of in range(NTI1):
            c.op("tensor", lambda h, of=of: h.matmul(
                out=pIP[of % 2][:, :], lhsT=dtw[:, of * 128:(of + 1) * 128],
                rhs=dtfm[:, :], start=True, stop=True), tick=True)
            softplus_chain(nc, c, sp_w[:, :], pIP[of % 2][:, :],
                           dtb1[:, of:of + 1], sp_w, sp_a, sp_e, sp_a, 128)
            c.dwait("vector", "sync")
            stz = stf[of % 2]
            c.op("vector", lambda h, stz=stz: h.tensor_copy(
                out=stz[:, :], in_=sp_w[:, :]), tick=True)
            c.dma("sync", lambda h, of=of, stz=stz, t0=t0: h.dma_start(
                out=sp1["delta"][of, :, t0:t0 + TC], in_=stz[:, :]))
        c.barrier()


def emit_l1passB(nc, ctx, c, sp1, S1ab, Wd, Cst, shared1):
    w1out = [sbt(nc, ctx, f"w1out{k}", [128, D], BF16) for k in range(NTI1)]
    A1c = sbt(nc, ctx, "A1c", [128, NTI1 * NST], F32)
    D1c = sbt(nc, ctx, "D1c", [128, 16], F32)
    zbg = sbt(nc, ctx, "b1zbg", [128, 4 * TC], BF16)
    Bbc = sbt(nc, ctx, "b1Bbc", [128, NST * TC], BF16)
    Cbc = sbt(nc, ctx, "b1Cbc", [128, NST * TC], BF16)
    bflat = sbt(nc, ctx, "b1bfl", [1, 4 * TC], BF16)
    dA = [sbt(nc, ctx, f"b1dA{i}", [128, NGRP * TC], F32) for i in range(2)]
    dBu = sbt(nc, ctx, "b1dBu", [128, NGRP * TC], F32)
    hb = sbt(nc, ctx, "b1h", [128, NGRP * TC], F32)
    delg = sbt(nc, ctx, "b1delg", [128, 4 * TC], F32)
    xcg = sbt(nc, ctx, "b1xcg", [128, 4 * TC], F32)
    du = sbt(nc, ctx, "b1du", [128, TC], F32)
    ycon = sbt(nc, ctx, "b1ycon", [128, TC], F32)
    yg = sbt(nc, ctx, "b1yg", [128, NTI1 * TC], BF16)
    s1t = [sbt(nc, ctx, f"b1s1{i}", [128, TC], F32) for i in range(2)]
    S1st = sbt(nc, ctx, "b1state", [128, 256], F32)
    pB1 = pst(nc, ctx, "b1pB1", [128, TC])
    pIP = [pst(nc, ctx, f"b1pIP{i}", [128, TC]) for i in range(2)]
    Bf, Cf = shared1["Bf"], shared1["Cf"]
    ones_row = Cst["ones_row"]

    for k in range(NTI1):
        c.dma("gpsimd", lambda h, k=k: h.dma_start(
            out=w1out[k][:], in_=Wd["out1T"][k * 128:(k + 1) * 128, :]))
    c.dma("sync", lambda h: h.dma_start(out=A1c[:], in_=Wd["A1col"][:, :]))
    c.dma("sync", lambda h: h.dma_start(out=D1c[:], in_=Wd["D1col"][:, :]))
    for e in ("scalar", "vector", "tensor"):
        c.dwait(e, "sync")
        c.dwait(e, "gpsimd")
    c.op("scalar", lambda h: h.activation(out=A1c[:, :], in_=A1c[:, :],
         func=AF.Exp), tick=True)
    c.op("vector", lambda h: h.tensor_scalar(out=A1c[:, :], in0=A1c[:, :],
         scalar1=-1.0, scalar2=None, op0=ALU.mult), tick=True)

    for tc in range(NCH):
        t0 = tc * TC
        for srcf, dstb in ((Bf, Bbc), (Cf, Cbc)):
            for g4 in range(4):
                c.dma("gpsimd", lambda h, srcf=srcf, g4=g4, t0=t0:
                      h.dma_start(
                          out=bflat[0:1, :].rearrange("a (b t) -> a b t", b=4),
                          in_=srcf[g4 * 4:(g4 + 1) * 4, t0:t0 + TC]))
                c.dwait("tensor", "gpsimd")
                for j in range(4):
                    n = g4 * 4 + j
                    c.op("tensor", lambda h, j=j: h.matmul(
                        out=pB1[:, :], lhsT=Cst["ones_row_bf"][0:1, :],
                        rhs=bflat[0:1, j * TC:(j + 1) * TC],
                        start=True, stop=True), tick=True)
                    c.op("scalar", lambda h, n=n, dstb=dstb: h.copy(
                        out=dstb[:, n * TC:(n + 1) * TC], in_=pB1[:, :]),
                        tick=True)
        for dg in range(4):
            c.dwait("vector", "sync")
            c.dwait("scalar", "sync")
            c.dma("sync", lambda h, dg=dg, t0=t0: h.dma_start(
                out=delg[:, :].rearrange("p (d t) -> p d t", d=4),
                in_=sp1["delta"][dg * 4:(dg + 1) * 4, :, t0:t0 + TC].transpose([1, 0, 2])))
            c.dma("sync", lambda h, dg=dg, t0=t0: h.dma_start(
                out=xcg[:, :].rearrange("p (d t) -> p d t", d=4),
                in_=sp1["xc"][dg * 4:(dg + 1) * 4, :, t0:t0 + TC].transpose([1, 0, 2])))
            c.dma("sync", lambda h, dg=dg, t0=t0: h.dma_start(
                out=zbg[:, :].rearrange("p (d t) -> p d t", d=4),
                in_=sp1["z"][dg * 4:(dg + 1) * 4, :, t0:t0 + TC].transpose([1, 0, 2])))
            for e in ("scalar", "vector"):
                c.dwait(e, "sync")
            def emit_exps(dl, grp, pg, tick):
                dsl = slice(dl * TC, (dl + 1) * TC)
                dtile = dg * 4 + dl
                for j in range(NGRP):
                    n = grp * NGRP + j
                    c.op("scalar", lambda h, dsl=dsl, j=j, dtile=dtile, n=n,
                         pg=pg:
                         h.activation(out=dA[pg][:, j * TC:(j + 1) * TC],
                                      in_=delg[:, dsl], func=AF.Exp,
                                      scale=A1c[:, dtile * NST + n:dtile * NST + n + 1]))
                if tick:
                    c.tick_last("scalar")

            units = [(dl, grp) for dl in range(4) for grp in range(2)]
            emit_exps(0, 0, 0, True)
            for ui, (dl, grp) in enumerate(units):
                dtile = dg * 4 + dl
                dsl = slice(dl * TC, (dl + 1) * TC)
                pg = ui % 2
                if grp == 0:
                    c.op("vector", lambda h, dsl=dsl: h.tensor_tensor(
                        out=du[:, :], in0=delg[:, dsl], in1=xcg[:, dsl],
                        op=ALU.mult), tick=True)
                bsl = slice(grp * NGRP * TC, (grp + 1) * NGRP * TC)
                c.op("vector", lambda h, bsl=bsl: h.tensor_tensor(
                    out=dBu[:, :].rearrange("p (g t) -> p g t", g=NGRP),
                    in0=du[:, :].unsqueeze(1).broadcast_to([128, NGRP, TC]),
                    in1=Bbc[:, bsl].rearrange("p (g t) -> p g t", g=NGRP),
                    op=ALU.mult), tick=True)
                c.fence("vector")
                if ui + 1 < len(units):
                    emit_exps(*units[ui + 1], 1 - pg, False)
                for j in range(NGRP):
                    n = grp * NGRP + j
                    if tc == 0:
                        ini = 0.0
                    else:
                        ini = S1st[:, dtile * NST + n:dtile * NST + n + 1]
                    c.op("vector", lambda h, j=j, ini=ini, pg=pg:
                         h.tensor_tensor_scan(
                             out=hb[:, j * TC:(j + 1) * TC],
                             data0=dA[pg][:, j * TC:(j + 1) * TC],
                             data1=dBu[:, j * TC:(j + 1) * TC],
                             initial=ini, op0=ALU.mult, op1=ALU.add))
                c.tick_last("vector")
                if ui + 1 < len(units):
                    c.tick_last("scalar")
                c.op("vector", lambda h, dtile=dtile, grp=grp:
                     h.tensor_copy(
                         out=S1st[:, dtile * NST + grp * NGRP:
                                  dtile * NST + (grp + 1) * NGRP],
                         in_=hb[:, TC - 1::TC]), tick=True)
                c.op("vector", lambda h, bsl=bsl: h.tensor_tensor(
                    out=hb[:, :].rearrange("p (g t) -> p g t", g=NGRP),
                    in0=hb[:, :].rearrange("p (g t) -> p g t", g=NGRP),
                    in1=Cbc[:, bsl].rearrange("p (g t) -> p g t", g=NGRP),
                    op=ALU.mult), tick=True)
                t4 = dBu
                c.op("vector", lambda h: h.tensor_tensor(
                    out=t4[:, 0:4 * TC], in0=hb[:, 0:4 * TC],
                    in1=hb[:, 4 * TC:8 * TC], op=ALU.add))
                c.op("vector", lambda h: h.tensor_tensor(
                    out=t4[:, 0:2 * TC], in0=t4[:, 0:2 * TC],
                    in1=t4[:, 2 * TC:4 * TC], op=ALU.add))
                if grp == 0:
                    c.op("vector", lambda h: h.tensor_tensor(
                        out=ycon[:, :], in0=t4[:, 0:TC], in1=t4[:, TC:2 * TC],
                        op=ALU.add), tick=True)
                else:
                    c.op("vector", lambda h, t4=t4: h.tensor_tensor(
                        out=t4[:, 2 * TC:3 * TC], in0=t4[:, 0:TC],
                        in1=t4[:, TC:2 * TC], op=ALU.add))
                    c.op("vector", lambda h, t4=t4: h.tensor_tensor(
                        out=ycon[:, :], in0=ycon[:, :],
                        in1=t4[:, 2 * TC:3 * TC], op=ALU.add), tick=True)
                if grp == 1:
                    c.op("vector", lambda h, dsl=dsl, dtile=dtile:
                         h.scalar_tensor_tensor(
                             out=ycon[:, :], in0=xcg[:, dsl],
                             scalar=D1c[:, dtile:dtile + 1], in1=ycon[:, :],
                             op0=ALU.mult, op1=ALU.add))
                    c.op("vector", lambda h, dtile=dtile, dsl=dsl:
                         h.tensor_tensor(
                             out=yg[:, dtile * TC:(dtile + 1) * TC],
                             in0=ycon[:, :], in1=zbg[:, dsl], op=ALU.mult),
                        tick=True)
        for of in range(ND):
            def mmo(h, of=of):
                for k in range(NTI1):
                    ins = h.matmul(out=pIP[of % 2][:, :],
                                   lhsT=w1out[k][:, of * 128:(of + 1) * 128],
                                   rhs=yg[:, k * TC:(k + 1) * TC],
                                   start=(k == 0), stop=(k == NTI1 - 1))
                return ins
            c.op("tensor", mmo, tick=True)
            st = s1t[of % 2]
            c.dwait("vector", "sync")
            c.op("vector", lambda h, of=of, st=st: h.tensor_scalar(
                out=st[:, :], in0=pIP[of % 2][:, :], scalar1=0.5,
                scalar2=None, op0=ALU.mult), tick=True)
            tgt, toff = (S1ab[0], t0) if t0 < TOWN else (S1ab[1], t0 - TOWN)
            c.dma("sync", lambda h, of=of, st=st, tgt=tgt, toff=toff:
                  h.dma_start(out=tgt[of, :, toff:toff + TC], in_=st[:, :]))
        c.barrier()


def emit_moe(nc, ctx, c, sp1, S1a_dram, Wd, Cst, dbg_g=None, dbg_g2=None, dbg_g3=None):
    NHOF = HID1 // 128
    gates = sbt(nc, ctx, "mgates", [8, TOWN], F32)
    gflat = sbt(nc, ctx, "mgflat", [1, TOWN], F32)
    scf = sbt(nc, ctx, "mscf", [8, TC], F32)
    scT = sbt(nc, ctx, "mscT", [128, 8], F32)
    m1 = sbt(nc, ctx, "mm1", [128, 1], F32)
    nm1 = sbt(nc, ctx, "mnm1", [128, 1], F32)
    m2 = sbt(nc, ctx, "mm2", [128, 1], F32)
    Et = sbt(nc, ctx, "mEt", [128, 8], F32)
    eqt = sbt(nc, ctx, "meqt", [128, 8], F32)
    mskd = sbt(nc, ctx, "mmskd", [128, 8], F32)
    ssum = sbt(nc, ctx, "mssum", [128, 1], F32)
    gtT = sbt(nc, ctx, "mgtT", [128, 8], F32)
    W1b = [sbt(nc, ctx, f"mW1{k}", [128, HID1], BF16) for k in range(ND)]
    W3b = [sbt(nc, ctx, f"mW3{k}", [128, HID1], BF16) for k in range(ND)]
    W2b = [sbt(nc, ctx, f"mW2{k}", [128, D], BF16) for k in range(NHOF)]
    A1b = [sbt(nc, ctx, f"mA1{k}", [128, RK], BF16) for k in range(ND)]
    A3b = [sbt(nc, ctx, f"mA3{k}", [128, RK], BF16) for k in range(ND)]
    A2b = [sbt(nc, ctx, f"mA2{k}", [128, RK], BF16) for k in range(NHOF)]
    B1b = sbt(nc, ctx, "mB1", [RK, HID1], BF16)
    B3b = sbt(nc, ctx, "mB3", [RK, HID1], BF16)
    B2b = sbt(nc, ctx, "mB2", [RK, D], BF16)
    b1c = sbt(nc, ctx, "mb1c", [128, NHOF], F32)
    b3c = sbt(nc, ctx, "mb3c", [128, NHOF], F32)
    b2c = sbt(nc, ctx, "mb2c", [128, ND], F32)
    sh1 = sbt(nc, ctx, "msh1", [128, NHOF * TC], BF16)
    gsb = sbt(nc, ctx, "mgsb", [128, NHOF * TC], BF16)
    l1b = sbt(nc, ctx, "ml1", [RK, TC], BF16)
    l3b = sbt(nc, ctx, "ml3", [RK, TC], BF16)
    l2b = sbt(nc, ctx, "ml2", [RK, TC], BF16)
    gb = sbt(nc, ctx, "mgb", [128, TC], F32)
    yw = [sbt(nc, ctx, f"myw{i}", [128, TC], F32) for i in range(2)]
    xnfull = sbt(nc, ctx, "mxnf", [128, ND * TOWN], BF16)
    pT8 = pst(nc, ctx, "mpT8", [128, 8])
    pT8b = pst(nc, ctx, "mpT8b", [8, 128])
    pL = pst(nc, ctx, "mpL", [RK, TC])
    pH = [pst(nc, ctx, f"mpH{i}", [128, TC]) for i in range(2)]
    pGB = pst(nc, ctx, "mpGB", [128, TC])
    ones_row, id128 = Cst["ones_row"], Cst["id128"]

    c.dma("sync", lambda h: h.dma_start(
        out=xnfull[:, :].rearrange("p (d t) -> p d t", d=ND),
        in_=sp1["xn"][:, :, 0:TOWN].transpose([1, 0, 2])))
    for e in ("scalar", "vector", "tensor"):
        c.dwait(e, "sync")
        c.dwait(e, "gpsimd")
    for tco in range(NCHO):
        t0 = tco * TC
        c.dma("sync", lambda h, t0=t0: h.dma_start(
            out=scf[:, :], in_=sp1["scores"][:, t0:t0 + TC]))
        for e_ in ("vector", "tensor"):
            c.dwait(e_, "sync")
        if dbg_g is not None:
            c.dma("sync", lambda h, tco=tco: h.dma_start(
                out=dbg_g[:, tco * TC:(tco + 1) * TC], in_=scf[:, :]))
            c.dwait("vector", "sync")
        for g in range(TC // 128):
            c.op("tensor", lambda h, g=g: h.transpose(
                out=pT8[:, :], in_=scf[:, g * 128:(g + 1) * 128],
                identity=id128[0:8, 0:8]), tick=True)

            def V(fn):
                c.op("vector", fn, tick=True)
                c.fence("vector")
            V(lambda h: h.reduce_max(out=m1[:, :], in_=pT8[:, :],
              axis=mybir.AxisListType.X))
            V(lambda h: h.tensor_scalar(out=mskd[:, :], in0=pT8[:, :],
              scalar1=m1[:, 0:1], scalar2=None, op0=ALU.subtract))
            c.fence("scalar")
            c.op("scalar", lambda h: h.activation(out=Et[:, :], in_=mskd[:, :],
                 func=AF.Exp), tick=True)
            c.fence("vector")
            V(lambda h: h.tensor_scalar(out=eqt[:, :], in0=pT8[:, :],
              scalar1=m1[:, 0:1], scalar2=None, op0=ALU.is_ge))
            V(lambda h: h.scalar_tensor_tensor(out=mskd[:, :],
              in0=eqt[:, :], scalar=-BIG, in1=pT8[:, :], op0=ALU.mult,
              op1=ALU.add))
            V(lambda h: h.reduce_max(out=m2[:, :], in_=mskd[:, :],
              axis=mybir.AxisListType.X))
            V(lambda h: h.tensor_scalar(out=eqt[:, :], in0=pT8[:, :],
              scalar1=m2[:, 0:1], scalar2=None, op0=ALU.is_ge))
            V(lambda h: h.tensor_tensor(out=Et[:, :], in0=Et[:, :],
              in1=eqt[:, :], op=ALU.mult))
            V(lambda h: h.reduce_sum(out=ssum[:, :], in_=Et[:, :],
              axis=mybir.AxisListType.X))
            V(lambda h: h.reciprocal(out=ssum[:, :], in_=ssum[:, :]))
            V(lambda h: h.tensor_scalar(out=gtT[:, :], in0=Et[:, :],
              scalar1=ssum[:, 0:1], scalar2=None, op0=ALU.mult))
            if dbg_g is not None and tco == 0 and g == 0:
                c.dma("sync", lambda h: h.dma_start(out=dbg_g3[0:128, 0:8], in_=mskd[:, :]))
                c.dma("sync", lambda h: h.dma_start(out=dbg_g3[0:128, 8:16], in_=gtT[:, :]))
                c.dma("sync", lambda h: h.dma_start(out=dbg_g3[0:128, 16:24], in_=Et[:, :]))
                c.dma("sync", lambda h: h.dma_start(out=dbg_g3[0:128, 24:25], in_=m1[:, :], allow_slow_non_contiguous=True))
                c.dma("sync", lambda h: h.dma_start(out=dbg_g3[0:128, 26:27], in_=m2[:, :], allow_slow_non_contiguous=True))
                c.dma("sync", lambda h: h.dma_start(out=dbg_g3[0:128, 28:30], in_=eqt[:, 0:2], allow_slow_non_contiguous=True))
                c.dwait("vector", "sync")
                c.dwait("tensor", "sync")
            c.op("tensor", lambda h: h.transpose(
                out=pT8b[:, :], in_=gtT[:, :], identity=id128[:, :]), tick=True)
            c.op("vector", lambda h, t0=t0, g=g: h.tensor_copy(
                out=gates[:, t0 + g * 128:t0 + (g + 1) * 128],
                in_=pT8b[:, :]), tick=True)
    if dbg_g is not None:
        c.dma("sync", lambda h: h.dma_start(out=dbg_g2[:, :], in_=gates[:, :]))
        c.dwait("vector", "sync")
    def batchA(e):
        for k in range(ND):
            c.dma("gpsimd", lambda h, e=e, k=k: h.dma_start(
                out=W1b[k][:], in_=Wd["W1T"][e, k * 128:(k + 1) * 128, :]))
            c.dma("gpsimd", lambda h, e=e, k=k: h.dma_start(
                out=W3b[k][:], in_=Wd["W3T"][e, k * 128:(k + 1) * 128, :]))
            c.dma("gpsimd", lambda h, e=e, k=k: h.dma_start(
                out=A1b[k][:], in_=Wd["A1T"][e, k * 128:(k + 1) * 128, :]))
            c.dma("gpsimd", lambda h, e=e, k=k: h.dma_start(
                out=A3b[k][:], in_=Wd["A3T"][e, k * 128:(k + 1) * 128, :]))
        c.dma("gpsimd", lambda h, e=e: h.dma_start(
            out=B1b[:], in_=Wd["B1T"][e, :, :]))
        c.dma("gpsimd", lambda h, e=e: h.dma_start(
            out=B3b[:], in_=Wd["B3T"][e, :, :]))
        c.dma("sync", lambda h, e=e: h.dma_start(
            out=b1c[:], in_=Wd["b1c"][e, :, :]))
        c.dma("sync", lambda h, e=e: h.dma_start(
            out=b3c[:], in_=Wd["b3c"][e, :, :]))
        c.dma("sync", lambda h, e=e: h.dma_start(
            out=gflat[:], in_=gates[e:e + 1, :]))

    def batchB(e):
        for k in range(NHOF):
            c.dma("gpsimd", lambda h, e=e, k=k: h.dma_start(
                out=W2b[k][:], in_=Wd["W2T"][e, k * 128:(k + 1) * 128, :]))
            c.dma("gpsimd", lambda h, e=e, k=k: h.dma_start(
                out=A2b[k][:], in_=Wd["A2T"][e, k * 128:(k + 1) * 128, :]))
        c.dma("gpsimd", lambda h, e=e: h.dma_start(
            out=B2b[:], in_=Wd["B2T"][e, :, :]))
        c.dma("sync", lambda h, e=e: h.dma_start(
            out=b2c[:], in_=Wd["b2c"][e, :, :]))
        return (c.mark("sync"), c.mark("gpsimd"))

    batchA(0)
    bmkB = None
    for e in range(NE):
        for eng in ("scalar", "vector", "tensor"):
            c.dwait(eng, "sync")
            c.dwait(eng, "gpsimd")
        for tco in range(NCHO):
            t0 = tco * TC
            xsl = [slice(k * TOWN + t0, k * TOWN + t0 + TC) for k in range(ND)]
            c.op("tensor", lambda h, t0=t0: h.matmul(
                out=pGB[:, :], lhsT=ones_row[0:1, :],
                rhs=gflat[0:1, t0:t0 + TC], start=True, stop=True), tick=True)
            c.op("scalar", lambda h: h.copy(out=gb[:, :], in_=pGB[:, :]),
                 tick=True)
            def mml1(h, xsl=tuple(xsl)):
                for k in range(ND):
                    ins = h.matmul(out=pL[:, :], lhsT=A1b[k][:, :],
                                   rhs=xnfull[:, xsl[k]],
                                   start=(k == 0), stop=(k == ND - 1))
                return ins
            c.op("tensor", mml1, tick=True)
            c.op("vector", lambda h: h.tensor_copy(out=l1b[:, :], in_=pL[:, :]),
                 tick=True)

            def mml3(h, xsl=tuple(xsl)):
                for k in range(ND):
                    ins = h.matmul(out=pL[:, :], lhsT=A3b[k][:, :],
                                   rhs=xnfull[:, xsl[k]],
                                   start=(k == 0), stop=(k == ND - 1))
                return ins
            c.op("tensor", mml3, tick=True)
            c.op("vector", lambda h: h.tensor_copy(out=l3b[:, :], in_=pL[:, :]),
                 tick=True)
            for of in range(NHOF):
                def mmh1(h, of=of, xsl=tuple(xsl)):
                    for k in range(ND):
                        h.matmul(out=pH[0][:, :],
                                 lhsT=W1b[k][:, of * 128:(of + 1) * 128],
                                 rhs=xnfull[:, xsl[k]],
                                 start=(k == 0), stop=False)
                    return h.matmul(out=pH[0][:, :],
                                    lhsT=B1b[:, of * 128:(of + 1) * 128],
                                    rhs=l1b[:, :], start=False, stop=True)
                c.op("tensor", mmh1, tick=True)
                if of == 0 and tco == 0:
                    bmkB = batchB(e)
                c.op("scalar", lambda h, of=of: h.activation(
                    out=sh1[:, of * TC:(of + 1) * TC], in_=pH[0][:, :],
                    func=AF.Silu, bias=b1c[:, of:of + 1]), tick=True)

                def mmh3(h, of=of, xsl=tuple(xsl)):
                    for k in range(ND):
                        h.matmul(out=pH[1][:, :],
                                 lhsT=W3b[k][:, of * 128:(of + 1) * 128],
                                 rhs=xnfull[:, xsl[k]],
                                 start=(k == 0), stop=False)
                    return h.matmul(out=pH[1][:, :],
                                    lhsT=B3b[:, of * 128:(of + 1) * 128],
                                    rhs=l3b[:, :], start=False, stop=True)
                c.op("tensor", mmh3, tick=True)
                c.op("vector", lambda h, of=of: h.scalar_tensor_tensor(
                    out=gsb[:, of * TC:(of + 1) * TC], in0=pH[1][:, :],
                    scalar=b3c[:, of:of + 1],
                    in1=sh1[:, of * TC:(of + 1) * TC],
                    op0=ALU.add, op1=ALU.mult), tick=True)
            def mml2(h):
                for k in range(NHOF):
                    ins = h.matmul(out=pL[:, :], lhsT=A2b[k][:, :],
                                   rhs=gsb[:, k * TC:(k + 1) * TC],
                                   start=(k == 0), stop=(k == NHOF - 1))
                return ins
            c.op("tensor", mml2, tick=True)
            c.op("vector", lambda h: h.tensor_copy(out=l2b[:, :], in_=pL[:, :]),
                 tick=True)
            c.dwait_cnt("tensor", "sync", bmkB[0])
            c.dwait_cnt("tensor", "gpsimd", bmkB[1])
            c.dwait_cnt("vector", "sync", bmkB[0])
            for of2 in range(ND):
                def mmy(h, of2=of2):
                    for k in range(NHOF):
                        h.matmul(out=pH[of2 % 2][:, :],
                                 lhsT=W2b[k][:, of2 * 128:(of2 + 1) * 128],
                                 rhs=gsb[:, k * TC:(k + 1) * TC],
                                 start=(k == 0), stop=False)
                    return h.matmul(out=pH[of2 % 2][:, :],
                                    lhsT=B2b[:, of2 * 128:(of2 + 1) * 128],
                                    rhs=l2b[:, :], start=False, stop=True)
                c.op("tensor", mmy, tick=True)
                if of2 == 0 and tco == NCHO - 1 and e + 1 < NE:
                    batchA(e + 1)
                st = yw[of2 % 2]
                c.dwait("vector", "gpsimd")
                c.op("vector", lambda h, of2=of2, st=st: h.scalar_tensor_tensor(
                    out=st[:, :], in0=pH[of2 % 2][:, :],
                    scalar=b2c[:, of2:of2 + 1], in1=gb[:, :],
                    op0=ALU.add, op1=ALU.mult), tick=True)
                c.dma("gpsimd", lambda h, of2=of2, st=st, t0=t0: h.dma_start(
                    out=S1a_dram[of2, :, t0:t0 + TC], in_=st[:, :],
                    accum_op=ALU.add))


def declare_l1_scratch(nc, dbg=False):
    kind = "ExternalOutput" if dbg else "Internal"
    return {
        "xn": nc.dram_tensor("sp1_xn", [ND, 128, T], BF16, kind=kind).ap(),
        "z": nc.dram_tensor("sp1_z", [16, 128, T], BF16, kind=kind).ap(),
        "xc": nc.dram_tensor("sp1_xc", [16, 128, T], F32, kind=kind).ap(),
        "delta": nc.dram_tensor("sp1_dl", [16, 128, T], F32, kind=kind).ap(),
        "scores": nc.dram_tensor("sp1_sc", [8, TOWN], F32, kind=kind).ap(),
    }


def emit_l1(nc, ctx, c, x1_dram, S1ab, Wd, Cst, parts="abm", dbg=False):
    sp1 = declare_l1_scratch(nc, dbg)
    with ExitStack() as mctx:
        shared1 = {
            "Bf": sbt(nc, mctx, "l1Bf", [NST, T], BF16),
            "Cf": sbt(nc, mctx, "l1Cf", [NST, T], BF16),
        }
        with ExitStack() as actx:
            emit_l1passA(nc, actx, c, x1_dram, sp1, Wd, Cst, shared1)
            c.barrier()
        with ExitStack() as a2ctx:
            emit_l1passA2(nc, a2ctx, c, sp1, Wd, Cst, shared1)
            c.barrier()
        if "b" in parts:
            with ExitStack() as bctx:
                emit_l1passB(nc, bctx, c, sp1, S1ab, Wd, Cst, shared1)
                c.barrier()
    if "m" in parts:
        dbg_g = (nc.dram_tensor("dbg_g", [8, TOWN], F32,
                                kind="ExternalOutput").ap() if dbg else None)
        dbg_g2 = (nc.dram_tensor("dbg_g2", [8, TOWN], F32,
                                 kind="ExternalOutput").ap() if dbg else None)
        dbg_g3 = (nc.dram_tensor("dbg_g3", [128, 32], F32,
                                 kind="ExternalOutput").ap() if dbg else None)
        with ExitStack() as mctx2:
            emit_moe(nc, mctx2, c, sp1, S1ab[0], Wd, Cst, dbg_g, dbg_g2,
                     dbg_g3)
            c.barrier()


def prep_l1_weights(inp, rev):
    d = "b" if rev else "f"
    W = {}
    W["in1T"] = np.ascontiguousarray(inp[f"L1_attn_{d}_in_w"].T)
    xp = inp[f"L1_attn_{d}_xproj_w"]
    W["xpdT"] = np.ascontiguousarray(xp[:DTR].T)
    W["xpbT"] = np.ascontiguousarray(xp[DTR:DTR + NST].T)
    W["xpcT"] = np.ascontiguousarray(xp[DTR + NST:].T)
    W["dtwT"] = np.ascontiguousarray(inp[f"L1_attn_{d}_dt_w"].T)
    cw = inp[f"L1_attn_{d}_conv_w"]
    W["conv1w"] = np.ascontiguousarray(
        cw.reshape(16, 128, 4).transpose(1, 0, 2).reshape(128, 64))
    W["conv1b"] = col_form(inp[f"L1_attn_{d}_conv_b"])
    W["dtb1"] = col_form(inp[f"L1_attn_{d}_dt_b"])
    W["lngr"] = np.asarray(inp["L1_attn_ln_g"], np.float32).reshape(1, D)
    W["lnbr"] = np.asarray(inp["L1_attn_ln_b"], np.float32).reshape(1, D)
    W["lngc"] = col_form(inp["L1_attn_ln_g"])
    W["out1T"] = np.ascontiguousarray(inp[f"L1_attn_{d}_out_w"].T)
    al = np.asarray(inp[f"L1_attn_{d}_A_log"], np.float32)
    W["A1col"] = np.ascontiguousarray(
        al.reshape(16, 128, NST).transpose(1, 0, 2).reshape(128, 16 * NST))
    W["D1col"] = col_form(inp[f"L1_attn_{d}_D_skip"])
    W["WgT"] = np.ascontiguousarray(
        inp["L1_ffn_Wg"].T.reshape(ND, 128, NE).reshape(128 * ND, NE)
    ).reshape(D, NE)
    W["WgT"] = np.ascontiguousarray(
        inp["L1_ffn_Wg"].T.reshape(ND, 128, NE).transpose(1, 0, 2).reshape(128, ND * NE))
    W["W1T"] = np.ascontiguousarray(np.transpose(inp["L1_ffn_W1"], (0, 2, 1)))
    W["W3T"] = np.ascontiguousarray(np.transpose(inp["L1_ffn_W3"], (0, 2, 1)))
    W["W2T"] = np.ascontiguousarray(np.transpose(inp["L1_ffn_W2"], (0, 2, 1)))
    W["A1T"] = np.ascontiguousarray(np.transpose(inp["L1_ffn_A1"], (0, 2, 1)))
    W["A3T"] = np.ascontiguousarray(np.transpose(inp["L1_ffn_A3"], (0, 2, 1)))
    W["A2T"] = np.ascontiguousarray(np.transpose(inp["L1_ffn_A2"], (0, 2, 1)))
    W["B1T"] = np.ascontiguousarray(np.transpose(inp["L1_ffn_B1"], (0, 2, 1)))
    W["B3T"] = np.ascontiguousarray(np.transpose(inp["L1_ffn_B3"], (0, 2, 1)))
    W["B2T"] = np.ascontiguousarray(np.transpose(inp["L1_ffn_B2"], (0, 2, 1)))
    W["b1c"] = np.ascontiguousarray(
        inp["L1_ffn_b1"].reshape(NE, 16, 128).transpose(0, 2, 1))
    W["b3c"] = np.ascontiguousarray(
        inp["L1_ffn_b3"].reshape(NE, 16, 128).transpose(0, 2, 1))
    W["b2c"] = np.ascontiguousarray(
        inp["L1_ffn_b2"].reshape(NE, ND, 128).transpose(0, 2, 1))
    return W


L1W_SHAPES = [
    ("in1T", [D, 2 * DIN1], F32), ("xpdT", [DIN1, DTR], F32),
    ("xpbT", [DIN1, NST], F32), ("xpcT", [DIN1, NST], F32),
    ("dtwT", [DTR, DIN1], F32), ("conv1w", [128, 64], F32),
    ("conv1b", [128, 16], F32), ("dtb1", [128, 16], F32),
    ("lngr", [1, D], F32), ("lnbr", [1, D], F32), ("lngc", [128, ND], F32),
    ("out1T", [DIN1, D], F32), ("A1col", [128, 16 * NST], F32),
    ("D1col", [128, 16], F32), ("WgT", [128, ND * NE], F32),
    ("W1T", [NE, D, HID1], F32), ("W3T", [NE, D, HID1], F32),
    ("W2T", [NE, HID1, D], F32), ("A1T", [NE, D, RK], F32),
    ("A3T", [NE, D, RK], F32), ("A2T", [NE, HID1, RK], F32),
    ("B1T", [NE, RK, HID1], F32), ("B3T", [NE, RK, HID1], F32),
    ("B2T", [NE, RK, D], F32), ("b1c", [NE, 128, 16], F32),
    ("b3c", [NE, 128, 16], F32), ("b2c", [NE, 128, ND], F32),
]


def emit_final(nc, ctx, c, x1_dram, Sown, Sprev, out_dram, masks=None, G=None):
    xt = sbt(nc, ctx, "fnx", [128, TC], F32)
    at = sbt(nc, ctx, "fna", [128, TC], F32)
    bt = sbt(nc, ctx, "fnb", [128, TC], F32)
    g0 = sbt(nc, ctx, "fng0", [128, TC], F32)
    g1t = sbt(nc, ctx, "fng1", [128, TC], F32)
    ot = [sbt(nc, ctx, f"fno{i}", [128, TC], F32) for i in range(2)]
    for tco in range(NCHO):
        t0 = tco * TC
        tm = (NCH - 1 - tco) * TC - TOWN
        for d in range(ND):
            c.dwait("vector", "sync")
            c.dma("sync", lambda h, d=d, t0=t0: h.dma_start(
                out=xt[:, :], in_=x1_dram[d, :, t0:t0 + TC]))
            c.dma("sync", lambda h, d=d, t0=t0: h.dma_start(
                out=at[:, :], in_=Sown[d, :, t0:t0 + TC]))
            if masks is None:
                c.dma("sync", lambda h, d=d, t0=t0: h.dma_start(
                    out=bt[:, :], in_=Sprev[d, :, t0:t0 + TC]))
            else:
                c.dma("sync", lambda h, d=d, tm=tm: h.dma_start(
                    out=g0[:, :], in_=G[0][d, :, tm:tm + TC]))
                c.dma("sync", lambda h, d=d, tm=tm: h.dma_start(
                    out=g1t[:, :], in_=G[1][d, :, tm:tm + TC]))
            c.dwait("vector", "sync")
            o = ot[d % 2]
            c.op("vector", lambda h, o=o: h.tensor_tensor(
                out=o[:, :], in0=xt[:, :], in1=at[:, :], op=ALU.add))
            if masks is None:
                c.op("vector", lambda h, o=o: h.tensor_tensor(
                    out=o[:, :], in0=o[:, :], in1=bt[:, :], op=ALU.add),
                    tick=True)
            else:
                c.op("vector", lambda h, o=o: h.scalar_tensor_tensor(
                    out=o[:, :], in0=g0[:, ::-1], scalar=masks[:, 0:1],
                    in1=o[:, :], op0=ALU.mult, op1=ALU.add))
                c.op("vector", lambda h, o=o: h.scalar_tensor_tensor(
                    out=o[:, :], in0=g1t[:, ::-1], scalar=masks[:, 1:2],
                    in1=o[:, :], op0=ALU.mult, op1=ALU.add), tick=True)
            c.dma("sync", lambda h, d=d, t0=t0, o=o: h.dma_start(
                out=out_dram[d, :, t0:t0 + TC], in_=o[:, :]))
    c.barrier()


def build_l1(parts="abm", dbg=False):
    nc = bass.Bass("TRN2", target_bir_lowering=False)
    x_dram = nc.dram_tensor("x", [ND, 128, T], F32, kind="ExternalInput").ap()
    S0o = nc.dram_tensor("S0own", [ND, 128, T], F32, kind="ExternalInput").ap()
    S0p = nc.dram_tensor("S0prev", [ND, 128, T], F32, kind="ExternalInput").ap()
    x1_dram = nc.dram_tensor("x1", [ND, 128, T], F32, kind="ExternalOutput").ap()
    S1a_d = nc.dram_tensor("S1a", [ND, 128, TOWN], F32,
                           kind="ExternalOutput").ap()
    S1b_d = nc.dram_tensor("S1b", [ND, 128, TOWN], F32,
                           kind="ExternalOutput").ap()
    Wd = {}
    for nm, shape, dt in L1W_SHAPES:
        Wd[nm] = nc.dram_tensor("w_" + nm, shape, dt, kind="ExternalInput").ap()
    with ExitStack() as ctx:
        c = Ctl(nc, ctx)
        Cst = load_consts(nc, ctx, c)
        with ExitStack() as cctx:
            emit_combine(nc, cctx, c, x_dram, S0o, S0p, x1_dram)
        emit_l1(nc, ctx, c, x1_dram, (S1a_d, S1b_d), Wd, Cst, parts, dbg)
        with nc.Block() as block:
            c.emit(block)
    return nc


def build_fin():
    nc = bass.Bass("TRN2", target_bir_lowering=False)
    x1_dram = nc.dram_tensor("x1", [ND, 128, T], F32, kind="ExternalInput").ap()
    S1o = nc.dram_tensor("S1own", [ND, 128, T], F32, kind="ExternalInput").ap()
    S1p = nc.dram_tensor("S1prev", [ND, 128, T], F32, kind="ExternalInput").ap()
    out_dram = nc.dram_tensor("out", [ND, 128, TOWN], F32,
                              kind="ExternalOutput").ap()
    with ExitStack() as ctx:
        c = Ctl(nc, ctx)
        with ExitStack() as fctx:
            emit_final(nc, fctx, c, x1_dram, S1o, S1p, out_dram)
        with nc.Block() as block:
            c.emit(block)
    return nc


def _host_inputs_common(inputs, c):
    rev = c % 2
    m = {}
    m.update(host_consts())
    W0 = prep_l0_weights(inputs, rev)
    for nm, shape, dt in L0W_SHAPES:
        m["w_" + nm] = W0[nm]
    return m


def kernel_unfused(inputs):
    x = np.asarray(inputs["L0_attn_x"])
    B = x.shape[0]
    ncores = 8
    xs = []
    for c in range(ncores):
        b, rev = c // 2, c % 2
        xb = np.ascontiguousarray(x[b].T) if not rev else \
            np.ascontiguousarray(x[b].T[:, ::-1])
        xs.append(xb.reshape(ND, 128, T))
    cst = host_consts()
    nc0 = build_l0()
    maps0 = []
    for c in range(ncores):
        W0 = prep_l0_weights(inputs, c % 2)
        m = {"x": xs[c]}
        for nm, shape, dt in L0W_SHAPES:
            m["w_" + nm] = W0[nm]
        m.update(cst)
        maps0.append(m)
    res0 = run_bass_kernel_spmd(nc0, maps0, core_ids=list(range(ncores)))
    S0 = [np.asarray(res0.results[c]["S0"]) for c in range(ncores)]
    nc1 = build_l1()
    maps1 = []
    for c in range(ncores):
        part = c ^ 1
        sp = np.ascontiguousarray(
            S0[part].reshape(D, T)[:, ::-1]).reshape(ND, 128, T)
        W1 = prep_l1_weights(inputs, c % 2)
        m = {"x": xs[c], "S0own": S0[c], "S0prev": sp}
        for nm, shape, dt in L1W_SHAPES:
            m["w_" + nm] = W1[nm]
        m.update(cst)
        maps1.append(m)
    res1 = run_bass_kernel_spmd(nc1, maps1, core_ids=list(range(ncores)))
    x1 = [np.asarray(res1.results[c]["x1"]) for c in range(ncores)]
    S1 = [np.concatenate([np.asarray(res1.results[c]["S1a"]),
                          np.asarray(res1.results[c]["S1b"])], axis=2)
          for c in range(ncores)]
    nc2 = build_fin()
    maps2 = []
    for c in range(ncores):
        part = c ^ 1
        sp = np.ascontiguousarray(
            S1[part].reshape(D, T)[:, ::-1]).reshape(ND, 128, T)
        maps2.append({"x1": x1[c], "S1own": S1[c], "S1prev": sp})
    res2 = run_bass_kernel_spmd(nc2, maps2, core_ids=list(range(ncores)))
    outs = [np.asarray(res2.results[c]["out"]).reshape(D, TOWN)
            for c in range(ncores)]
    full = np.empty((B, T, D), np.float32)
    for b in range(B):
        full[b, :TOWN] = outs[2 * b].T
        full[b, TOWN:] = outs[2 * b + 1].T[::-1]
    return full


def build_full():
    nc = bass.Bass("TRN2", target_bir_lowering=False)
    x_dram = nc.dram_tensor("x", [ND, 128, T], F32, kind="ExternalInput").ap()
    pmask = nc.dram_tensor("pmask", [128, 2], F32, kind="ExternalInput").ap()
    out_dram = nc.dram_tensor("out", [ND, 128, TOWN], F32,
                              kind="ExternalOutput").ap()
    S0 = nc.dram_tensor("S0i", [ND, 128, T], F32, kind="Internal").ap()
    S1a = nc.dram_tensor("S1ai", [ND, 128, TOWN], F32, kind="Internal").ap()
    S1b = nc.dram_tensor("S1bi", [ND, 128, TOWN], F32, kind="Internal").ap()
    x1 = nc.dram_tensor("x1i", [ND, 128, T], F32, kind="Internal").ap()
    G0 = nc.dram_tensor("G0i", [2 * ND, 128, T], F32, kind="Internal").ap()
    G1 = nc.dram_tensor("G1i", [2 * ND, 128, TOWN], F32,
                        kind="Internal").ap()
    Wd = {}
    for nm, shape, dt in L0W_SHAPES + L1W_SHAPES:
        Wd[nm] = nc.dram_tensor("w_" + nm, shape, dt, kind="ExternalInput").ap()
    groups = [[0, 1], [2, 3], [4, 5], [6, 7]]
    with ExitStack() as ctx:
        c = Ctl(nc, ctx)
        Cst = load_consts(nc, ctx, c)
        pm = sbt(nc, ctx, "pmt", [128, 2], F32)
        c.dma("sync", lambda h: h.dma_start(out=pm[:], in_=pmask[:, :]))
        emit_l0(nc, ctx, c, x_dram, S0, Wd, Cst)
        c.barrier()
        c.op("gpsimd", lambda h: h.collective_compute(
            "AllGather", ALU.bypass, replica_groups=groups,
            ins=[S0.opt()], outs=[G0.opt()]), tick=True)
        c.barrier()
        with ExitStack() as cctx:
            emit_combine(nc, cctx, c, x_dram, S0, None, x1, masks=pm,
                         G=(G0[0:ND], G0[ND:2 * ND]))
        emit_l1(nc, ctx, c, x1, (S1a, S1b), Wd, Cst)
        c.barrier()
        c.op("gpsimd", lambda h: h.collective_compute(
            "AllGather", ALU.bypass, replica_groups=groups,
            ins=[S1b.opt()], outs=[G1.opt()]), tick=True)
        c.barrier()
        with ExitStack() as fctx:
            emit_final(nc, fctx, c, x1, S1a, None, out_dram, masks=pm,
                       G=(G1[0:ND], G1[ND:2 * ND]))
        c.barrier()
        with nc.Block() as block:
            c.emit(block)
    return nc


def kernel_fused(inputs):
    x = np.asarray(inputs["L0_attn_x"])
    B = x.shape[0]
    ncores = 8
    nc = build_full()
    cst = host_consts()
    maps = []
    for c in range(ncores):
        b, rev = c // 2, c % 2
        xb = np.ascontiguousarray(x[b].T) if not rev else \
            np.ascontiguousarray(x[b].T[:, ::-1])
        m = {"x": xb.reshape(ND, 128, T)}
        m["pmask"] = np.ascontiguousarray(np.broadcast_to(
            np.array([[1.0 if rev == 1 else 0.0,
                       1.0 if rev == 0 else 0.0]], np.float32), (128, 2)))
        W0 = prep_l0_weights(inputs, rev)
        for nm, shape, dt in L0W_SHAPES:
            m["w_" + nm] = W0[nm]
        W1 = prep_l1_weights(inputs, rev)
        for nm, shape, dt in L1W_SHAPES:
            m["w_" + nm] = W1[nm]
        m.update(cst)
        maps.append(m)
    res = run_bass_kernel_spmd(nc, maps, core_ids=list(range(ncores)))
    outs = [np.asarray(res.results[c]["out"]).reshape(D, TOWN)
            for c in range(ncores)]
    full = np.empty((B, T, D), np.float32)
    for b in range(B):
        full[b, :TOWN] = outs[2 * b].T
        full[b, TOWN:] = outs[2 * b + 1].T[::-1]
    return full


def kernel(**inputs):
    return kernel_fused(inputs)


# version 24
# speedup vs baseline: 1.0127x; 1.0007x over previous
import numpy as np
import ml_dtypes
from contextlib import ExitStack
import concourse.bass as bass
import concourse.mybir as mybir
from concourse.bass_utils import run_bass_kernel_spmd

F32 = mybir.dt.float32
BF16 = mybir.dt.bfloat16
I32 = mybir.dt.int32
AF = mybir.ActivationFunctionType
ALU = mybir.AluOpType

T = 4096
TC = 512
NCH = T // TC
LC = 128
NLC = TC // LC
D = 1024
ND = D // 128
BIG = 1.0e9
DIN0 = 2048
NH = 32
HD = 64
DST = 128
CONVD = 2304
NTI0 = CONVD // 128
HID0 = 3072
QS = 1.0 / 127.0
DIN1 = 2048
NST = 16
DTR = 64
NTI1 = DIN1 // 128
NGRP = 8
NE = 8
HID1 = 2048
RK = 16
TOWN = 2048
NCHO = TOWN // TC

ENGS = ("sync", "vector", "scalar", "tensor", "gpsimd")


class Ctl:

    def __init__(self, nc, ctx):
        self.nc = nc
        self.ops = {e: [] for e in ENGS}
        self.s_tick = ctx.enter_context(nc.semaphore("tick"))
        self.ntick = 0
        self.last_wait = {e: 0 for e in ENGS}
        self.dsem = {}
        self.dcnt = {}
        for q in ("sync", "gpsimd", "scalar"):
            self.dsem[q] = ctx.enter_context(nc.semaphore(f"dma_{q}"))
            self.dcnt[q] = 0
        self.last_dwait = {e: {q: 0 for q in self.dsem} for e in ENGS}

    def op(self, eng, fn, tick=False):
        wv = self.ntick if self.ntick > self.last_wait[eng] else 0
        if wv:
            self.last_wait[eng] = wv
        ent = {"kind": "op", "fn": fn, "wv": wv, "tickid": None, "dwv": None}
        if tick:
            self.ntick += 1
            self.last_wait[eng] = self.ntick
            ent["tickid"] = self.ntick
        self.ops[eng].append(ent)

    def tick_last(self, eng):
        lst = self.ops[eng]
        assert lst and lst[-1]["kind"] == "op" and lst[-1]["tickid"] is None
        self.ntick += 1
        self.last_wait[eng] = self.ntick
        lst[-1]["tickid"] = self.ntick

    def fence(self, eng):
        self.ops[eng].append({"kind": "wait", "wv": self.ntick})
        self.last_wait[eng] = self.ntick

    def dma(self, q, fn):
        self.dcnt[q] += 16
        wv = self.ntick if self.ntick > self.last_wait[q] else 0
        if wv:
            self.last_wait[q] = wv
        self.ops[q].append({"kind": "dma", "fn": fn, "wv": wv,
                            "tickid": None, "dwv": None, "q": q})

    def dwait(self, eng, q):
        cnt = self.dcnt[q]
        if cnt <= self.last_dwait[eng][q]:
            return
        self.last_dwait[eng][q] = cnt
        self.ops[eng].append({"kind": "dwait", "q": q, "cnt": cnt})

    def mark(self, q):
        return self.dcnt[q]

    def dwait_cnt(self, eng, q, cnt):
        if cnt <= self.last_dwait[eng][q]:
            return
        self.last_dwait[eng][q] = cnt
        self.ops[eng].append({"kind": "dwait", "q": q, "cnt": cnt})

    def barrier(self, dmas=True):
        if dmas:
            for e in ENGS:
                for q in self.dsem:
                    self.dwait(e, q)
        for e in ENGS:
            lst = self.ops[e]
            j = len(lst) - 1
            while j >= 0 and lst[j]["kind"] == "dwait":
                j -= 1
            if j >= 0 and lst[j]["kind"] == "op" and lst[j]["tickid"] is None:
                self.ntick += 1
                lst[j]["tickid"] = self.ntick
                self.last_wait[e] = self.ntick
            else:
                self.op(e, lambda h: h.nop(), tick=True)
        wv = self.ntick
        for e in ENGS:
            self.last_wait[e] = wv
            self.ops[e].append({"kind": "wait", "wv": wv})

    def emit(self, block):
        tsem = self.s_tick
        dsem = self.dsem
        for name in ENGS:
            oplist = self.ops[name]

            def body(h, oplist=oplist):
                for ent in oplist:
                    k = ent["kind"]
                    if k == "dwait":
                        h.wait_ge(dsem[ent["q"]], ent["cnt"])
                    elif k == "wait":
                        h.wait_ge(tsem, ent["wv"])
                    elif k == "dma":
                        if ent["wv"]:
                            h.wait_ge(tsem, ent["wv"])
                        ent["fn"](h).then_inc(dsem[ent["q"]], 16)
                    else:
                        if ent["wv"]:
                            h.wait_ge(tsem, ent["wv"])
                        ins = ent["fn"](h)
                        if ent["tickid"] is not None:
                            ins.then_inc(tsem, 1)
            getattr(block, name)(body)
        assert self.ntick < 60000, f"tick overflow {self.ntick}"
        for q, cval in self.dcnt.items():
            assert cval < 60000, f"dma sem overflow {q}={cval}"


def sbt(nc, ctx, name, shape, dtype):
    return ctx.enter_context(nc.sbuf_tensor(name, shape, dtype))


def pst(nc, ctx, name, shape, dtype=F32):
    return ctx.enter_context(nc.psum_tensor(name, shape, dtype))


def emit_ln_stats(nc, c, xt, sqt, ones_col, p_s, p_q, r, eps2=None, W=TC):
    c.op("scalar", lambda h: h.activation(out=sqt[:, 0:ND * W], in_=xt[:, 0:ND * W],
                                          func=AF.Square), tick=True)

    def mm_s(h):
        for d in range(ND):
            ins = h.matmul(out=p_s[:, 0:W], lhsT=ones_col[:, 0:1],
                           rhs=xt[:, d * W:(d + 1) * W],
                           start=(d == 0), stop=(d == ND - 1))
        return ins
    c.op("tensor", mm_s, tick=True)

    def mm_q(h):
        for d in range(ND):
            ins = h.matmul(out=p_q[:, 0:W], lhsT=ones_col[:, 0:1],
                           rhs=sqt[:, d * W:(d + 1) * W],
                           start=(d == 0), stop=(d == ND - 1))
        return ins
    c.op("tensor", mm_q, tick=True)
    c.op("vector", lambda h: h.tensor_scalar(out=r["m"][:, 0:W], in0=p_s[0:1, :],
         scalar1=1.0 / D, scalar2=None, op0=ALU.mult))
    c.op("vector", lambda h: h.tensor_scalar(out=r["e2"][:, 0:W], in0=p_q[0:1, :],
         scalar1=1.0 / D, scalar2=None, op0=ALU.mult))
    c.op("vector", lambda h: h.tensor_tensor(out=r["m2"][:, 0:W], in0=r["m"][:, 0:W],
         in1=r["m"][:, 0:W], op=ALU.mult))
    c.op("vector", lambda h: h.tensor_tensor(out=r["var"][:, 0:W], in0=r["e2"][:, 0:W],
         in1=r["m2"][:, 0:W], op=ALU.subtract))
    c.op("vector", lambda h: h.tensor_scalar(out=r["vp"][:, 0:W], in0=r["var"][:, 0:W],
         scalar1=1e-5, scalar2=None, op0=ALU.add))
    c.op("vector", lambda h: h.reciprocal(out=r["rv"][:, 0:W], in_=r["vp"][:, 0:W]),
         tick=(eps2 is None))
    if eps2 is not None:
        c.op("vector", lambda h: h.tensor_tensor(out=r["q2"][:, 0:W], in0=r["var"][:, 0:W],
             in1=r["rv"][:, 0:W], op=ALU.mult))
        c.op("vector", lambda h, e=eps2: h.tensor_scalar(out=r["qe"][:, 0:W],
             in0=r["q2"][:, 0:W], scalar1=e, scalar2=None, op0=ALU.add))
        c.op("vector", lambda h: h.reciprocal(out=r["rq"][:, 0:W], in_=r["qe"][:, 0:W]),
             tick=True)
        c.op("scalar", lambda h: h.sqrt(out=r["g2"][:, 0:W], in_=r["rq"][:, 0:W]))
    c.op("scalar", lambda h: h.sqrt(out=r["g1"][:, 0:W], in_=r["rv"][:, 0:W]),
         tick=True)
    c.op("vector", lambda h: h.tensor_tensor(out=r["mg1"][:, 0:W], in0=r["m"][:, 0:W],
         in1=r["g1"][:, 0:W], op=ALU.mult), tick=(eps2 is None))
    if eps2 is not None:
        c.op("vector", lambda h: h.tensor_tensor(out=r["g12"][:, 0:W], in0=r["g1"][:, 0:W],
             in1=r["g2"][:, 0:W], op=ALU.mult))
        c.op("vector", lambda h: h.tensor_tensor(out=r["mg12"][:, 0:W], in0=r["m"][:, 0:W],
             in1=r["g12"][:, 0:W], op=ALU.mult), tick=True)


def make_rows(nc, ctx, prefix):
    rt = [sbt(nc, ctx, f"{prefix}_{i}", [1, TC], F32) for i in range(6)]
    alias = {"m": 0, "e2": 1, "m2": 2, "var": 3, "vp": 1, "rv": 2, "q2": 4,
             "qe": 3, "rq": 4, "g1": 5, "g2": 3, "mg1": 2, "g12": 4,
             "mg12": 1}
    return {n: rt[i] for n, i in alias.items()}


def softplus_chain(nc, c, out_sb, psum_in, bias_col, w_sb, aw_sb, e_sb, l_sb, P):
    c.op("vector", lambda h: h.tensor_scalar(out=w_sb[:P, :], in0=psum_in,
         scalar1=bias_col, scalar2=None, op0=ALU.add), tick=True)
    c.op("scalar", lambda h: h.activation(out=aw_sb[:P, :], in_=w_sb[:P, :],
         func=AF.Abs))
    c.op("scalar", lambda h: h.activation(out=e_sb[:P, :], in_=aw_sb[:P, :],
         func=AF.Exp, scale=-1.0))
    c.op("scalar", lambda h: h.activation(out=l_sb[:P, :], in_=e_sb[:P, :],
         func=AF.Ln, bias=1.0), tick=True)
    c.op("vector", lambda h: h.tensor_scalar(out=w_sb[:P, :], in0=w_sb[:P, :],
         scalar1=0.0, scalar2=None, op0=ALU.max))
    c.op("vector", lambda h: h.tensor_tensor(out=out_sb, in0=w_sb[:P, :],
         in1=l_sb[:P, :], op=ALU.add), tick=True)


def emit_l0passA(nc, ctx, c, x_dram, S0_dram, sp, Wd, Cst, shared, dbg_h=None):
    w0in = [sbt(nc, ctx, f"w0in{k}", [128, 4384], BF16) for k in range(ND)]
    convw = sbt(nc, ctx, "cvw0", [128, 72], F32)
    convb = sbt(nc, ctx, "cvb0", [128, 18], F32)
    dtb = sbt(nc, ctx, "dtb0", [32, 1], F32)
    projbh = sbt(nc, ctx, "pjbh0", [128, ND], F32)
    nwc = sbt(nc, ctx, "nwc0", [128, ND], F32)
    xt = sbt(nc, ctx, "a0xt", [128, ND * TC], F32)
    sqt = sbt(nc, ctx, "a0sq", [128, ND * TC], F32)
    h_sb = sbt(nc, ctx, "a0h", [128, ND * TC], BF16)
    xbc = [sbt(nc, ctx, f"a0xbc{i}", [128, 3 + TC], F32) for i in range(NTI0)]
    halo = sbt(nc, ctx, "a0halo", [128, 54], F32)
    ct = sbt(nc, ctx, "a0ct", [128, TC], F32)
    stg = [sbt(nc, ctx, f"a0stg{i}", [128, TC], BF16) for i in range(2)]
    s0p = [sbt(nc, ctx, f"a0s0p{i}", [128, TC], F32) for i in range(2)]
    dts = sbt(nc, ctx, "a0dts", [32, TC], F32)
    sp_w = sbt(nc, ctx, "a0spw", [32, TC], F32)
    sp_a = sbt(nc, ctx, "a0spa", [32, TC], F32)
    sp_e = sbt(nc, ctx, "a0spe", [32, TC], F32)
    hrow = sbt(nc, ctx, "a0hrow", [1, TC], F32)
    hrow2 = sbt(nc, ctx, "a0hrow2", [1, TC], F32)
    rows = make_rows(nc, ctx, "a0r")
    pS = pst(nc, ctx, "a0pS", [1, TC])
    pQ = pst(nc, ctx, "a0pQ", [1, TC])
    pB1 = pst(nc, ctx, "a0pB1", [128, TC])
    pB2 = pst(nc, ctx, "a0pB2", [128, TC])
    pIP = [pst(nc, ctx, f"a0pIP{i}", [128, TC]) for i in range(2)]
    pDT = pB2[0:32, :]
    Bb, Cb = shared["Bb"], shared["Cb"]
    ones_col, ones_row = Cst["ones_col"], Cst["ones_row"]

    for k in range(ND):
        c.dma("gpsimd", lambda h, k=k: h.dma_start(
            out=w0in[k][:], in_=Wd["inWT"][k * 128:(k + 1) * 128, :]))
    for nm, dst in [("convw", convw), ("convb", convb), ("dtb", dtb),
                    ("nwc", nwc)]:
        c.dma("sync", lambda h, nm=nm, dst=dst: h.dma_start(
            out=dst[:], in_=Wd[nm][:, :]))
    c.dma("sync", lambda h: h.dma_start(out=projbh[:], in_=Wd["projb"][:, :]))
    for e in ("scalar", "vector", "tensor"):
        c.dwait(e, "sync")
        c.dwait(e, "gpsimd")
    c.op("scalar", lambda h: h.mul(out=projbh[:, :], in_=projbh[:, :], mul=0.5),
         tick=True)
    c.op("vector", lambda h: h.memset(halo[:, :], 0.0), tick=True)

    for tc in range(NCH):
        t0 = tc * TC
        c.dma("sync", lambda h, t0=t0: h.dma_start(
            out=xt[:, :].rearrange("p (d t) -> p d t", d=ND),
            in_=x_dram[:, :, t0:t0 + TC].transpose([1, 0, 2])))
        for e in ("scalar", "vector", "tensor"):
            c.dwait(e, "sync")
        emit_ln_stats(nc, c, xt, sqt, ones_col, pS, pQ, rows, eps2=1e-4)
        c.op("tensor", lambda h: h.matmul(out=pB1[:, :], lhsT=ones_row[0:1, :],
             rhs=rows["g12"][:, :], start=True, stop=True), tick=True)
        c.op("tensor", lambda h: h.matmul(out=pB2[:, :], lhsT=ones_row[0:1, :],
             rhs=rows["mg12"][:, :], start=True, stop=True), tick=True)
        for d in range(ND):
            sl = slice(d * TC, (d + 1) * TC)
            c.op("vector", lambda h, sl=sl: h.tensor_tensor(
                out=sqt[:, sl], in0=xt[:, sl], in1=pB1[:, :], op=ALU.mult))
            c.op("vector", lambda h, sl=sl: h.tensor_tensor(
                out=sqt[:, sl], in0=sqt[:, sl], in1=pB2[:, :], op=ALU.subtract))
            c.op("vector", lambda h, d=d, sl=sl: h.tensor_scalar(
                out=h_sb[:, sl], in0=sqt[:, sl], scalar1=nwc[:, d:d + 1],
                scalar2=None, op0=ALU.mult))
        c.tick_last("vector")
        if dbg_h is not None:
            c.dma("sync", lambda h, t0=t0: h.dma_start(
                out=dbg_h[:, :, t0:t0 + TC].transpose([1, 0, 2]),
                in_=h_sb[:, :].rearrange("p (d t) -> p d t", d=ND)))
        c.op("vector", lambda h: h.tensor_scalar(out=hrow[:, :],
             in0=rows["g1"][:, :], scalar1=0.5, scalar2=None, op0=ALU.mult))
        c.op("vector", lambda h: h.tensor_scalar(out=hrow2[:, :],
             in0=rows["mg1"][:, :], scalar1=0.5, scalar2=None, op0=ALU.mult),
             tick=True)
        c.op("tensor", lambda h: h.matmul(out=pB1[:, :], lhsT=ones_row[0:1, :],
             rhs=hrow[:, :], start=True, stop=True), tick=True)
        c.op("tensor", lambda h: h.matmul(out=pB2[:, :], lhsT=ones_row[0:1, :],
             rhs=hrow2[:, :], start=True, stop=True), tick=True)
        for d in range(ND):
            sl = slice(d * TC, (d + 1) * TC)
            st = s0p[d % 2]
            if d >= 2:
                c.dwait("vector", "sync")
            c.op("vector", lambda h, sl=sl, st=st: h.tensor_tensor(
                out=st[:, :], in0=xt[:, sl], in1=pB1[:, :], op=ALU.mult))
            c.op("vector", lambda h, st=st: h.tensor_tensor(
                out=st[:, :], in0=st[:, :], in1=pB2[:, :], op=ALU.subtract))
            c.op("vector", lambda h, d=d, st=st: h.tensor_scalar(
                out=st[:, :], in0=st[:, :], scalar1=projbh[:, d:d + 1],
                scalar2=None, op0=ALU.add), tick=True)
            c.dma("sync", lambda h, d=d, st=st, t0=t0: h.dma_start(
                out=S0_dram[d, :, t0:t0 + TC], in_=st[:, :]))
        def cons0(of, t0):
            pp = pIP[(of + 1) % 2]
            if of < 16:
                stz = stg[of % 2]
                if of >= 2:
                    c.dwait("scalar", "sync")
                c.op("scalar", lambda h, pp=pp, stz=stz: h.activation(
                    out=stz[:, :], in_=pp[:, :], func=AF.Silu), tick=True)
                c.dma("sync", lambda h, of=of, stz=stz, t0=t0: h.dma_start(
                    out=sp["z"][of, :, t0:t0 + TC], in_=stz[:, :]))
            else:
                ti = of - 16
                c.op("scalar", lambda h, ti=ti, pp=pp: h.copy(
                    out=xbc[ti][:, 3:3 + TC], in_=pp[:, :]), tick=True)

        for of in range(35):
            M = 32 if of == 34 else 128
            pp = pDT if of == 34 else pIP[(of + 1) % 2]

            def mms(h, of=of, M=M, pp=pp):
                for k in range(ND):
                    ins = h.matmul(out=pp[0:M, :] if M != 128 else pp[:, :],
                                   lhsT=w0in[k][:, of * 128:of * 128 + M],
                                   rhs=h_sb[:, k * TC:(k + 1) * TC],
                                   start=(k == 0), stop=(k == ND - 1))
                return ins
            c.op("tensor", mms, tick=True)
            if of >= 1 and of < 34:
                cons0(of - 1, t0)
            if of == 34:
                cons0(33, t0)
        for ti in range(NTI0):
            c.op("vector", lambda h, ti=ti: h.tensor_copy(
                out=xbc[ti][:, 0:3], in_=halo[:, ti * 3:ti * 3 + 3]))
        c.tick_last("vector")
        softplus_chain(nc, c, dts[:, :], pDT, dtb[:, 0:1], sp_w, sp_a, sp_e,
                       sp_a, 32)
        c.dma("sync", lambda h, t0=t0: h.dma_start(
            out=sp["dts"][:, t0:t0 + TC], in_=dts[:, :]))
        for ti in range(NTI0):
            c.op("vector", lambda h, ti=ti: h.tensor_scalar(
                out=ct[:, :], in0=xbc[ti][:, 0:TC],
                scalar1=convw[:, ti * 4:ti * 4 + 1],
                scalar2=convb[:, ti:ti + 1], op0=ALU.mult, op1=ALU.add))
            for k in range(1, 4):
                c.op("vector", lambda h, ti=ti, k=k: h.scalar_tensor_tensor(
                    out=ct[:, :], in0=xbc[ti][:, k:k + TC],
                    scalar=convw[:, ti * 4 + k:ti * 4 + k + 1], in1=ct[:, :],
                    op0=ALU.mult, op1=ALU.add))
            c.op("vector", lambda h, ti=ti: h.tensor_copy(
                out=halo[:, ti * 3:ti * 3 + 3], in_=xbc[ti][:, TC:TC + 3]),
                tick=True)
            if ti < 16:
                stz = stg[ti % 2]
                c.dwait("scalar", "sync")
                c.op("scalar", lambda h, stz=stz: h.activation(
                    out=stz[:, :], in_=ct[:, :], func=AF.Silu), tick=True)
                c.dma("sync", lambda h, ti=ti, stz=stz, t0=t0: h.dma_start(
                    out=sp["xs"][ti, :, t0:t0 + TC], in_=stz[:, :]))
            elif ti == 16:
                c.op("scalar", lambda h, t0=t0: h.activation(
                    out=Bb[:, t0:t0 + TC], in_=ct[:, :], func=AF.Silu),
                    tick=True)
            else:
                c.op("scalar", lambda h, t0=t0: h.activation(
                    out=Cb[:, t0:t0 + TC], in_=ct[:, :], func=AF.Silu),
                    tick=True)
        c.barrier()


def emit_l0passB(nc, ctx, c, S0_dram, sp, Wd, Cst, shared, dbg_y=None, dbg_W=None, dbg_ssd=None, dbg2=None):
    NPO = DIN0 // 128
    POT = [sbt(nc, ctx, f"pot{k}", [128, D], BF16) for k in range(NPO)]
    Bb, Cb = shared["Bb"], shared["Cb"]
    ones_col, ones_row = Cst["ones_col"], Cst["ones_row"]
    id128, idbf, tri, ones32 = Cst["id128"], Cst["idbf"], Cst["tri"], Cst["ones32"]
    pB1 = pst(nc, ctx, "b0pB1", [128, TC])
    pIP = [pst(nc, ctx, f"b0pIP{i}", [128, TC]) for i in range(2)]
    pS = pst(nc, ctx, "b0pS", [1, TC])
    bankA = pst(nc, ctx, "b0bankA", [128, TC])
    bankB = pst(nc, ctx, "b0bankB", [128, 224])
    pWB = bankA[:, 0:128]
    pSC = bankA[:, 128:256]
    pY = bankA[:, 256:384]
    pEC = bankA[:, 384:416]
    pXTt = pst(nc, ctx, "b0pXT", [128, 2 * HD], BF16)
    pTR = bankB[:, 128:224]
    pBTt = pst(nc, ctx, "b0pBT", [128, LC], BF16)
    pBT = pBTt[:, :]

    with ExitStack() as pctx:
        owr = [sbt(nc, pctx, f"owr{k}", [128, DIN0], BF16) for k in range(ND)]
        pjt = [sbt(nc, pctx, f"pjt{k}", [128, D], BF16) for k in range(ND)]
        for k in range(ND):
            c.dma("gpsimd", lambda h, k=k: h.dma_start(
                out=owr[k][:], in_=Wd["outWraw"][k * 128:(k + 1) * 128, :]))
            c.dma("gpsimd", lambda h, k=k: h.dma_start(
                out=pjt[k][:], in_=Wd["projT"][k * 128:(k + 1) * 128, :]))
        c.dwait("tensor", "gpsimd")
        for m in range(NPO):
            for nh in range(2):
                def mmp(h, m=m, nh=nh):
                    for k in range(ND):
                        ins = h.matmul(
                            out=pIP[nh][:, 0:TC],
                            lhsT=owr[k][:, m * 128:(m + 1) * 128],
                            rhs=pjt[k][:, nh * TC:(nh + 1) * TC],
                            start=(k == 0), stop=(k == ND - 1))
                    return ins
                c.op("tensor", mmp, tick=True)
                c.op("vector", lambda h, m=m, nh=nh: h.tensor_copy(
                    out=POT[m][:, nh * TC:(nh + 1) * TC], in_=pIP[nh][:, 0:TC]),
                    tick=True)
        c.barrier()

    xsb = sbt(nc, ctx, "b0xsb", [128, 16 * TC], BF16)
    zb = sbt(nc, ctx, "b0zb", [128, 16 * TC], BF16)
    dts = sbt(nc, ctx, "b0dts", [32, TC], F32)
    W_ = sbt(nc, ctx, "b0W", [32, TC], F32)
    V_ = sbt(nc, ctx, "b0V", [32, TC], F32)
    wflat = sbt(nc, ctx, "b0wfl", [1, NH * LC], F32)
    dtT = sbt(nc, ctx, "b0dtT", [128, 32], F32)
    negb = sbt(nc, ctx, "b0negb", [128, 32], F32)
    dect = sbt(nc, ctx, "b0dec", [128, 32], F32)
    combT = sbt(nc, ctx, "b0comb", [128, 32], F32)
    echb = sbt(nc, ctx, "b0echb", [128, 32], F32)
    scr32 = sbt(nc, ctx, "b0scr32", [128, 32], F32)
    scoresM = sbt(nc, ctx, "b0scm", [128, LC], F32)
    BT = sbt(nc, ctx, "b0BT", [128, LC], BF16)
    Ea = sbt(nc, ctx, "b0Ea", [128, 2 * LC], F32)
    Ep = sbt(nc, ctx, "b0Ep", [128, 2 * LC], F32)
    dtxT = sbt(nc, ctx, "b0dtx", [128, 2 * HD], BF16)
    dtx3 = sbt(nc, ctx, "b0dtx3", [128, 2 * HD], BF16)
    Mt = sbt(nc, ctx, "b0Mt", [128, 2 * LC], BF16)
    Cpp = sbt(nc, ctx, "b0Cpp", [128, 2 * LC], BF16)
    y_sb = sbt(nc, ctx, "b0y", [128, 16 * TC], F32)
    yn_sb = sbt(nc, ctx, "b0yn", [128, 16 * TC], BF16)
    zsil = sbt(nc, ctx, "b0zs", [128, TC], F32)
    sq1 = sbt(nc, ctx, "b0sq1", [128, TC], F32)
    s0a = sbt(nc, ctx, "b0s0a", [128, TC], F32)
    s0b = sbt(nc, ctx, "b0s0b", [128, TC], F32)
    grms = sbt(nc, ctx, "b0grms", [1, TC], F32)
    rtmp = sbt(nc, ctx, "b0rtmp", [1, TC], F32)
    aheads = sbt(nc, ctx, "b0ah", [32, 1], F32)
    abc = sbt(nc, ctx, "b0abc", [128, 32], F32)
    S0f = sbt(nc, ctx, "b0Sf", [128, NH * HD], F32)
    S0bf = sbt(nc, ctx, "b0Sb", [128, NH * HD], BF16)
    alog = sbt(nc, ctx, "b0alog", [32, 1], F32)
    drep = sbt(nc, ctx, "b0drep", [128, 16], F32)
    gnw = sbt(nc, ctx, "b0gnw", [128, 16], F32)

    c.dma("sync", lambda h: h.dma_start(out=alog[:], in_=Wd["alog"][:, :]))
    c.dma("sync", lambda h: h.dma_start(out=drep[:], in_=Wd["drep"][:, :]))
    c.dma("sync", lambda h: h.dma_start(out=gnw[:], in_=Wd["gnw"][:, :]))
    for e in ("scalar", "vector", "tensor"):
        c.dwait(e, "sync")
    c.op("scalar", lambda h: h.activation(out=aheads[:, :], in_=alog[:, :],
         func=AF.Exp), tick=True)
    c.op("vector", lambda h: h.tensor_scalar(out=aheads[:, :], in0=aheads[:, :],
         scalar1=-1.0, scalar2=None, op0=ALU.mult), tick=True)
    c.dma("gpsimd", lambda h: h.dma_start(out=wflat[0:1, 0:32],
         in_=aheads[:, :], allow_slow_non_contiguous=True))
    c.dwait("tensor", "gpsimd")
    c.op("tensor", lambda h: h.matmul(out=pEC, lhsT=ones_row[0:1, :],
         rhs=wflat[0:1, 0:32], start=True, stop=True), tick=True)
    c.op("vector", lambda h: h.tensor_copy(out=abc[:, :], in_=pEC), tick=True)
    c.op("vector", lambda h: h.memset(S0f[:, :], 0.0))
    c.op("vector", lambda h: h.memset(S0bf[:, :], 0.0), tick=True)

    for tc in range(NCH):
        t0 = tc * TC
        c.dma("sync", lambda h, t0=t0: h.dma_start(
            out=xsb[:, :].rearrange("p (d t) -> p d t", d=16),
            in_=sp["xs"][:, :, t0:t0 + TC].transpose([1, 0, 2])))
        c.dma("sync", lambda h, t0=t0: h.dma_start(
            out=zb[:, :].rearrange("p (d t) -> p d t", d=16),
            in_=sp["z"][:, :, t0:t0 + TC].transpose([1, 0, 2])))
        c.dma("sync", lambda h, t0=t0: h.dma_start(
            out=dts[:, :], in_=sp["dts"][:, t0:t0 + TC]))
        for e in ("scalar", "vector", "tensor", "gpsimd"):
            c.dwait(e, "sync")
        for lc in range(NLC):
            l0 = lc * LC
            c.op("vector", lambda h, l0=l0: h.tensor_tensor_scan(
                out=W_[:, l0:l0 + LC], data0=ones32[:, 0:LC],
                data1=dts[:, l0:l0 + LC], initial=0.0,
                op0=ALU.mult, op1=ALU.add), tick=True)
            c.fence("vector")
            c.op("vector", lambda h, l0=l0: h.tensor_scalar(
                out=V_[:, l0:l0 + LC], in0=W_[:, l0:l0 + LC], scalar1=-1.0,
                scalar2=W_[:, l0 + LC - 1:l0 + LC], op0=ALU.mult, op1=ALU.add),
                tick=True)
        for lc in range(NLC):
            l0 = lc * LC

            def trs(h, l0=l0):
                h.transpose(out=pTR[:, 0:32], in_=dts[:, l0:l0 + LC],
                            identity=id128[0:32, 0:32])
                h.transpose(out=pTR[:, 32:64], in_=W_[:, l0:l0 + LC],
                            identity=id128[0:32, 0:32])
                return h.transpose(out=pTR[:, 64:96], in_=V_[:, l0:l0 + LC],
                                   identity=id128[0:32, 0:32])
            c.op("tensor", trs, tick=True)
            c.op("vector", lambda h: h.tensor_copy(out=dtT[:, :],
                 in_=pTR[:, 0:32]))
            c.op("vector", lambda h: h.scalar_tensor_tensor(
                out=negb[:, :], in0=abc[:, :], scalar=-1.0, in1=pTR[:, 32:64],
                op0=ALU.mult, op1=ALU.mult))
            c.op("vector", lambda h: h.tensor_tensor(
                out=scr32[:, :], in0=abc[:, :], in1=pTR[:, 64:96],
                op=ALU.mult), tick=True)
            c.op("scalar", lambda h: h.activation(out=dect[:, :],
                 in_=scr32[:, :], func=AF.Exp), tick=True)
            c.op("vector", lambda h: h.tensor_tensor(out=combT[:, :],
                 in0=dtT[:, :], in1=dect[:, :], op=ALU.mult), tick=True)
            c.dma("gpsimd", lambda h, l0=l0: h.dma_start(
                out=wflat[0:1, :].rearrange("a (b c) -> a b c", b=NH),
                in_=W_[:, l0:l0 + LC]))
            c.dwait("tensor", "gpsimd")
            c.op("tensor", lambda h: h.matmul(out=pEC,
                 lhsT=ones_row[0:1, :], rhs=wflat[0:1, LC - 1::LC],
                 start=True, stop=True), tick=True)
            c.op("vector", lambda h: h.tensor_tensor(out=scr32[:, :],
                 in0=abc[:, :], in1=pEC, op=ALU.mult), tick=True)
            c.op("scalar", lambda h: h.activation(out=echb[:, :],
                 in_=scr32[:, :], func=AF.Exp), tick=True)
            c.op("tensor", lambda h, t0=t0, l0=l0: h.matmul(out=pSC,
                 lhsT=Bb[:, t0 + l0:t0 + l0 + LC], rhs=Cb[:, t0 + l0:t0 + l0 + LC],
                 start=True, stop=True), tick=True)
            c.op("vector", lambda h: h.tensor_tensor(out=scoresM[:, :],
                 in0=pSC, in1=tri[:, :], op=ALU.mult), tick=True)
            c.op("tensor", lambda h, t0=t0, l0=l0: h.transpose(out=pBT,
                 in_=Bb[:, t0 + l0:t0 + l0 + LC], identity=idbf[:, :]),
                 tick=True)
            c.op("vector", lambda h: h.tensor_copy(out=BT[:, :], in_=pBT),
                 tick=True)
            if dbg_ssd is not None and tc == 0 and lc == 0:
                c.op("vector", lambda h: h.tensor_copy(out=zsil[:, 128:160],
                     in_=pTR[:, 64:96]), tick=True)
                c.dma("sync", lambda h: h.dma_start(out=dbg2[0:128, 0:32], in_=dtT[:, :]))
                c.dma("sync", lambda h: h.dma_start(out=dbg2[0:128, 32:64], in_=dect[:, :]))
                c.dma("sync", lambda h: h.dma_start(out=dbg2[0:128, 64:96], in_=combT[:, :]))
                c.dma("sync", lambda h: h.dma_start(out=dbg2[0:128, 96:128], in_=zsil[:, 128:160]))
                c.dma("sync", lambda h: h.dma_start(out=dbg2[0:128, 128:160], in_=negb[:, :]))
                c.dma("sync", lambda h: h.dma_start(out=dbg2[0:128, 160:192], in_=echb[:, :]))
                c.dwait("vector", "sync")
                c.dwait("scalar", "sync")
            for hq in range(NH // 2):
                hd0 = 2 * hq

                def pe1(h, hd0=hd0, hq=hq, l0=l0):
                    h.matmul(out=bankA[:, 0:256], lhsT=ones_row[0:1, :],
                             rhs=wflat[0:1, hd0 * LC:(hd0 + 2) * LC],
                             start=True, stop=True)
                    return h.transpose(
                        out=pXTt[:, 0:2 * HD],
                        in_=xsb[0:128, hq * TC + l0:hq * TC + l0 + LC],
                        identity=Cst["idbf"][:, :])
                c.op("tensor", pe1, tick=True)
                for s_ in range(2):
                    hd = hd0 + s_
                    esl = slice(s_ * LC, (s_ + 1) * LC)
                    c.op("scalar", lambda h, hd=hd, esl=esl, s_=s_:
                         h.activation(
                             out=Ea[:, esl],
                             in_=bankA[:, s_ * 128:(s_ + 1) * 128],
                             func=AF.Exp,
                             scale=abc[:, hd:hd + 1], bias=negb[:, hd:hd + 1]))
                    c.op("scalar", lambda h, hd=hd, esl=esl, s_=s_:
                         h.activation(
                             out=Ep[:, esl],
                             in_=bankA[:, s_ * 128:(s_ + 1) * 128],
                             func=AF.Exp,
                             scale=abc[:, hd:hd + 1]))
                c.tick_last("scalar")
                for s_ in range(2):
                    hd = hd0 + s_
                    xsl = slice(s_ * HD, (s_ + 1) * HD)
                    esl = slice(s_ * LC, (s_ + 1) * LC)
                    c.op("vector", lambda h, hd=hd, xsl=xsl: h.tensor_scalar(
                        out=dtxT[:, xsl], in0=pXTt[:, xsl],
                        scalar1=dtT[:, hd:hd + 1],
                        scalar2=None, op0=ALU.mult))
                    c.op("vector", lambda h, hd=hd, xsl=xsl: h.tensor_scalar(
                        out=dtx3[:, xsl], in0=pXTt[:, xsl],
                        scalar1=combT[:, hd:hd + 1],
                        scalar2=None, op0=ALU.mult))
                    c.op("vector", lambda h, esl=esl: h.scalar_tensor_tensor(
                        out=Mt[:, esl], in0=Ea[:, esl], scalar=BIG,
                        in1=scoresM[:, :],
                        op0=ALU.min, op1=ALU.mult))
                    c.op("vector", lambda h, t0=t0, l0=l0, esl=esl:
                         h.tensor_tensor(
                             out=Cpp[:, esl],
                             in0=Cb[:, t0 + l0:t0 + l0 + LC], in1=Ep[:, esl],
                             op=ALU.mult))
                c.tick_last("vector")

                def pe2(h, hd0=hd0):
                    h.matmul(out=pY[0:64], lhsT=dtxT[:, 0:HD],
                             rhs=Mt[:, 0:LC], start=True, stop=False)
                    h.matmul(out=pY[0:64],
                             lhsT=S0bf[:, hd0 * HD:(hd0 + 1) * HD],
                             rhs=Cpp[:, 0:LC],
                             start=False, stop=True)
                    h.matmul(out=pY[64:128], lhsT=dtxT[:, HD:2 * HD],
                             rhs=Mt[:, LC:2 * LC], start=True, stop=False)
                    h.matmul(out=pY[64:128],
                             lhsT=S0bf[:, (hd0 + 1) * HD:(hd0 + 2) * HD],
                             rhs=Cpp[:, LC:2 * LC],
                             start=False, stop=True)
                    return h.matmul(out=bankB[:, 0:128], lhsT=BT[:, :],
                                    rhs=dtx3[:, 0:2 * HD],
                                    start=True, stop=True)
                c.op("tensor", pe2, tick=True)
                if dbg_ssd is not None and tc == 0 and lc == 0 and hd == 0:
                    c.op("vector", lambda h: h.tensor_copy(out=zsil[:, 0:128],
                         in_=pY[0:128]), tick=True)
                    c.dma("sync", lambda h: h.dma_start(out=dbg_ssd[0:128, 0:128], in_=zsil[:, 0:128]))
                    c.dma("sync", lambda h: h.dma_start(out=dbg_ssd[0:128, 128:256], in_=scoresM[:, :]))
                    c.dma("gpsimd", lambda h: h.dma_start(out=dbg_ssd[0:128, 256:384], in_=Mt[:, :]))
                    c.dma("gpsimd", lambda h: h.dma_start(out=dbg_ssd[0:128, 384:448], in_=dtxT[:, :]))
                    c.dma("gpsimd", lambda h: h.dma_start(out=dbg_ssd[0:128, 448:512], in_=dtx3[:, :]))
                c.op("vector", lambda h, hd0=hd0: h.scalar_tensor_tensor(
                    out=S0f[:, hd0 * HD:(hd0 + 1) * HD],
                    in0=S0f[:, hd0 * HD:(hd0 + 1) * HD],
                    scalar=echb[:, hd0:hd0 + 1], in1=bankB[:, 0:64],
                    op0=ALU.mult, op1=ALU.add))
                c.op("vector", lambda h, hd0=hd0: h.scalar_tensor_tensor(
                    out=S0f[:, (hd0 + 1) * HD:(hd0 + 2) * HD],
                    in0=S0f[:, (hd0 + 1) * HD:(hd0 + 2) * HD],
                    scalar=echb[:, hd0 + 1:hd0 + 2], in1=bankB[:, 64:128],
                    op0=ALU.mult, op1=ALU.add))
                c.op("vector", lambda h, hd0=hd0: h.tensor_copy(
                    out=S0bf[:, hd0 * HD:(hd0 + 2) * HD],
                    in_=S0f[:, hd0 * HD:(hd0 + 2) * HD]))
                c.op("vector", lambda h, hq=hq, l0=l0: h.scalar_tensor_tensor(
                    out=y_sb[0:64, hq * TC + l0:hq * TC + l0 + LC],
                    in0=xsb[0:64, hq * TC + l0:hq * TC + l0 + LC],
                    scalar=drep[0:64, hq:hq + 1], in1=pY[0:64],
                    op0=ALU.mult, op1=ALU.add))
                c.op("vector", lambda h, hq=hq, l0=l0: h.scalar_tensor_tensor(
                    out=y_sb[64:128, hq * TC + l0:hq * TC + l0 + LC],
                    in0=xsb[64:128, hq * TC + l0:hq * TC + l0 + LC],
                    scalar=drep[64:128, hq:hq + 1], in1=pY[64:128],
                    op0=ALU.mult, op1=ALU.add), tick=True)
        if dbg_y is not None:
            c.dma("sync", lambda h, t0=t0: h.dma_start(
                out=dbg_y[:, :, t0:t0 + TC].transpose([1, 0, 2]),
                in_=y_sb[:, :].rearrange("p (d t) -> p d t", d=16)))
            c.dma("sync", lambda h, t0=t0: h.dma_start(
                out=dbg_W[:, t0:t0 + TC], in_=W_[:, :]))
            c.dwait("vector", "sync")
            c.dwait("scalar", "sync")
        for of in range(16):
            sl = slice(of * TC, (of + 1) * TC)
            c.op("vector", lambda h, sl=sl: h.tensor_tensor(
                out=y_sb[:, sl], in0=y_sb[:, sl], in1=zb[:, sl],
                op=ALU.mult), tick=True)
            c.op("scalar", lambda h, sl=sl: h.activation(
                out=sq1[:, :], in_=y_sb[:, sl], func=AF.Square), tick=True)
            c.op("tensor", lambda h, of=of: h.matmul(
                out=pS[:, :], lhsT=ones_col[:, 0:1], rhs=sq1[:, :],
                start=(of == 0), stop=(of == 15)), tick=(of == 15))
        c.op("vector", lambda h: h.tensor_scalar(out=rtmp[:, :], in0=pS[0:1, :],
             scalar1=1.0 / DIN0, scalar2=1e-5, op0=ALU.mult, op1=ALU.add))
        c.op("vector", lambda h: h.reciprocal(out=grms[:, :], in_=rtmp[:, :]),
             tick=True)
        c.op("scalar", lambda h: h.sqrt(out=grms[:, :], in_=grms[:, :]),
             tick=True)
        c.op("tensor", lambda h: h.matmul(out=pB1[:, :], lhsT=ones_row[0:1, :],
             rhs=grms[:, :], start=True, stop=True), tick=True)
        for of in range(16):
            sl = slice(of * TC, (of + 1) * TC)
            c.op("vector", lambda h, of=of, sl=sl: h.scalar_tensor_tensor(
                out=yn_sb[:, sl], in0=y_sb[:, sl], scalar=gnw[:, of:of + 1],
                in1=pB1[:, :], op0=ALU.mult, op1=ALU.mult))
        c.tick_last("vector")
        for of in range(ND):
            def mmo(h, of=of):
                for k in range(NPO):
                    ins = h.matmul(out=pIP[of % 2][:, :],
                                   lhsT=POT[k][:, of * 128:(of + 1) * 128],
                                   rhs=yn_sb[:, k * TC:(k + 1) * TC],
                                   start=(k == 0), stop=(k == NPO - 1))
                return ins
            c.op("tensor", mmo, tick=True)
            st = s0a if of % 2 == 0 else s0b
            if of >= 2:
                c.dwait("vector", "gpsimd")
            c.op("vector", lambda h, of=of, st=st: h.tensor_copy(
                out=st[:, :], in_=pIP[of % 2][:, :]), tick=True)
            c.dma("gpsimd", lambda h, of=of, st=st, t0=t0: h.dma_start(
                out=S0_dram[of, :, t0:t0 + TC], in_=st[:, :],
                accum_op=ALU.add))
        c.barrier()


def emit_l0ffn(nc, ctx, c, x_dram, S0_dram, Wd, Cst):
    TCF = 256
    NCF = TOWN // TCF
    NOF = HID0 // 128
    wgb = [sbt(nc, ctx, f"wgb{k}", [128, HID0], BF16) for k in range(ND)]
    wub = [sbt(nc, ctx, f"wub{k}", [128, HID0], BF16) for k in range(ND)]
    wdb = [sbt(nc, ctx, f"wdb{k}", [128, D], BF16) for k in range(NOF)]
    for k in range(ND):
        c.dma("gpsimd", lambda h, k=k: h.dma_start(
            out=wgb[k][:], in_=Wd["wgT"][k * 128:(k + 1) * 128, :]))
        c.dma("gpsimd", lambda h, k=k: h.dma_start(
            out=wub[k][:], in_=Wd["wuT"][k * 128:(k + 1) * 128, :]))
    gumk = c.mark("gpsimd")
    for k in range(NOF):
        c.dma("gpsimd", lambda h, k=k: h.dma_start(
            out=wdb[k][:], in_=Wd["wdT"][k * 128:(k + 1) * 128, :]))
    xt = sbt(nc, ctx, "f0xt", [128, ND * TCF], F32)
    sqt = sbt(nc, ctx, "f0sq", [128, ND * TCF], F32)
    xnb = sbt(nc, ctx, "f0xn", [128, ND * TCF], BF16)
    gub = sbt(nc, ctx, "f0gu", [128, NOF * TCF], BF16)
    sgt = sbt(nc, ctx, "f0sg", [128, TCF], BF16)
    f0t = [sbt(nc, ctx, f"f0o{i}", [128, TCF], F32) for i in range(2)]
    rows = make_rows(nc, ctx, "f0r")
    pS = pst(nc, ctx, "f0pS", [1, TCF])
    pQ = pst(nc, ctx, "f0pQ", [1, TCF])
    pB1 = pst(nc, ctx, "f0pB1", [128, TCF])
    pB2 = pst(nc, ctx, "f0pB2", [128, TCF])
    pIP = [pst(nc, ctx, f"f0pIP{i}", [128, TCF]) for i in range(2)]
    ones_col, ones_row = Cst["ones_col"], Cst["ones_row"]
    c.dwait_cnt("tensor", "gpsimd", gumk)
    c.dwait_cnt("vector", "gpsimd", gumk)
    c.dwait_cnt("scalar", "gpsimd", gumk)
    for fc in range(NCF):
        t0 = fc * TCF
        c.dma("sync", lambda h, t0=t0: h.dma_start(
            out=xt[:, :].rearrange("p (d t) -> p d t", d=ND),
            in_=x_dram[:, :, t0:t0 + TCF].transpose([1, 0, 2])))
        c.dwait("scalar", "sync")
        c.dwait("tensor", "sync")
        c.dwait("vector", "sync")
        emit_ln_stats(nc, c, xt, sqt, ones_col, pS, pQ, rows, eps2=None, W=TCF)
        c.op("tensor", lambda h: h.matmul(out=pB1[:, 0:TCF], lhsT=ones_row[0:1, :],
             rhs=rows["g1"][:, 0:TCF], start=True, stop=True), tick=True)
        c.op("tensor", lambda h: h.matmul(out=pB2[:, 0:TCF], lhsT=ones_row[0:1, :],
             rhs=rows["mg1"][:, 0:TCF], start=True, stop=True), tick=True)
        for d in range(ND):
            sl = slice(d * TCF, (d + 1) * TCF)
            c.op("vector", lambda h, sl=sl: h.tensor_tensor(
                out=sqt[:, sl], in0=xt[:, sl], in1=pB1[:, 0:TCF], op=ALU.mult))
            c.op("vector", lambda h, sl=sl: h.tensor_tensor(
                out=xnb[:, sl], in0=sqt[:, sl], in1=pB2[:, 0:TCF],
                op=ALU.subtract))
        c.tick_last("vector")
        def mmg(h, of):
            for k in range(ND):
                ins = h.matmul(out=pIP[0][:, 0:TCF],
                               lhsT=wgb[k][:, of * 128:(of + 1) * 128],
                               rhs=xnb[:, k * TCF:(k + 1) * TCF],
                               start=(k == 0), stop=(k == ND - 1))
            return ins

        def mmu(h, of):
            for k in range(ND):
                ins = h.matmul(out=pIP[1][:, 0:TCF],
                               lhsT=wub[k][:, of * 128:(of + 1) * 128],
                               rhs=xnb[:, k * TCF:(k + 1) * TCF],
                               start=(k == 0), stop=(k == ND - 1))
            return ins
        for of in range(NOF):
            c.op("tensor", lambda h, of=of: mmg(h, of), tick=True)
            c.op("tensor", lambda h, of=of: mmu(h, of), tick=True)
            c.op("scalar", lambda h: h.activation(
                out=sgt[:, :], in_=pIP[0][:, 0:TCF], func=AF.Silu, scale=QS),
                tick=True)
            c.op("vector", lambda h, of=of: h.scalar_tensor_tensor(
                out=gub[:, of * TCF:(of + 1) * TCF], in0=pIP[1][:, 0:TCF],
                scalar=QS, in1=sgt[:, :], op0=ALU.mult, op1=ALU.mult),
                tick=True)
        def ev_dn(of2, t0):
            ft = f0t[of2 % 2]
            if of2 >= 2:
                c.dwait("vector", "gpsimd")
            c.op("vector", lambda h, of2=of2, ft=ft: h.tensor_scalar(
                out=ft[:, :], in0=pIP[of2 % 2][:, 0:TCF], scalar1=QS,
                scalar2=None, op0=ALU.mult), tick=True)
            c.dma("gpsimd", lambda h, of2=of2, ft=ft, t0=t0: h.dma_start(
                out=S0_dram[of2, :, t0:t0 + TCF], in_=ft[:, :],
                accum_op=ALU.add))
        c.dwait("tensor", "gpsimd")
        for of2 in range(ND):
            def mmd(h, of2=of2):
                for k in range(NOF):
                    ins = h.matmul(out=pIP[of2 % 2][:, 0:TCF],
                                   lhsT=wdb[k][:, of2 * 128:(of2 + 1) * 128],
                                   rhs=gub[:, k * TCF:(k + 1) * TCF],
                                   start=(k == 0), stop=(k == NOF - 1))
                return ins
            c.op("tensor", mmd, tick=True)
            if of2 >= 1:
                ev_dn(of2 - 1, t0)
        ev_dn(ND - 1, t0)
        c.barrier()


def load_consts(nc, ctx, c):
    Cst = {}
    decls = [("ones_col", [128, 1], F32), ("ones_row", [1, 128], F32),
             ("id128", [128, 128], F32), ("idbf", [128, 128], BF16),
             ("tri", [128, 128], F32), ("ones32", [32, 128], F32),
             ("negones", [1, TC], F32), ("idbf64", [128, 64], BF16),
             ("ones_row_bf", [1, 128], BF16)]
    for nm, shape, dt in decls:
        dr = nc.dram_tensor("c_" + nm, shape, dt, kind="ExternalInput").ap()
        t = sbt(nc, ctx, "cs_" + nm, shape, dt)
        c.dma("sync", lambda h, t=t, dr=dr: h.dma_start(out=t[:], in_=dr[:, :]))
        Cst[nm] = t
    return Cst


def host_consts():
    return {
        "c_ones_col": np.ones((128, 1), np.float32),
        "c_ones_row": np.ones((1, 128), np.float32),
        "c_id128": np.eye(128, dtype=np.float32),
        "c_idbf": np.eye(128).astype(ml_dtypes.bfloat16),
        "c_tri": np.triu(np.ones((128, 128), np.float32)),
        "c_ones32": np.ones((32, 128), np.float32),
        "c_negones": -np.ones((1, TC), np.float32),
        "c_idbf64": np.vstack([np.eye(64), np.eye(64)]).astype(ml_dtypes.bfloat16),
        "c_ones_row_bf": np.ones((1, 128), ml_dtypes.bfloat16),
    }


def col_form(v, nt=None):
    v = np.asarray(v, np.float32)
    n = v.shape[0]
    nt = n // 128
    return np.ascontiguousarray(v.reshape(nt, 128).T)


def prep_l0_weights(inp, rev):
    d = "bwd" if rev else "fwd"
    W = {}
    W["inWT"] = np.ascontiguousarray(inp[f"L0_attn_{d}_in_W"].T)
    W["outWraw"] = np.ascontiguousarray(inp[f"L0_attn_{d}_out_W"])
    pj = inp["L0_attn_proj_W"]
    W["projT"] = np.ascontiguousarray((pj[:, 1024:] if rev else pj[:, :1024]).T)
    cw = inp[f"L0_attn_{d}_conv_w"]
    W["convw"] = np.ascontiguousarray(
        cw.reshape(18, 128, 4).transpose(1, 0, 2).reshape(128, 72))
    W["convb"] = col_form(inp[f"L0_attn_{d}_conv_b"])
    W["dtb"] = np.asarray(inp[f"L0_attn_{d}_dt_bias"], np.float32).reshape(32, 1)
    W["alog"] = np.asarray(inp[f"L0_attn_{d}_A_log"], np.float32).reshape(32, 1)
    W["drep"] = col_form(np.repeat(np.asarray(inp[f"L0_attn_{d}_D"], np.float32), HD))
    W["gnw"] = col_form(inp[f"L0_attn_{d}_gnorm_w"])
    W["projb"] = col_form(inp["L0_attn_proj_b"])
    W["nwc"] = col_form(inp["L0_attn_norm_w"])
    W["wgT"] = np.ascontiguousarray(inp["L0_ffn_w_gate"].T)
    W["wuT"] = np.ascontiguousarray(inp["L0_ffn_w_up"].T)
    W["wdT"] = np.ascontiguousarray(inp["L0_ffn_w_down"].T)
    return W


L0W_SHAPES = [("inWT", [D, 4384], F32), ("outWraw", [D, DIN0], F32),
              ("projT", [D, D], F32), ("convw", [128, 72], F32),
              ("convb", [128, 18], F32), ("dtb", [32, 1], F32),
              ("alog", [32, 1], F32), ("drep", [128, 16], F32),
              ("gnw", [128, 16], F32), ("projb", [128, ND], F32),
              ("nwc", [128, ND], F32),
              ("wgT", [D, HID0], I32), ("wuT", [D, HID0], I32),
              ("wdT", [HID0, D], I32)]


def declare_l0_scratch(nc, dbg=False):
    kind = "ExternalOutput" if dbg else "Internal"
    return {
        "xs": nc.dram_tensor("sp_xs", [16, 128, T], BF16, kind=kind).ap(),
        "z": nc.dram_tensor("sp_z", [16, 128, T], BF16, kind=kind).ap(),
        "dts": nc.dram_tensor("sp_dts", [32, T], F32, kind=kind).ap(),
    }


def emit_l0(nc, ctx, c, x_dram, S0_dram, Wd, Cst, parts="abf", dbg=False):
    sp = declare_l0_scratch(nc, dbg)
    dbg_h = (nc.dram_tensor("dbg_h", [ND, 128, T], BF16,
                            kind="ExternalOutput").ap() if dbg else None)
    dbg_y = (nc.dram_tensor("dbg_y", [16, 128, T], F32,
                            kind="ExternalOutput").ap() if dbg else None)
    dbg_W = (nc.dram_tensor("dbg_W", [32, T], F32,
                            kind="ExternalOutput").ap() if dbg else None)
    dbg_ssd = (nc.dram_tensor("dbg_ssd", [128, 512], F32,
                              kind="ExternalOutput").ap() if dbg else None)
    dbg2 = (nc.dram_tensor("dbg2", [128, 256], F32,
                           kind="ExternalOutput").ap() if dbg else None)
    with ExitStack() as mctx:
        shared = {
            "Bb": sbt(nc, mctx, "l0Bb", [128, T], BF16),
            "Cb": sbt(nc, mctx, "l0Cb", [128, T], BF16),
        }
        with ExitStack() as actx:
            emit_l0passA(nc, actx, c, x_dram, S0_dram, sp, Wd, Cst, shared,
                         dbg_h)
            c.barrier()
        if "b" in parts:
            with ExitStack() as bctx:
                emit_l0passB(nc, bctx, c, S0_dram, sp, Wd, Cst, shared, dbg_y, dbg_W, dbg_ssd, dbg2)
                c.barrier()
    if "f" in parts:
        with ExitStack() as fctx:
            emit_l0ffn(nc, fctx, c, x_dram, S0_dram, Wd, Cst)
            c.barrier()


def build_l0(parts="abf", dbg=False):
    nc = bass.Bass("TRN2", target_bir_lowering=False)
    x_dram = nc.dram_tensor("x", [ND, 128, T], F32, kind="ExternalInput").ap()
    S0_dram = nc.dram_tensor("S0", [ND, 128, T], F32, kind="ExternalOutput").ap()
    Wd = {}
    for nm, shape, dt in L0W_SHAPES:
        Wd[nm] = nc.dram_tensor("w_" + nm, shape, dt, kind="ExternalInput").ap()
    with ExitStack() as ctx:
        c = Ctl(nc, ctx)
        Cst = load_consts(nc, ctx, c)
        emit_l0(nc, ctx, c, x_dram, S0_dram, Wd, Cst, parts, dbg)
        with nc.Block() as block:
            c.emit(block)
    return nc


def emit_combine(nc, ctx, c, x_dram, Sown, Sprev, x1_dram, rev_prev=False,
                 masks=None, G=None):
    xt = sbt(nc, ctx, "cbx", [128, TC], F32)
    at = sbt(nc, ctx, "cba", [128, TC], F32)
    bt = sbt(nc, ctx, "cbb", [128, TC], F32)
    g0 = sbt(nc, ctx, "cbg0", [128, TC], F32)
    g1t = sbt(nc, ctx, "cbg1", [128, TC], F32)
    ot = [sbt(nc, ctx, f"cbo{i}", [128, TC], F32) for i in range(2)]
    for tc in range(NCH):
        t0 = tc * TC
        tm = (NCH - 1 - tc) * TC
        for d in range(ND):
            c.dwait("vector", "sync")
            c.dma("sync", lambda h, d=d, t0=t0: h.dma_start(
                out=xt[:, :], in_=x_dram[d, :, t0:t0 + TC]))
            c.dma("sync", lambda h, d=d, t0=t0: h.dma_start(
                out=at[:, :], in_=Sown[d, :, t0:t0 + TC]))
            if masks is None:
                c.dma("sync", lambda h, d=d, t0=t0: h.dma_start(
                    out=bt[:, :], in_=Sprev[d, :, t0:t0 + TC]))
            else:
                c.dma("sync", lambda h, d=d, tm=tm: h.dma_start(
                    out=g0[:, :], in_=G[0][d, :, tm:tm + TC]))
                c.dma("sync", lambda h, d=d, tm=tm: h.dma_start(
                    out=g1t[:, :], in_=G[1][d, :, tm:tm + TC]))
            c.dwait("vector", "sync")
            o = ot[d % 2]
            c.op("vector", lambda h, o=o: h.tensor_tensor(
                out=o[:, :], in0=xt[:, :], in1=at[:, :], op=ALU.add))
            if masks is None:
                c.op("vector", lambda h, o=o: h.tensor_tensor(
                    out=o[:, :], in0=o[:, :], in1=bt[:, :], op=ALU.add),
                    tick=True)
            else:
                c.op("vector", lambda h, o=o: h.scalar_tensor_tensor(
                    out=o[:, :], in0=g0[:, ::-1], scalar=masks[:, 0:1],
                    in1=o[:, :], op0=ALU.mult, op1=ALU.add))
                c.op("vector", lambda h, o=o: h.scalar_tensor_tensor(
                    out=o[:, :], in0=g1t[:, ::-1], scalar=masks[:, 1:2],
                    in1=o[:, :], op0=ALU.mult, op1=ALU.add), tick=True)
            c.dma("sync", lambda h, d=d, t0=t0, o=o: h.dma_start(
                out=x1_dram[d, :, t0:t0 + TC], in_=o[:, :]))
    c.barrier()


def emit_l1passA(nc, ctx, c, x1_dram, sp1, Wd, Cst, shared1):
    w1in = [sbt(nc, ctx, f"w1in{k}", [128, 2 * DIN1], BF16) for k in range(ND)]
    convw = sbt(nc, ctx, "cvw1", [128, 64], F32)
    convb = sbt(nc, ctx, "cvb1", [128, 16], F32)
    lngr = sbt(nc, ctx, "lngr", [1, D], F32)
    lnbr = sbt(nc, ctx, "lnbr", [1, D], F32)
    lngc = sbt(nc, ctx, "lngc", [128, ND], F32)
    xt = sbt(nc, ctx, "a1xt", [128, ND * TC], F32)
    sqt = sbt(nc, ctx, "a1sq", [128, ND * TC], F32)
    xn2b = sbt(nc, ctx, "a1xn2", [128, ND * TC], BF16)
    xnst = sbt(nc, ctx, "a1xnst", [128, TC], BF16)
    xnf = sbt(nc, ctx, "a1xnf", [128, TC], F32)
    wgf = sbt(nc, ctx, "a1wgf", [128, ND * NE], F32)
    scst = sbt(nc, ctx, "a1scst", [8, TC], F32)
    wgsum = sbt(nc, ctx, "a1wgsum", [8, 1], F32)
    xbc = [sbt(nc, ctx, f"a1xbc{i}", [128, 3 + TC], F32) for i in range(NTI1)]
    halo = sbt(nc, ctx, "a1halo", [128, 48], F32)
    ct = sbt(nc, ctx, "a1ct", [128, TC], F32)
    stg = [sbt(nc, ctx, f"a1stg{i}", [128, TC], BF16) for i in range(2)]
    stf = [sbt(nc, ctx, f"a1stf{i}", [128, TC], F32) for i in range(2)]
    rows = make_rows(nc, ctx, "a1r")
    pS = pst(nc, ctx, "a1pS", [1, TC])
    pQ = pst(nc, ctx, "a1pQ", [1, TC])
    pB1 = pst(nc, ctx, "a1pB1", [128, TC])
    pB2 = pst(nc, ctx, "a1pB2", [128, TC])
    pIP = [pst(nc, ctx, f"a1pIP{i}", [128, TC]) for i in range(2)]
    pSC8 = pst(nc, ctx, "a1pSC8", [8, TC])
    ones_col, ones_row = Cst["ones_col"], Cst["ones_row"]

    for k in range(ND):
        c.dma("gpsimd", lambda h, k=k: h.dma_start(
            out=w1in[k][:], in_=Wd["in1T"][k * 128:(k + 1) * 128, :]))
    c.dma("sync", lambda h: h.dma_start(out=wgf[:], in_=Wd["WgT"][:, :]))
    for nm, dst in [("conv1w", convw), ("conv1b", convb),
                    ("lngr", lngr), ("lnbr", lnbr), ("lngc", lngc)]:
        c.dma("sync", lambda h, nm=nm, dst=dst: h.dma_start(
            out=dst[:], in_=Wd[nm][:, :]))
    for e in ("scalar", "vector", "tensor"):
        c.dwait(e, "sync")
        c.dwait(e, "gpsimd")
    c.op("vector", lambda h: h.memset(halo[:, :], 0.0), tick=True)

    def mmwsum(h):
        for d in range(ND):
            ins = h.matmul(out=pSC8[:, 0:1], lhsT=wgf[:, d * NE:(d + 1) * NE],
                           rhs=Cst["ones_col"][:, 0:1],
                           start=(d == 0), stop=(d == ND - 1))
        return ins
    c.op("tensor", mmwsum, tick=True)
    c.op("vector", lambda h: h.tensor_copy(out=wgsum[:, :],
         in_=pSC8[:, 0:1]), tick=True)

    for tc in range(NCH):
        t0 = tc * TC
        c.dma("sync", lambda h, t0=t0: h.dma_start(
            out=xt[:, :].rearrange("p (d t) -> p d t", d=ND),
            in_=x1_dram[:, :, t0:t0 + TC].transpose([1, 0, 2])))
        for e in ("scalar", "vector", "tensor"):
            c.dwait(e, "sync")
        emit_ln_stats(nc, c, xt, sqt, ones_col, pS, pQ, rows, eps2=1e-5)
        if tc < NCHO:
            c.op("tensor", lambda h: h.matmul(out=pB1[:, :], lhsT=ones_row[0:1, :],
                 rhs=rows["g1"][:, :], start=True, stop=True), tick=True)
            c.op("tensor", lambda h: h.matmul(out=pB2[:, :], lhsT=ones_row[0:1, :],
                 rhs=rows["mg1"][:, :], start=True, stop=True), tick=True)
            for d in range(ND):
                sl = slice(d * TC, (d + 1) * TC)
                c.dwait("vector", "sync")
                c.op("vector", lambda h, sl=sl: h.tensor_tensor(
                    out=sqt[:, sl], in0=xt[:, sl], in1=pB1[:, :], op=ALU.mult))
                c.op("vector", lambda h, sl=sl: h.tensor_tensor(
                    out=xnst[:, :], in0=sqt[:, sl], in1=pB2[:, :],
                    op=ALU.subtract), tick=True)
                c.dma("sync", lambda h, d=d, t0=t0: h.dma_start(
                    out=sp1["xn"][d, :, t0:t0 + TC], in_=xnst[:, :]))
            def mmwx(h):
                for d in range(ND):
                    ins = h.matmul(out=pSC8[:, :],
                                   lhsT=wgf[:, d * NE:(d + 1) * NE],
                                   rhs=xt[:, d * TC:(d + 1) * TC],
                                   start=(d == 0), stop=(d == ND - 1))
                return ins
            c.op("tensor", mmwx, tick=True)
            c.op("scalar", lambda h: h.copy(out=scst[:, :],
                 in_=pSC8[:, :]), tick=True)
            c.fence("vector")
            c.op("vector", lambda h: h.tensor_tensor(out=scst[:, :],
                 in0=scst[:, :], in1=pB1[0:8, :], op=ALU.mult), tick=True)
            c.op("vector", lambda h: h.tensor_scalar(out=xnf[0:8, :],
                 in0=pB2[0:8, :], scalar1=wgsum[:, 0:1], scalar2=None,
                 op0=ALU.mult), tick=True)
            c.fence("vector")
            c.op("vector", lambda h: h.tensor_tensor(out=scst[:, :],
                 in0=scst[:, :], in1=xnf[0:8, :], op=ALU.subtract), tick=True)
            c.dma("sync", lambda h, t0=t0: h.dma_start(
                out=sp1["scores"][:, t0:t0 + TC], in_=scst[:, :]))
        c.op("tensor", lambda h: h.matmul(out=pB1[:, :], lhsT=ones_row[0:1, :],
             rhs=rows["g12"][:, :], start=True, stop=True), tick=True)
        for d in range(ND):
            sl = slice(d * TC, (d + 1) * TC)

            def mmr(h, d=d):
                h.matmul(out=pB2[:, :], lhsT=lngr[0:1, d * 128:(d + 1) * 128],
                         rhs=rows["mg12"][:, :], start=True, stop=False)
                return h.matmul(out=pB2[:, :],
                                lhsT=lnbr[0:1, d * 128:(d + 1) * 128],
                                rhs=Cst["negones"][0:1, :],
                                start=False, stop=True)
            c.op("tensor", mmr, tick=True)
            c.op("vector", lambda h, sl=sl: h.tensor_tensor(
                out=sqt[:, sl], in0=xt[:, sl], in1=pB1[:, :], op=ALU.mult))
            c.op("vector", lambda h, d=d, sl=sl: h.scalar_tensor_tensor(
                out=xn2b[:, sl], in0=sqt[:, sl], scalar=lngc[:, d:d + 1],
                in1=pB2[:, :], op0=ALU.mult, op1=ALU.subtract), tick=True)
        def cons1(of, t0):
            if of < 16:
                c.op("scalar", lambda h, of=of: h.copy(
                    out=xbc[of][:, 3:3 + TC], in_=pIP[of % 2][:, :]), tick=True)
            else:
                stz = stg[of % 2]
                c.dwait("scalar", "sync")
                c.op("scalar", lambda h, of=of, stz=stz: h.activation(
                    out=stz[:, :], in_=pIP[of % 2][:, :], func=AF.Silu),
                    tick=True)
                c.dma("sync", lambda h, of=of, stz=stz, t0=t0: h.dma_start(
                    out=sp1["z"][of - 16, :, t0:t0 + TC], in_=stz[:, :]))

        for of in range(32):
            def mms(h, of=of):
                for k in range(ND):
                    ins = h.matmul(out=pIP[of % 2][:, :],
                                   lhsT=w1in[k][:, of * 128:(of + 1) * 128],
                                   rhs=xn2b[:, k * TC:(k + 1) * TC],
                                   start=(k == 0), stop=(k == ND - 1))
                return ins
            c.op("tensor", mms, tick=True)
            if of >= 1:
                cons1(of - 1, t0)
        cons1(31, t0)
        for ti in range(NTI1):
            c.op("vector", lambda h, ti=ti: h.tensor_copy(
                out=xbc[ti][:, 0:3], in_=halo[:, ti * 3:ti * 3 + 3]))
        c.tick_last("vector")
        for ti in range(NTI1):
            c.op("vector", lambda h, ti=ti: h.tensor_scalar(
                out=ct[:, :], in0=xbc[ti][:, 0:TC],
                scalar1=convw[:, ti * 4:ti * 4 + 1],
                scalar2=convb[:, ti:ti + 1], op0=ALU.mult, op1=ALU.add))
            for k in range(1, 4):
                c.op("vector", lambda h, ti=ti, k=k: h.scalar_tensor_tensor(
                    out=ct[:, :], in0=xbc[ti][:, k:k + TC],
                    scalar=convw[:, ti * 4 + k:ti * 4 + k + 1], in1=ct[:, :],
                    op0=ALU.mult, op1=ALU.add))
            c.op("vector", lambda h, ti=ti: h.tensor_copy(
                out=halo[:, ti * 3:ti * 3 + 3], in_=xbc[ti][:, TC:TC + 3]),
                tick=True)
            stz = stf[ti % 2]
            c.dwait("scalar", "sync")
            c.op("scalar", lambda h, stz=stz: h.activation(
                out=stz[:, :], in_=ct[:, :], func=AF.Silu), tick=True)
            c.dma("sync", lambda h, ti=ti, stz=stz, t0=t0: h.dma_start(
                out=sp1["xc"][ti, :, t0:t0 + TC], in_=stz[:, :]))
        c.barrier()


def emit_l1passA2(nc, ctx, c, sp1, Wd, Cst, shared1):
    xpd = [sbt(nc, ctx, f"xpd{k}", [128, DTR], BF16) for k in range(NTI1)]
    xpb = [sbt(nc, ctx, f"xpb{k}", [128, NST], BF16) for k in range(NTI1)]
    xpc = [sbt(nc, ctx, f"xpc{k}", [128, NST], BF16) for k in range(NTI1)]
    dtw = sbt(nc, ctx, "dtw1", [64, DIN1], BF16)
    dtb1 = sbt(nc, ctx, "dtb1", [128, 16], F32)
    xcb = sbt(nc, ctx, "a2xcb", [128, NTI1 * TC], BF16)
    dtfm = sbt(nc, ctx, "a2dtfm", [64, TC], BF16)
    stf = [sbt(nc, ctx, f"a2stf{i}", [128, TC], F32) for i in range(2)]
    sp_w = sbt(nc, ctx, "a2spw", [128, TC], F32)
    sp_a = sbt(nc, ctx, "a2spa", [128, TC], F32)
    sp_e = sbt(nc, ctx, "a2spe", [128, TC], F32)
    pB1 = pst(nc, ctx, "a2pB1", [128, TC])
    pB2 = pst(nc, ctx, "a2pB2", [128, TC])
    pIP = [pst(nc, ctx, f"a2pIP{i}", [128, TC]) for i in range(2)]
    pDT = pB2[0:64, :]
    pBC = pB1[0:16, :]
    Bf, Cf = shared1["Bf"], shared1["Cf"]
    for k in range(NTI1):
        c.dma("gpsimd", lambda h, k=k: h.dma_start(
            out=xpd[k][:], in_=Wd["xpdT"][k * 128:(k + 1) * 128, :]))
        c.dma("gpsimd", lambda h, k=k: h.dma_start(
            out=xpb[k][:], in_=Wd["xpbT"][k * 128:(k + 1) * 128, :]))
        c.dma("gpsimd", lambda h, k=k: h.dma_start(
            out=xpc[k][:], in_=Wd["xpcT"][k * 128:(k + 1) * 128, :]))
    c.dma("gpsimd", lambda h: h.dma_start(out=dtw[:], in_=Wd["dtwT"][:, :]))
    c.dma("sync", lambda h: h.dma_start(out=dtb1[:], in_=Wd["dtb1"][:, :]))
    for e in ("scalar", "vector", "tensor"):
        c.dwait(e, "sync")
        c.dwait(e, "gpsimd")
    for tc in range(NCH):
        t0 = tc * TC
        c.dma("gpsimd", lambda h, t0=t0: h.dma_start(
            out=xcb[:, :].rearrange("p (d t) -> p d t", d=NTI1),
            in_=sp1["xc"][:, :, t0:t0 + TC].transpose([1, 0, 2])))
        for e in ("scalar", "vector", "tensor"):
            c.dwait(e, "gpsimd")
            c.dwait(e, "sync")
        def mmdt(h):
            for k in range(NTI1):
                ins = h.matmul(out=pDT, lhsT=xpd[k][:, :],
                               rhs=xcb[:, k * TC:(k + 1) * TC],
                               start=(k == 0), stop=(k == NTI1 - 1))
            return ins
        c.op("tensor", mmdt, tick=True)
        c.op("vector", lambda h: h.tensor_copy(out=dtfm[:, :], in_=pDT),
             tick=True)

        def mmb(h):
            for k in range(NTI1):
                ins = h.matmul(out=pBC, lhsT=xpb[k][:, :],
                               rhs=xcb[:, k * TC:(k + 1) * TC],
                               start=(k == 0), stop=(k == NTI1 - 1))
            return ins
        c.op("tensor", mmb, tick=True)
        c.op("vector", lambda h, t0=t0: h.tensor_copy(
            out=Bf[:, t0:t0 + TC], in_=pBC), tick=True)

        def mmc(h):
            for k in range(NTI1):
                ins = h.matmul(out=pBC, lhsT=xpc[k][:, :],
                               rhs=xcb[:, k * TC:(k + 1) * TC],
                               start=(k == 0), stop=(k == NTI1 - 1))
            return ins
        c.op("tensor", mmc, tick=True)
        c.op("vector", lambda h, t0=t0: h.tensor_copy(
            out=Cf[:, t0:t0 + TC], in_=pBC), tick=True)
        for of in range(NTI1):
            c.op("tensor", lambda h, of=of: h.matmul(
                out=pIP[of % 2][:, :], lhsT=dtw[:, of * 128:(of + 1) * 128],
                rhs=dtfm[:, :], start=True, stop=True), tick=True)
            softplus_chain(nc, c, sp_w[:, :], pIP[of % 2][:, :],
                           dtb1[:, of:of + 1], sp_w, sp_a, sp_e, sp_a, 128)
            c.dwait("vector", "sync")
            stz = stf[of % 2]
            c.op("vector", lambda h, stz=stz: h.tensor_copy(
                out=stz[:, :], in_=sp_w[:, :]), tick=True)
            c.dma("sync", lambda h, of=of, stz=stz, t0=t0: h.dma_start(
                out=sp1["delta"][of, :, t0:t0 + TC], in_=stz[:, :]))
        c.barrier()


def emit_l1passB(nc, ctx, c, sp1, S1ab, Wd, Cst, shared1):
    w1out = [sbt(nc, ctx, f"w1out{k}", [128, D], BF16) for k in range(NTI1)]
    A1c = sbt(nc, ctx, "A1c", [128, NTI1 * NST], F32)
    D1c = sbt(nc, ctx, "D1c", [128, 16], F32)
    zbg = sbt(nc, ctx, "b1zbg", [128, 4 * TC], BF16)
    Bbc = sbt(nc, ctx, "b1Bbc", [128, NST * TC], BF16)
    Cbc = sbt(nc, ctx, "b1Cbc", [128, NST * TC], BF16)
    bflat = sbt(nc, ctx, "b1bfl", [1, 4 * TC], BF16)
    dA = [sbt(nc, ctx, f"b1dA{i}", [128, NGRP * TC], F32) for i in range(2)]
    dBu = sbt(nc, ctx, "b1dBu", [128, NGRP * TC], F32)
    hb = sbt(nc, ctx, "b1h", [128, NGRP * TC], F32)
    delg = sbt(nc, ctx, "b1delg", [128, 4 * TC], F32)
    xcg = sbt(nc, ctx, "b1xcg", [128, 4 * TC], F32)
    du = sbt(nc, ctx, "b1du", [128, TC], F32)
    ycon = sbt(nc, ctx, "b1ycon", [128, TC], F32)
    yg = sbt(nc, ctx, "b1yg", [128, NTI1 * TC], BF16)
    s1t = [sbt(nc, ctx, f"b1s1{i}", [128, TC], F32) for i in range(2)]
    S1st = sbt(nc, ctx, "b1state", [128, 256], F32)
    pB1 = pst(nc, ctx, "b1pB1", [128, TC])
    pIP = [pst(nc, ctx, f"b1pIP{i}", [128, TC]) for i in range(2)]
    Bf, Cf = shared1["Bf"], shared1["Cf"]
    ones_row = Cst["ones_row"]

    for k in range(NTI1):
        c.dma("gpsimd", lambda h, k=k: h.dma_start(
            out=w1out[k][:], in_=Wd["out1T"][k * 128:(k + 1) * 128, :]))
    c.dma("sync", lambda h: h.dma_start(out=A1c[:], in_=Wd["A1col"][:, :]))
    c.dma("sync", lambda h: h.dma_start(out=D1c[:], in_=Wd["D1col"][:, :]))
    for e in ("scalar", "vector", "tensor"):
        c.dwait(e, "sync")
        c.dwait(e, "gpsimd")
    c.op("scalar", lambda h: h.activation(out=A1c[:, :], in_=A1c[:, :],
         func=AF.Exp), tick=True)
    c.op("vector", lambda h: h.tensor_scalar(out=A1c[:, :], in0=A1c[:, :],
         scalar1=-1.0, scalar2=None, op0=ALU.mult), tick=True)

    for tc in range(NCH):
        t0 = tc * TC
        for srcf, dstb in ((Bf, Bbc), (Cf, Cbc)):
            for g4 in range(4):
                c.dma("gpsimd", lambda h, srcf=srcf, g4=g4, t0=t0:
                      h.dma_start(
                          out=bflat[0:1, :].rearrange("a (b t) -> a b t", b=4),
                          in_=srcf[g4 * 4:(g4 + 1) * 4, t0:t0 + TC]))
                c.dwait("tensor", "gpsimd")
                for j in range(4):
                    n = g4 * 4 + j
                    c.op("tensor", lambda h, j=j: h.matmul(
                        out=pB1[:, :], lhsT=Cst["ones_row_bf"][0:1, :],
                        rhs=bflat[0:1, j * TC:(j + 1) * TC],
                        start=True, stop=True), tick=True)
                    c.op("scalar", lambda h, n=n, dstb=dstb: h.copy(
                        out=dstb[:, n * TC:(n + 1) * TC], in_=pB1[:, :]),
                        tick=True)
        for dg in range(4):
            c.dwait("vector", "sync")
            c.dwait("scalar", "sync")
            c.dma("sync", lambda h, dg=dg, t0=t0: h.dma_start(
                out=delg[:, :].rearrange("p (d t) -> p d t", d=4),
                in_=sp1["delta"][dg * 4:(dg + 1) * 4, :, t0:t0 + TC].transpose([1, 0, 2])))
            c.dma("sync", lambda h, dg=dg, t0=t0: h.dma_start(
                out=xcg[:, :].rearrange("p (d t) -> p d t", d=4),
                in_=sp1["xc"][dg * 4:(dg + 1) * 4, :, t0:t0 + TC].transpose([1, 0, 2])))
            c.dma("sync", lambda h, dg=dg, t0=t0: h.dma_start(
                out=zbg[:, :].rearrange("p (d t) -> p d t", d=4),
                in_=sp1["z"][dg * 4:(dg + 1) * 4, :, t0:t0 + TC].transpose([1, 0, 2])))
            for e in ("scalar", "vector"):
                c.dwait(e, "sync")
            def emit_exps(dl, grp, pg, tick):
                dsl = slice(dl * TC, (dl + 1) * TC)
                dtile = dg * 4 + dl
                for j in range(NGRP):
                    n = grp * NGRP + j
                    c.op("scalar", lambda h, dsl=dsl, j=j, dtile=dtile, n=n,
                         pg=pg:
                         h.activation(out=dA[pg][:, j * TC:(j + 1) * TC],
                                      in_=delg[:, dsl], func=AF.Exp,
                                      scale=A1c[:, dtile * NST + n:dtile * NST + n + 1]))
                if tick:
                    c.tick_last("scalar")

            units = [(dl, grp) for dl in range(4) for grp in range(2)]
            emit_exps(0, 0, 0, True)
            for ui, (dl, grp) in enumerate(units):
                dtile = dg * 4 + dl
                dsl = slice(dl * TC, (dl + 1) * TC)
                pg = ui % 2
                if grp == 0:
                    c.op("vector", lambda h, dsl=dsl: h.tensor_tensor(
                        out=du[:, :], in0=delg[:, dsl], in1=xcg[:, dsl],
                        op=ALU.mult), tick=True)
                bsl = slice(grp * NGRP * TC, (grp + 1) * NGRP * TC)
                c.op("vector", lambda h, bsl=bsl: h.tensor_tensor(
                    out=dBu[:, :].rearrange("p (g t) -> p g t", g=NGRP),
                    in0=du[:, :].unsqueeze(1).broadcast_to([128, NGRP, TC]),
                    in1=Bbc[:, bsl].rearrange("p (g t) -> p g t", g=NGRP),
                    op=ALU.mult), tick=True)
                c.fence("vector")
                if ui + 1 < len(units):
                    emit_exps(*units[ui + 1], 1 - pg, False)
                for j in range(NGRP):
                    n = grp * NGRP + j
                    if tc == 0:
                        ini = 0.0
                    else:
                        ini = S1st[:, dtile * NST + n:dtile * NST + n + 1]
                    c.op("vector", lambda h, j=j, ini=ini, pg=pg:
                         h.tensor_tensor_scan(
                             out=hb[:, j * TC:(j + 1) * TC],
                             data0=dA[pg][:, j * TC:(j + 1) * TC],
                             data1=dBu[:, j * TC:(j + 1) * TC],
                             initial=ini, op0=ALU.mult, op1=ALU.add))
                c.tick_last("vector")
                if ui + 1 < len(units):
                    c.tick_last("scalar")
                c.op("vector", lambda h, dtile=dtile, grp=grp:
                     h.tensor_copy(
                         out=S1st[:, dtile * NST + grp * NGRP:
                                  dtile * NST + (grp + 1) * NGRP],
                         in_=hb[:, TC - 1::TC]), tick=True)
                c.op("vector", lambda h, bsl=bsl: h.tensor_tensor(
                    out=hb[:, :].rearrange("p (g t) -> p g t", g=NGRP),
                    in0=hb[:, :].rearrange("p (g t) -> p g t", g=NGRP),
                    in1=Cbc[:, bsl].rearrange("p (g t) -> p g t", g=NGRP),
                    op=ALU.mult), tick=True)
                t4 = dBu
                c.op("vector", lambda h: h.tensor_tensor(
                    out=t4[:, 0:4 * TC], in0=hb[:, 0:4 * TC],
                    in1=hb[:, 4 * TC:8 * TC], op=ALU.add))
                c.op("vector", lambda h: h.tensor_tensor(
                    out=t4[:, 0:2 * TC], in0=t4[:, 0:2 * TC],
                    in1=t4[:, 2 * TC:4 * TC], op=ALU.add))
                if grp == 0:
                    c.op("vector", lambda h: h.tensor_tensor(
                        out=ycon[:, :], in0=t4[:, 0:TC], in1=t4[:, TC:2 * TC],
                        op=ALU.add), tick=True)
                else:
                    c.op("vector", lambda h, t4=t4: h.tensor_tensor(
                        out=t4[:, 2 * TC:3 * TC], in0=t4[:, 0:TC],
                        in1=t4[:, TC:2 * TC], op=ALU.add))
                    c.op("vector", lambda h, t4=t4: h.tensor_tensor(
                        out=ycon[:, :], in0=ycon[:, :],
                        in1=t4[:, 2 * TC:3 * TC], op=ALU.add), tick=True)
                if grp == 1:
                    c.op("vector", lambda h, dsl=dsl, dtile=dtile:
                         h.scalar_tensor_tensor(
                             out=ycon[:, :], in0=xcg[:, dsl],
                             scalar=D1c[:, dtile:dtile + 1], in1=ycon[:, :],
                             op0=ALU.mult, op1=ALU.add))
                    c.op("vector", lambda h, dtile=dtile, dsl=dsl:
                         h.tensor_tensor(
                             out=yg[:, dtile * TC:(dtile + 1) * TC],
                             in0=ycon[:, :], in1=zbg[:, dsl], op=ALU.mult),
                        tick=True)
        for of in range(ND):
            def mmo(h, of=of):
                for k in range(NTI1):
                    ins = h.matmul(out=pIP[of % 2][:, :],
                                   lhsT=w1out[k][:, of * 128:(of + 1) * 128],
                                   rhs=yg[:, k * TC:(k + 1) * TC],
                                   start=(k == 0), stop=(k == NTI1 - 1))
                return ins
            c.op("tensor", mmo, tick=True)
            st = s1t[of % 2]
            c.dwait("vector", "sync")
            c.op("vector", lambda h, of=of, st=st: h.tensor_scalar(
                out=st[:, :], in0=pIP[of % 2][:, :], scalar1=0.5,
                scalar2=None, op0=ALU.mult), tick=True)
            tgt, toff = (S1ab[0], t0) if t0 < TOWN else (S1ab[1], t0 - TOWN)
            c.dma("sync", lambda h, of=of, st=st, tgt=tgt, toff=toff:
                  h.dma_start(out=tgt[of, :, toff:toff + TC], in_=st[:, :]))
        c.barrier()


def emit_moe(nc, ctx, c, sp1, S1a_dram, Wd, Cst, dbg_g=None, dbg_g2=None, dbg_g3=None):
    NHOF = HID1 // 128
    gates = sbt(nc, ctx, "mgates", [8, TOWN], F32)
    gflat = sbt(nc, ctx, "mgflat", [1, TOWN], F32)
    scf = sbt(nc, ctx, "mscf", [8, TC], F32)
    scT = sbt(nc, ctx, "mscT", [128, 8], F32)
    m1 = sbt(nc, ctx, "mm1", [128, 1], F32)
    nm1 = sbt(nc, ctx, "mnm1", [128, 1], F32)
    m2 = sbt(nc, ctx, "mm2", [128, 1], F32)
    Et = sbt(nc, ctx, "mEt", [128, 8], F32)
    eqt = sbt(nc, ctx, "meqt", [128, 8], F32)
    mskd = sbt(nc, ctx, "mmskd", [128, 8], F32)
    ssum = sbt(nc, ctx, "mssum", [128, 1], F32)
    gtT = sbt(nc, ctx, "mgtT", [128, 8], F32)
    W1b = [sbt(nc, ctx, f"mW1{k}", [128, HID1], BF16) for k in range(ND)]
    W3b = [sbt(nc, ctx, f"mW3{k}", [128, HID1], BF16) for k in range(ND)]
    W2b = [sbt(nc, ctx, f"mW2{k}", [128, D], BF16) for k in range(NHOF)]
    A1b = [sbt(nc, ctx, f"mA1{k}", [128, RK], BF16) for k in range(ND)]
    A3b = [sbt(nc, ctx, f"mA3{k}", [128, RK], BF16) for k in range(ND)]
    A2b = [sbt(nc, ctx, f"mA2{k}", [128, RK], BF16) for k in range(NHOF)]
    B1b = sbt(nc, ctx, "mB1", [RK, HID1], BF16)
    B3b = sbt(nc, ctx, "mB3", [RK, HID1], BF16)
    B2b = sbt(nc, ctx, "mB2", [RK, D], BF16)
    b1c = sbt(nc, ctx, "mb1c", [128, NHOF], F32)
    b3c = sbt(nc, ctx, "mb3c", [128, NHOF], F32)
    b2c = sbt(nc, ctx, "mb2c", [128, ND], F32)
    sh1 = sbt(nc, ctx, "msh1", [128, NHOF * TC], BF16)
    gsb = sbt(nc, ctx, "mgsb", [128, NHOF * TC], BF16)
    l1b = sbt(nc, ctx, "ml1", [RK, TC], BF16)
    l3b = sbt(nc, ctx, "ml3", [RK, TC], BF16)
    l2b = sbt(nc, ctx, "ml2", [RK, TC], BF16)
    gb = sbt(nc, ctx, "mgb", [128, TC], F32)
    yw = [sbt(nc, ctx, f"myw{i}", [128, TC], F32) for i in range(2)]
    xnfull = sbt(nc, ctx, "mxnf", [128, ND * TOWN], BF16)
    pT8 = pst(nc, ctx, "mpT8", [128, 8])
    pT8b = pst(nc, ctx, "mpT8b", [8, 128])
    pL = pst(nc, ctx, "mpL", [RK, TC])
    pH = [pst(nc, ctx, f"mpH{i}", [128, TC]) for i in range(2)]
    pGB = pst(nc, ctx, "mpGB", [128, TC])
    ones_row, id128 = Cst["ones_row"], Cst["id128"]

    c.dma("scalar", lambda h: h.dma_start(
        out=xnfull[:, :].rearrange("p (d t) -> p d t", d=ND),
        in_=sp1["xn"][:, :, 0:TOWN].transpose([1, 0, 2])))
    for e in ("scalar", "vector", "tensor"):
        c.dwait(e, "sync")
        c.dwait(e, "gpsimd")
    for tco in range(NCHO):
        t0 = tco * TC
        c.dma("sync", lambda h, t0=t0: h.dma_start(
            out=scf[:, :], in_=sp1["scores"][:, t0:t0 + TC]))
        for e_ in ("vector", "tensor"):
            c.dwait(e_, "sync")
        if dbg_g is not None:
            c.dma("sync", lambda h, tco=tco: h.dma_start(
                out=dbg_g[:, tco * TC:(tco + 1) * TC], in_=scf[:, :]))
            c.dwait("vector", "sync")
        for g in range(TC // 128):
            c.op("tensor", lambda h, g=g: h.transpose(
                out=pT8[:, :], in_=scf[:, g * 128:(g + 1) * 128],
                identity=id128[0:8, 0:8]), tick=True)

            def V(fn):
                c.op("vector", fn, tick=True)
                c.fence("vector")
            V(lambda h: h.reduce_max(out=m1[:, :], in_=pT8[:, :],
              axis=mybir.AxisListType.X))
            V(lambda h: h.tensor_scalar(out=mskd[:, :], in0=pT8[:, :],
              scalar1=m1[:, 0:1], scalar2=None, op0=ALU.subtract))
            c.fence("scalar")
            c.op("scalar", lambda h: h.activation(out=Et[:, :], in_=mskd[:, :],
                 func=AF.Exp), tick=True)
            c.fence("vector")
            V(lambda h: h.tensor_scalar(out=eqt[:, :], in0=pT8[:, :],
              scalar1=m1[:, 0:1], scalar2=None, op0=ALU.is_ge))
            V(lambda h: h.scalar_tensor_tensor(out=mskd[:, :],
              in0=eqt[:, :], scalar=-BIG, in1=pT8[:, :], op0=ALU.mult,
              op1=ALU.add))
            V(lambda h: h.reduce_max(out=m2[:, :], in_=mskd[:, :],
              axis=mybir.AxisListType.X))
            V(lambda h: h.tensor_scalar(out=eqt[:, :], in0=pT8[:, :],
              scalar1=m2[:, 0:1], scalar2=None, op0=ALU.is_ge))
            V(lambda h: h.tensor_tensor(out=Et[:, :], in0=Et[:, :],
              in1=eqt[:, :], op=ALU.mult))
            V(lambda h: h.reduce_sum(out=ssum[:, :], in_=Et[:, :],
              axis=mybir.AxisListType.X))
            V(lambda h: h.reciprocal(out=ssum[:, :], in_=ssum[:, :]))
            V(lambda h: h.tensor_scalar(out=gtT[:, :], in0=Et[:, :],
              scalar1=ssum[:, 0:1], scalar2=None, op0=ALU.mult))
            if dbg_g is not None and tco == 0 and g == 0:
                c.dma("sync", lambda h: h.dma_start(out=dbg_g3[0:128, 0:8], in_=mskd[:, :]))
                c.dma("sync", lambda h: h.dma_start(out=dbg_g3[0:128, 8:16], in_=gtT[:, :]))
                c.dma("sync", lambda h: h.dma_start(out=dbg_g3[0:128, 16:24], in_=Et[:, :]))
                c.dma("sync", lambda h: h.dma_start(out=dbg_g3[0:128, 24:25], in_=m1[:, :], allow_slow_non_contiguous=True))
                c.dma("sync", lambda h: h.dma_start(out=dbg_g3[0:128, 26:27], in_=m2[:, :], allow_slow_non_contiguous=True))
                c.dma("sync", lambda h: h.dma_start(out=dbg_g3[0:128, 28:30], in_=eqt[:, 0:2], allow_slow_non_contiguous=True))
                c.dwait("vector", "sync")
                c.dwait("tensor", "sync")
            c.op("tensor", lambda h: h.transpose(
                out=pT8b[:, :], in_=gtT[:, :], identity=id128[:, :]), tick=True)
            c.op("vector", lambda h, t0=t0, g=g: h.tensor_copy(
                out=gates[:, t0 + g * 128:t0 + (g + 1) * 128],
                in_=pT8b[:, :]), tick=True)
    if dbg_g is not None:
        c.dma("sync", lambda h: h.dma_start(out=dbg_g2[:, :], in_=gates[:, :]))
        c.dwait("vector", "sync")
    def batchA(e):
        for k in range(ND):
            c.dma("gpsimd", lambda h, e=e, k=k: h.dma_start(
                out=W1b[k][:], in_=Wd["W1T"][e, k * 128:(k + 1) * 128, :]))
            c.dma("gpsimd", lambda h, e=e, k=k: h.dma_start(
                out=W3b[k][:], in_=Wd["W3T"][e, k * 128:(k + 1) * 128, :]))
            c.dma("gpsimd", lambda h, e=e, k=k: h.dma_start(
                out=A1b[k][:], in_=Wd["A1T"][e, k * 128:(k + 1) * 128, :]))
            c.dma("gpsimd", lambda h, e=e, k=k: h.dma_start(
                out=A3b[k][:], in_=Wd["A3T"][e, k * 128:(k + 1) * 128, :]))
        c.dma("gpsimd", lambda h, e=e: h.dma_start(
            out=B1b[:], in_=Wd["B1T"][e, :, :]))
        c.dma("gpsimd", lambda h, e=e: h.dma_start(
            out=B3b[:], in_=Wd["B3T"][e, :, :]))
        c.dma("sync", lambda h, e=e: h.dma_start(
            out=b1c[:], in_=Wd["b1c"][e, :, :]))
        c.dma("sync", lambda h, e=e: h.dma_start(
            out=b3c[:], in_=Wd["b3c"][e, :, :]))
        c.dma("sync", lambda h, e=e: h.dma_start(
            out=gflat[:], in_=gates[e:e + 1, :]))

    def batchB(e):
        for k in range(NHOF):
            c.dma("gpsimd", lambda h, e=e, k=k: h.dma_start(
                out=W2b[k][:], in_=Wd["W2T"][e, k * 128:(k + 1) * 128, :]))
            c.dma("gpsimd", lambda h, e=e, k=k: h.dma_start(
                out=A2b[k][:], in_=Wd["A2T"][e, k * 128:(k + 1) * 128, :]))
        c.dma("gpsimd", lambda h, e=e: h.dma_start(
            out=B2b[:], in_=Wd["B2T"][e, :, :]))
        c.dma("sync", lambda h, e=e: h.dma_start(
            out=b2c[:], in_=Wd["b2c"][e, :, :]))
        return (c.mark("sync"), c.mark("gpsimd"))

    batchA(0)
    bmkB = None
    for e in range(NE):
        for eng in ("scalar", "vector", "tensor"):
            c.dwait(eng, "sync")
            c.dwait(eng, "gpsimd")
            c.dwait(eng, "scalar")
        for tco in range(NCHO):
            t0 = tco * TC
            xsl = [slice(k * TOWN + t0, k * TOWN + t0 + TC) for k in range(ND)]
            c.op("tensor", lambda h, t0=t0: h.matmul(
                out=pGB[:, :], lhsT=ones_row[0:1, :],
                rhs=gflat[0:1, t0:t0 + TC], start=True, stop=True), tick=True)
            c.op("scalar", lambda h: h.copy(out=gb[:, :], in_=pGB[:, :]),
                 tick=True)
            def mml1(h, xsl=tuple(xsl)):
                for k in range(ND):
                    ins = h.matmul(out=pL[:, :], lhsT=A1b[k][:, :],
                                   rhs=xnfull[:, xsl[k]],
                                   start=(k == 0), stop=(k == ND - 1))
                return ins
            c.op("tensor", mml1, tick=True)
            c.op("vector", lambda h: h.tensor_copy(out=l1b[:, :], in_=pL[:, :]),
                 tick=True)

            def mml3(h, xsl=tuple(xsl)):
                for k in range(ND):
                    ins = h.matmul(out=pL[:, :], lhsT=A3b[k][:, :],
                                   rhs=xnfull[:, xsl[k]],
                                   start=(k == 0), stop=(k == ND - 1))
                return ins
            c.op("tensor", mml3, tick=True)
            c.op("vector", lambda h: h.tensor_copy(out=l3b[:, :], in_=pL[:, :]),
                 tick=True)
            for of in range(NHOF):
                def mmh1(h, of=of, xsl=tuple(xsl)):
                    for k in range(ND):
                        h.matmul(out=pH[0][:, :],
                                 lhsT=W1b[k][:, of * 128:(of + 1) * 128],
                                 rhs=xnfull[:, xsl[k]],
                                 start=(k == 0), stop=False)
                    return h.matmul(out=pH[0][:, :],
                                    lhsT=B1b[:, of * 128:(of + 1) * 128],
                                    rhs=l1b[:, :], start=False, stop=True)
                c.op("tensor", mmh1, tick=True)
                if of == 0 and tco == 0:
                    bmkB = batchB(e)
                c.op("scalar", lambda h, of=of: h.activation(
                    out=sh1[:, of * TC:(of + 1) * TC], in_=pH[0][:, :],
                    func=AF.Silu, bias=b1c[:, of:of + 1]), tick=True)

                def mmh3(h, of=of, xsl=tuple(xsl)):
                    for k in range(ND):
                        h.matmul(out=pH[1][:, :],
                                 lhsT=W3b[k][:, of * 128:(of + 1) * 128],
                                 rhs=xnfull[:, xsl[k]],
                                 start=(k == 0), stop=False)
                    return h.matmul(out=pH[1][:, :],
                                    lhsT=B3b[:, of * 128:(of + 1) * 128],
                                    rhs=l3b[:, :], start=False, stop=True)
                c.op("tensor", mmh3, tick=True)
                c.op("vector", lambda h, of=of: h.scalar_tensor_tensor(
                    out=gsb[:, of * TC:(of + 1) * TC], in0=pH[1][:, :],
                    scalar=b3c[:, of:of + 1],
                    in1=sh1[:, of * TC:(of + 1) * TC],
                    op0=ALU.add, op1=ALU.mult), tick=True)
            def mml2(h):
                for k in range(NHOF):
                    ins = h.matmul(out=pL[:, :], lhsT=A2b[k][:, :],
                                   rhs=gsb[:, k * TC:(k + 1) * TC],
                                   start=(k == 0), stop=(k == NHOF - 1))
                return ins
            c.op("tensor", mml2, tick=True)
            c.op("vector", lambda h: h.tensor_copy(out=l2b[:, :], in_=pL[:, :]),
                 tick=True)
            c.dwait_cnt("tensor", "sync", bmkB[0])
            c.dwait_cnt("tensor", "gpsimd", bmkB[1])
            c.dwait_cnt("vector", "sync", bmkB[0])
            for of2 in range(ND):
                def mmy(h, of2=of2):
                    for k in range(NHOF):
                        h.matmul(out=pH[of2 % 2][:, :],
                                 lhsT=W2b[k][:, of2 * 128:(of2 + 1) * 128],
                                 rhs=gsb[:, k * TC:(k + 1) * TC],
                                 start=(k == 0), stop=False)
                    return h.matmul(out=pH[of2 % 2][:, :],
                                    lhsT=B2b[:, of2 * 128:(of2 + 1) * 128],
                                    rhs=l2b[:, :], start=False, stop=True)
                c.op("tensor", mmy, tick=True)
                if of2 == 0 and tco == NCHO - 1 and e + 1 < NE:
                    batchA(e + 1)
                st = yw[of2 % 2]
                c.dwait("vector", "gpsimd")
                c.op("vector", lambda h, of2=of2, st=st: h.scalar_tensor_tensor(
                    out=st[:, :], in0=pH[of2 % 2][:, :],
                    scalar=b2c[:, of2:of2 + 1], in1=gb[:, :],
                    op0=ALU.add, op1=ALU.mult), tick=True)
                c.dma("gpsimd", lambda h, of2=of2, st=st, t0=t0: h.dma_start(
                    out=S1a_dram[of2, :, t0:t0 + TC], in_=st[:, :],
                    accum_op=ALU.add))


def declare_l1_scratch(nc, dbg=False):
    kind = "ExternalOutput" if dbg else "Internal"
    return {
        "xn": nc.dram_tensor("sp1_xn", [ND, 128, T], BF16, kind=kind).ap(),
        "z": nc.dram_tensor("sp1_z", [16, 128, T], BF16, kind=kind).ap(),
        "xc": nc.dram_tensor("sp1_xc", [16, 128, T], F32, kind=kind).ap(),
        "delta": nc.dram_tensor("sp1_dl", [16, 128, T], F32, kind=kind).ap(),
        "scores": nc.dram_tensor("sp1_sc", [8, TOWN], F32, kind=kind).ap(),
    }


def emit_l1(nc, ctx, c, x1_dram, S1ab, Wd, Cst, parts="abm", dbg=False):
    sp1 = declare_l1_scratch(nc, dbg)
    with ExitStack() as mctx:
        shared1 = {
            "Bf": sbt(nc, mctx, "l1Bf", [NST, T], BF16),
            "Cf": sbt(nc, mctx, "l1Cf", [NST, T], BF16),
        }
        with ExitStack() as actx:
            emit_l1passA(nc, actx, c, x1_dram, sp1, Wd, Cst, shared1)
            c.barrier()
        with ExitStack() as a2ctx:
            emit_l1passA2(nc, a2ctx, c, sp1, Wd, Cst, shared1)
            c.barrier()
        if "b" in parts:
            with ExitStack() as bctx:
                emit_l1passB(nc, bctx, c, sp1, S1ab, Wd, Cst, shared1)
                c.barrier()
    if "m" in parts:
        dbg_g = (nc.dram_tensor("dbg_g", [8, TOWN], F32,
                                kind="ExternalOutput").ap() if dbg else None)
        dbg_g2 = (nc.dram_tensor("dbg_g2", [8, TOWN], F32,
                                 kind="ExternalOutput").ap() if dbg else None)
        dbg_g3 = (nc.dram_tensor("dbg_g3", [128, 32], F32,
                                 kind="ExternalOutput").ap() if dbg else None)
        with ExitStack() as mctx2:
            emit_moe(nc, mctx2, c, sp1, S1ab[0], Wd, Cst, dbg_g, dbg_g2,
                     dbg_g3)
            c.barrier()


def prep_l1_weights(inp, rev):
    d = "b" if rev else "f"
    W = {}
    W["in1T"] = np.ascontiguousarray(inp[f"L1_attn_{d}_in_w"].T)
    xp = inp[f"L1_attn_{d}_xproj_w"]
    W["xpdT"] = np.ascontiguousarray(xp[:DTR].T)
    W["xpbT"] = np.ascontiguousarray(xp[DTR:DTR + NST].T)
    W["xpcT"] = np.ascontiguousarray(xp[DTR + NST:].T)
    W["dtwT"] = np.ascontiguousarray(inp[f"L1_attn_{d}_dt_w"].T)
    cw = inp[f"L1_attn_{d}_conv_w"]
    W["conv1w"] = np.ascontiguousarray(
        cw.reshape(16, 128, 4).transpose(1, 0, 2).reshape(128, 64))
    W["conv1b"] = col_form(inp[f"L1_attn_{d}_conv_b"])
    W["dtb1"] = col_form(inp[f"L1_attn_{d}_dt_b"])
    W["lngr"] = np.asarray(inp["L1_attn_ln_g"], np.float32).reshape(1, D)
    W["lnbr"] = np.asarray(inp["L1_attn_ln_b"], np.float32).reshape(1, D)
    W["lngc"] = col_form(inp["L1_attn_ln_g"])
    W["out1T"] = np.ascontiguousarray(inp[f"L1_attn_{d}_out_w"].T)
    al = np.asarray(inp[f"L1_attn_{d}_A_log"], np.float32)
    W["A1col"] = np.ascontiguousarray(
        al.reshape(16, 128, NST).transpose(1, 0, 2).reshape(128, 16 * NST))
    W["D1col"] = col_form(inp[f"L1_attn_{d}_D_skip"])
    W["WgT"] = np.ascontiguousarray(
        inp["L1_ffn_Wg"].T.reshape(ND, 128, NE).reshape(128 * ND, NE)
    ).reshape(D, NE)
    W["WgT"] = np.ascontiguousarray(
        inp["L1_ffn_Wg"].T.reshape(ND, 128, NE).transpose(1, 0, 2).reshape(128, ND * NE))
    W["W1T"] = np.ascontiguousarray(np.transpose(inp["L1_ffn_W1"], (0, 2, 1)))
    W["W3T"] = np.ascontiguousarray(np.transpose(inp["L1_ffn_W3"], (0, 2, 1)))
    W["W2T"] = np.ascontiguousarray(np.transpose(inp["L1_ffn_W2"], (0, 2, 1)))
    W["A1T"] = np.ascontiguousarray(np.transpose(inp["L1_ffn_A1"], (0, 2, 1)))
    W["A3T"] = np.ascontiguousarray(np.transpose(inp["L1_ffn_A3"], (0, 2, 1)))
    W["A2T"] = np.ascontiguousarray(np.transpose(inp["L1_ffn_A2"], (0, 2, 1)))
    W["B1T"] = np.ascontiguousarray(np.transpose(inp["L1_ffn_B1"], (0, 2, 1)))
    W["B3T"] = np.ascontiguousarray(np.transpose(inp["L1_ffn_B3"], (0, 2, 1)))
    W["B2T"] = np.ascontiguousarray(np.transpose(inp["L1_ffn_B2"], (0, 2, 1)))
    W["b1c"] = np.ascontiguousarray(
        inp["L1_ffn_b1"].reshape(NE, 16, 128).transpose(0, 2, 1))
    W["b3c"] = np.ascontiguousarray(
        inp["L1_ffn_b3"].reshape(NE, 16, 128).transpose(0, 2, 1))
    W["b2c"] = np.ascontiguousarray(
        inp["L1_ffn_b2"].reshape(NE, ND, 128).transpose(0, 2, 1))
    return W


L1W_SHAPES = [
    ("in1T", [D, 2 * DIN1], F32), ("xpdT", [DIN1, DTR], F32),
    ("xpbT", [DIN1, NST], F32), ("xpcT", [DIN1, NST], F32),
    ("dtwT", [DTR, DIN1], F32), ("conv1w", [128, 64], F32),
    ("conv1b", [128, 16], F32), ("dtb1", [128, 16], F32),
    ("lngr", [1, D], F32), ("lnbr", [1, D], F32), ("lngc", [128, ND], F32),
    ("out1T", [DIN1, D], F32), ("A1col", [128, 16 * NST], F32),
    ("D1col", [128, 16], F32), ("WgT", [128, ND * NE], F32),
    ("W1T", [NE, D, HID1], F32), ("W3T", [NE, D, HID1], F32),
    ("W2T", [NE, HID1, D], F32), ("A1T", [NE, D, RK], F32),
    ("A3T", [NE, D, RK], F32), ("A2T", [NE, HID1, RK], F32),
    ("B1T", [NE, RK, HID1], F32), ("B3T", [NE, RK, HID1], F32),
    ("B2T", [NE, RK, D], F32), ("b1c", [NE, 128, 16], F32),
    ("b3c", [NE, 128, 16], F32), ("b2c", [NE, 128, ND], F32),
]


def emit_final(nc, ctx, c, x1_dram, Sown, Sprev, out_dram, masks=None, G=None):
    xt = sbt(nc, ctx, "fnx", [128, TC], F32)
    at = sbt(nc, ctx, "fna", [128, TC], F32)
    bt = sbt(nc, ctx, "fnb", [128, TC], F32)
    g0 = sbt(nc, ctx, "fng0", [128, TC], F32)
    g1t = sbt(nc, ctx, "fng1", [128, TC], F32)
    ot = [sbt(nc, ctx, f"fno{i}", [128, TC], F32) for i in range(2)]
    for tco in range(NCHO):
        t0 = tco * TC
        tm = (NCH - 1 - tco) * TC - TOWN
        for d in range(ND):
            c.dwait("vector", "sync")
            c.dma("sync", lambda h, d=d, t0=t0: h.dma_start(
                out=xt[:, :], in_=x1_dram[d, :, t0:t0 + TC]))
            c.dma("sync", lambda h, d=d, t0=t0: h.dma_start(
                out=at[:, :], in_=Sown[d, :, t0:t0 + TC]))
            if masks is None:
                c.dma("sync", lambda h, d=d, t0=t0: h.dma_start(
                    out=bt[:, :], in_=Sprev[d, :, t0:t0 + TC]))
            else:
                c.dma("sync", lambda h, d=d, tm=tm: h.dma_start(
                    out=g0[:, :], in_=G[0][d, :, tm:tm + TC]))
                c.dma("sync", lambda h, d=d, tm=tm: h.dma_start(
                    out=g1t[:, :], in_=G[1][d, :, tm:tm + TC]))
            c.dwait("vector", "sync")
            o = ot[d % 2]
            c.op("vector", lambda h, o=o: h.tensor_tensor(
                out=o[:, :], in0=xt[:, :], in1=at[:, :], op=ALU.add))
            if masks is None:
                c.op("vector", lambda h, o=o: h.tensor_tensor(
                    out=o[:, :], in0=o[:, :], in1=bt[:, :], op=ALU.add),
                    tick=True)
            else:
                c.op("vector", lambda h, o=o: h.scalar_tensor_tensor(
                    out=o[:, :], in0=g0[:, ::-1], scalar=masks[:, 0:1],
                    in1=o[:, :], op0=ALU.mult, op1=ALU.add))
                c.op("vector", lambda h, o=o: h.scalar_tensor_tensor(
                    out=o[:, :], in0=g1t[:, ::-1], scalar=masks[:, 1:2],
                    in1=o[:, :], op0=ALU.mult, op1=ALU.add), tick=True)
            c.dma("sync", lambda h, d=d, t0=t0, o=o: h.dma_start(
                out=out_dram[d, :, t0:t0 + TC], in_=o[:, :]))
    c.barrier()


def build_l1(parts="abm", dbg=False):
    nc = bass.Bass("TRN2", target_bir_lowering=False)
    x_dram = nc.dram_tensor("x", [ND, 128, T], F32, kind="ExternalInput").ap()
    S0o = nc.dram_tensor("S0own", [ND, 128, T], F32, kind="ExternalInput").ap()
    S0p = nc.dram_tensor("S0prev", [ND, 128, T], F32, kind="ExternalInput").ap()
    x1_dram = nc.dram_tensor("x1", [ND, 128, T], F32, kind="ExternalOutput").ap()
    S1a_d = nc.dram_tensor("S1a", [ND, 128, TOWN], F32,
                           kind="ExternalOutput").ap()
    S1b_d = nc.dram_tensor("S1b", [ND, 128, TOWN], F32,
                           kind="ExternalOutput").ap()
    Wd = {}
    for nm, shape, dt in L1W_SHAPES:
        Wd[nm] = nc.dram_tensor("w_" + nm, shape, dt, kind="ExternalInput").ap()
    with ExitStack() as ctx:
        c = Ctl(nc, ctx)
        Cst = load_consts(nc, ctx, c)
        with ExitStack() as cctx:
            emit_combine(nc, cctx, c, x_dram, S0o, S0p, x1_dram)
        emit_l1(nc, ctx, c, x1_dram, (S1a_d, S1b_d), Wd, Cst, parts, dbg)
        with nc.Block() as block:
            c.emit(block)
    return nc


def build_fin():
    nc = bass.Bass("TRN2", target_bir_lowering=False)
    x1_dram = nc.dram_tensor("x1", [ND, 128, T], F32, kind="ExternalInput").ap()
    S1o = nc.dram_tensor("S1own", [ND, 128, T], F32, kind="ExternalInput").ap()
    S1p = nc.dram_tensor("S1prev", [ND, 128, T], F32, kind="ExternalInput").ap()
    out_dram = nc.dram_tensor("out", [ND, 128, TOWN], F32,
                              kind="ExternalOutput").ap()
    with ExitStack() as ctx:
        c = Ctl(nc, ctx)
        with ExitStack() as fctx:
            emit_final(nc, fctx, c, x1_dram, S1o, S1p, out_dram)
        with nc.Block() as block:
            c.emit(block)
    return nc


def _host_inputs_common(inputs, c):
    rev = c % 2
    m = {}
    m.update(host_consts())
    W0 = prep_l0_weights(inputs, rev)
    for nm, shape, dt in L0W_SHAPES:
        m["w_" + nm] = W0[nm]
    return m


def kernel_unfused(inputs):
    x = np.asarray(inputs["L0_attn_x"])
    B = x.shape[0]
    ncores = 8
    xs = []
    for c in range(ncores):
        b, rev = c // 2, c % 2
        xb = np.ascontiguousarray(x[b].T) if not rev else \
            np.ascontiguousarray(x[b].T[:, ::-1])
        xs.append(xb.reshape(ND, 128, T))
    cst = host_consts()
    nc0 = build_l0()
    maps0 = []
    for c in range(ncores):
        W0 = prep_l0_weights(inputs, c % 2)
        m = {"x": xs[c]}
        for nm, shape, dt in L0W_SHAPES:
            m["w_" + nm] = W0[nm]
        m.update(cst)
        maps0.append(m)
    res0 = run_bass_kernel_spmd(nc0, maps0, core_ids=list(range(ncores)))
    S0 = [np.asarray(res0.results[c]["S0"]) for c in range(ncores)]
    nc1 = build_l1()
    maps1 = []
    for c in range(ncores):
        part = c ^ 1
        sp = np.ascontiguousarray(
            S0[part].reshape(D, T)[:, ::-1]).reshape(ND, 128, T)
        W1 = prep_l1_weights(inputs, c % 2)
        m = {"x": xs[c], "S0own": S0[c], "S0prev": sp}
        for nm, shape, dt in L1W_SHAPES:
            m["w_" + nm] = W1[nm]
        m.update(cst)
        maps1.append(m)
    res1 = run_bass_kernel_spmd(nc1, maps1, core_ids=list(range(ncores)))
    x1 = [np.asarray(res1.results[c]["x1"]) for c in range(ncores)]
    S1 = [np.concatenate([np.asarray(res1.results[c]["S1a"]),
                          np.asarray(res1.results[c]["S1b"])], axis=2)
          for c in range(ncores)]
    nc2 = build_fin()
    maps2 = []
    for c in range(ncores):
        part = c ^ 1
        sp = np.ascontiguousarray(
            S1[part].reshape(D, T)[:, ::-1]).reshape(ND, 128, T)
        maps2.append({"x1": x1[c], "S1own": S1[c], "S1prev": sp})
    res2 = run_bass_kernel_spmd(nc2, maps2, core_ids=list(range(ncores)))
    outs = [np.asarray(res2.results[c]["out"]).reshape(D, TOWN)
            for c in range(ncores)]
    full = np.empty((B, T, D), np.float32)
    for b in range(B):
        full[b, :TOWN] = outs[2 * b].T
        full[b, TOWN:] = outs[2 * b + 1].T[::-1]
    return full


def build_full():
    nc = bass.Bass("TRN2", target_bir_lowering=False)
    x_dram = nc.dram_tensor("x", [ND, 128, T], F32, kind="ExternalInput").ap()
    pmask = nc.dram_tensor("pmask", [128, 2], F32, kind="ExternalInput").ap()
    out_dram = nc.dram_tensor("out", [ND, 128, TOWN], F32,
                              kind="ExternalOutput").ap()
    S0 = nc.dram_tensor("S0i", [ND, 128, T], F32, kind="Internal").ap()
    S1a = nc.dram_tensor("S1ai", [ND, 128, TOWN], F32, kind="Internal").ap()
    S1b = nc.dram_tensor("S1bi", [ND, 128, TOWN], F32, kind="Internal").ap()
    x1 = nc.dram_tensor("x1i", [ND, 128, T], F32, kind="Internal").ap()
    G0 = nc.dram_tensor("G0i", [2 * ND, 128, T], F32, kind="Internal").ap()
    G1 = nc.dram_tensor("G1i", [2 * ND, 128, TOWN], F32,
                        kind="Internal").ap()
    Wd = {}
    for nm, shape, dt in L0W_SHAPES + L1W_SHAPES:
        Wd[nm] = nc.dram_tensor("w_" + nm, shape, dt, kind="ExternalInput").ap()
    groups = [[0, 1], [2, 3], [4, 5], [6, 7]]
    with ExitStack() as ctx:
        c = Ctl(nc, ctx)
        Cst = load_consts(nc, ctx, c)
        pm = sbt(nc, ctx, "pmt", [128, 2], F32)
        c.dma("sync", lambda h: h.dma_start(out=pm[:], in_=pmask[:, :]))
        emit_l0(nc, ctx, c, x_dram, S0, Wd, Cst)
        c.barrier()
        c.op("gpsimd", lambda h: h.collective_compute(
            "AllGather", ALU.bypass, replica_groups=groups,
            ins=[S0.opt()], outs=[G0.opt()]), tick=True)
        c.barrier()
        with ExitStack() as cctx:
            emit_combine(nc, cctx, c, x_dram, S0, None, x1, masks=pm,
                         G=(G0[0:ND], G0[ND:2 * ND]))
        emit_l1(nc, ctx, c, x1, (S1a, S1b), Wd, Cst)
        c.barrier()
        c.op("gpsimd", lambda h: h.collective_compute(
            "AllGather", ALU.bypass, replica_groups=groups,
            ins=[S1b.opt()], outs=[G1.opt()]), tick=True)
        c.barrier()
        with ExitStack() as fctx:
            emit_final(nc, fctx, c, x1, S1a, None, out_dram, masks=pm,
                       G=(G1[0:ND], G1[ND:2 * ND]))
        c.barrier()
        with nc.Block() as block:
            c.emit(block)
    return nc


def kernel_fused(inputs):
    x = np.asarray(inputs["L0_attn_x"])
    B = x.shape[0]
    ncores = 8
    nc = build_full()
    cst = host_consts()
    maps = []
    for c in range(ncores):
        b, rev = c // 2, c % 2
        xb = np.ascontiguousarray(x[b].T) if not rev else \
            np.ascontiguousarray(x[b].T[:, ::-1])
        m = {"x": xb.reshape(ND, 128, T)}
        m["pmask"] = np.ascontiguousarray(np.broadcast_to(
            np.array([[1.0 if rev == 1 else 0.0,
                       1.0 if rev == 0 else 0.0]], np.float32), (128, 2)))
        W0 = prep_l0_weights(inputs, rev)
        for nm, shape, dt in L0W_SHAPES:
            m["w_" + nm] = W0[nm]
        W1 = prep_l1_weights(inputs, rev)
        for nm, shape, dt in L1W_SHAPES:
            m["w_" + nm] = W1[nm]
        m.update(cst)
        maps.append(m)
    res = run_bass_kernel_spmd(nc, maps, core_ids=list(range(ncores)))
    outs = [np.asarray(res.results[c]["out"]).reshape(D, TOWN)
            for c in range(ncores)]
    full = np.empty((B, T, D), np.float32)
    for b in range(B):
        full[b, :TOWN] = outs[2 * b].T
        full[b, TOWN:] = outs[2 * b + 1].T[::-1]
    return full


def kernel(**inputs):
    return kernel_fused(inputs)
